# version 33
# baseline (speedup 1.0000x reference)
import numpy as np
import ml_dtypes
import concourse.bass as bass
import concourse.mybir as mybir
from concourse.bass_utils import run_bass_kernel_spmd

F32 = mybir.dt.float32
BF16 = mybir.dt.bfloat16
AF = mybir.ActivationFunctionType
ALU = mybir.AluOpType
AX = mybir.AxisListType.X
BF = ml_dtypes.bfloat16

T = 2048
NCORES = 8
QG = 512
NQG = T // QG
TC = 512
NTC = T // TC
SHIFT = 10.0
SCALE = float(1.0 / np.sqrt(96.0))

ENGINES = ("sync", "tensor", "vector", "scalar", "gpsimd")


class Sem:
    def __init__(self, handle, name, chained=False):
        self.h = handle
        self.name = name
        self.n = 0
        self.chained = chained

    def tick(self, k=1):
        self.n += k
        return (self, self.n)


class IncH:
    __slots__ = ("incs", "prog", "eng", "idx")
    def __init__(self, incs, prog, eng, idx):
        self.incs = incs
        self.prog = prog
        self.eng = eng
        self.idx = idx

    def then_inc(self, sem, k):
        if sem.chained and sem.n > 0:
            w = self.prog.water[self.eng]
            if w.get(sem.name, 0) < sem.n:
                w[sem.name] = sem.n
                stream = self.prog.streams[self.eng]
                stream.insert(len(stream) - 1 - self.prog._ninserted(self.eng, self.idx),
                              ("wait_ge", (sem.h, sem.n), {}, []))
        self.incs.append((sem, k))
        return sem.tick(k)


class Rec:
    def __init__(self, prog, eng):
        self._p = prog
        self._e = eng

    def wait(self, tok):
        if tok is None:
            return
        sem, v = tok
        w = self._p.water[self._e]
        if w.get(sem.name, 0) >= v:
            return
        w[sem.name] = v
        self._p.streams[self._e].append(("wait_ge", (sem.h, v), {}, []))

    def waits(self, toks):
        for tk in toks:
            self.wait(tk)

    def __getattr__(self, name):
        def call(*a, **k):
            incs = []
            stream = self._p.streams[self._e]
            stream.append((name, a, k, incs))
            return IncH(incs, self._p, self._e, len(stream) - 1)
        return call


class Prog:
    def __init__(self, nc):
        self.nc = nc
        self.streams = {e: [] for e in ENGINES}
        self.water = {e: {} for e in ENGINES}
        self.recs = {e: Rec(self, e) for e in ENGINES}
        self._semid = 0
        self.toks = {}

    def sem(self, name=None, chained=False):
        self._semid += 1
        nm = name or f"s{self._semid}"
        return Sem(self.nc.alloc_semaphore(name=nm), nm, chained)

    def _ninserted(self, eng, idx):
        return 0

    def dma(self, rec, stream, out, in_, **kw):
        if not hasattr(self, "dma_sems"):
            self.dma_sems = {}
        s = self.dma_sems.get(stream)
        if s is None:
            s = self.sem("d_" + stream)
            self.dma_sems[stream] = s
        return rec.dma_start(out=out, in_=in_, **kw).then_inc(s, 16)

    def on(self, eng):
        return self.recs[eng]

    def bank_write(self, TE, bank):
        for tk in self.toks.get(("bankR", bank), []):
            TE.wait(tk)
        self.toks[("bankR", bank)] = []

    def bank_written(self, bank, tok):
        self.toks[("bankW", bank)] = tok

    def bank_read(self, rec, bank):
        rec.wait(self.toks.get(("bankW", bank)))

    def bank_read_done(self, bank, tok):
        self.toks.setdefault(("bankR", bank), []).append(tok)

    def finalize(self, block):
        def mk(stream):
            def body(eng):
                for (name, a, k, incs) in stream:
                    inst = getattr(eng, name)(*a, **k)
                    for (sem, kk) in incs:
                        inst.then_inc(sem.h, kk)
            return body
        for e in ENGINES:
            if self.streams[e]:
                getattr(block, e)(mk(self.streams[e]))

    def n_instructions(self):
        return sum(len(s) for s in self.streams.values())


class Bump:
    def __init__(self, limit=229376, base=16544):
        self.pos = base
        self.peak = base
        self.limit = limit

    def alloc(self, nbytes):
        off = (self.pos + 31) & ~31
        self.pos = off + nbytes
        self.peak = max(self.peak, self.pos)
        assert self.pos <= self.limit, f"SBUF overflow: {self.pos} > {self.limit}"
        return off

    def mark(self):
        return self.pos

    def reset(self, mark):
        self.pos = mark


def sb(nc, lay, name, shape, dtype):
    n = 1
    for s in shape[1:]:
        n *= s
    nbytes = n * (2 if dtype == BF16 else 4)
    off = lay.alloc(nbytes)
    return nc.alloc_sbuf_tensor_at(name, list(shape), dtype, offset=off)


def _fm(a):
    R = a.shape[0] // 128
    return np.ascontiguousarray(a.reshape(R, 128, a.shape[1]).transpose(1, 0, 2))


def _vec(a):
    R = a.shape[0] // 128
    return np.ascontiguousarray(a.reshape(R, 128).T)


def pack_host(ins):
    d = {}
    d["wkv"] = _fm(ins["L0_attn_W_kv"])
    d["wq"] = _fm(ins["L0_attn_W_q"])
    d["wo"] = np.ascontiguousarray(
        ins["L0_attn_W_o"].reshape(8, 128, 8, 128).transpose(1, 0, 2, 3))
    kd = ins["L0_attn_W_kdec"].reshape(2, 128, 8, 128).transpose(2, 1, 0, 3)
    vd = ins["L0_attn_W_vdec"].reshape(2, 128, 8, 128).transpose(2, 1, 0, 3)
    qd = ins["L0_attn_W_qdec"].reshape(4, 128, 8, 128).transpose(2, 1, 0, 3)
    d["kdec"] = np.ascontiguousarray(kd)
    d["vdec"] = np.ascontiguousarray(vd)
    d["qdec"] = np.ascontiguousarray(qd)
    perm = (np.arange(32) + 16) % 32
    def rope_pack(w, nkc):
        wp = w.reshape(nkc, 128, 4, 128).transpose(2, 1, 0, 3)
        return np.ascontiguousarray(wp)
    wkr = ins["L0_attn_W_krope"]
    wqr = ins["L0_attn_W_qrope"]
    permcols = np.concatenate([np.arange(512).reshape(16, 32)[h][perm] for h in range(16)])
    d["krope"] = rope_pack(wkr, 8)
    d["krope_p"] = rope_pack(wkr[:, permcols], 8)
    d["qrope"] = rope_pack(wqr, 4)
    d["qrope_p"] = rope_pack(wqr[:, permcols], 4)
    d["gate_wt"] = _fm(np.ascontiguousarray(ins["L0_ffn_gate_w"].T))
    gp = np.concatenate([ins["L0_ffn_gate_proj_w"], ins["L0_ffn_shared_gate_w"]], 0)
    up = np.concatenate([ins["L0_ffn_up_proj_w"], ins["L0_ffn_shared_up_w"]], 0)
    dn = np.concatenate([ins["L0_ffn_down_proj_w"], ins["L0_ffn_shared_down_w"]], 0)
    d["wg"] = np.ascontiguousarray(gp.reshape(9, 8, 128, 16, 128).transpose(0, 3, 2, 1, 4))
    d["wu"] = np.ascontiguousarray(up.reshape(9, 8, 128, 16, 128).transpose(0, 3, 2, 1, 4))
    d["wd"] = np.ascontiguousarray(dn.reshape(9, 16, 128, 8, 128).transpose(0, 2, 1, 3, 4))
    for nm, key in [("wk", "L1_attn_Wk"), ("wv", "L1_attn_Wv"),
                    ("wr", "L1_attn_Wr"), ("wo1", "L1_attn_Wo")]:
        wt = np.ascontiguousarray(ins[key].T)
        d[nm] = np.ascontiguousarray(wt.reshape(8, 128, 8, 128).transpose(2, 1, 0, 3))
    d["w1"] = np.ascontiguousarray(
        ins["L1_ffn_W1"].reshape(8, 128, 32, 128).transpose(2, 1, 0, 3))
    d["w2"] = np.ascontiguousarray(ins["L1_ffn_W2"].reshape(32, 128, 8, 128))
    d["l1vecs"] = np.stack([_vec(ins[k]) for k in
                            ["L1_attn_time_mix_k", "L1_attn_time_mix_v", "L1_attn_time_mix_r",
                             "L1_attn_time_decay", "L1_attn_time_first",
                             "L1_attn_ln_w", "L1_attn_ln_b", "L1_ffn_b2"]], 1)
    d["b1"] = _vec(ins["L1_ffn_b1"])
    d["gkv"] = _vec(ins["L0_attn_g_kv"])
    d["beta"] = np.ascontiguousarray(ins["L0_ffn_beta"].reshape(8, 1))
    c = {}
    c["eye"] = np.eye(128, dtype=np.float32)
    ki = np.arange(128)[:, None]; qi = np.arange(128)[None, :]
    c["maskdia"] = np.where(ki > qi, np.float32(-1e30), np.float32(0.0))
    c["mask01"] = np.where(ki <= qi, 1.0, 0.0).astype(BF)
    c["ones_bf"] = np.ones((128, 1), dtype=BF)
    c["ones_row"] = np.ones((1, 128), dtype=np.float32)
    inv_freq = 1.0 / 10000.0 ** (np.arange(0, 32, 2, dtype=np.float32) / 32)
    freqs = np.outer(np.arange(T, dtype=np.float32), inv_freq)
    emb = np.concatenate([freqs, freqs], -1)
    cos1 = np.cos(emb).astype(np.float32).T
    sin1 = np.sin(emb).astype(np.float32).T
    sins = sin1.copy(); sins[:16] = -sin1[:16]
    c["cos2"] = np.ascontiguousarray(np.concatenate([cos1, cos1], 0)).astype(BF)
    c["sin2"] = np.ascontiguousarray(np.concatenate([sins, sins], 0)).astype(BF)
    sel = np.zeros((8, 8, 128), dtype=np.float32)
    for e in range(8):
        sel[e, e, :] = 1.0
    c["sel"] = np.ascontiguousarray(sel.transpose(1, 0, 2))
    return d, c


def declare_inputs(nc, packed, const, need):
    aps = {}
    for nm in need:
        src = packed if nm in packed else const
        a = src[nm]
        dt = BF16 if a.dtype == BF else F32
        aps[nm] = nc.dram_tensor(nm, list(a.shape), dt, kind="ExternalInput").ap()
    return aps


class NS:
    pass


def ln_stats(P, t, sems, x_view_fn, width, b_sum, b_sq, cast_s=False):
    V, S, TE, G = P.on("vector"), P.on("scalar"), P.on("tensor"), P.on("gpsimd")
    toks = P.toks
    for kc in range(8):
        xt = x_view_fn(kc)
        if cast_s:
            S.wait(toks.get(("lnb", kc % 2)))
            tk_b = S.copy(t.xb_bf[kc % 2][:, 0:width], xt).then_inc(sems["s"], 1)
        else:
            G.wait(toks.get(("lnb", kc % 2)))
            tk_b = G.tensor_copy(t.xb_bf[kc % 2][:, 0:width], xt).then_inc(sems["g"], 1)
        S.wait(toks.get(("lns", kc % 2)))
        tk_s = S.activation(t.xsq_bf[kc % 2][:, 0:width], xt, AF.Square).then_inc(sems["s"], 1)
        if kc == 0:
            P.bank_write(TE, b_sum)
            P.bank_write(TE, b_sq)
        TE.wait(tk_b)
        tmm = TE.matmul(t.pb[b_sum][0:1, 0:width], t.ones_bf[:], t.xb_bf[kc % 2][:, 0:width],
                        start=(kc == 0), stop=(kc == 7)).then_inc(sems["t"], 1)
        toks[("lnb", kc % 2)] = tmm
        TE.wait(tk_s)
        tmm2 = TE.matmul(t.pb[b_sq][0:1, 0:width], t.ones_bf[:], t.xsq_bf[kc % 2][:, 0:width],
                         start=(kc == 0), stop=(kc == 7)).then_inc(sems["t"], 1)
        toks[("lns", kc % 2)] = tmm2
    P.bank_written(b_sum, tmm2)
    P.bank_written(b_sq, tmm2)
    return tmm2


def ln_rows(P, t, sems, width, b_sum, b_sq, m_dst, rstd_dst, inv_n, eps_ap,
            rstd2_dst=None):
    V, S = P.on("vector"), P.on("scalar")
    P.bank_read(V, b_sum)
    P.bank_read(V, b_sq)
    V.tensor_scalar_mul(m_dst, t.pb[b_sum][0:1, 0:width], inv_n)
    tk = V.tensor_scalar_mul(t.row_a[0:1, 0:width], t.pb[b_sq][0:1, 0:width], inv_n)\
        .then_inc(sems["v"], 1)
    P.bank_read_done(b_sum, tk)
    P.bank_read_done(b_sq, tk)
    V.wait(tk)
    tk = V.scalar_tensor_tensor(t.row_b[0:1, 0:width], m_dst, -1.0, m_dst,
                                ALU.mult, ALU.mult).then_inc(sems["v"], 1)
    V.wait(tk)
    tk = V.tensor_add(t.row_a[0:1, 0:width], t.row_a[0:1, 0:width],
                      t.row_b[0:1, 0:width]).then_inc(sems["v"], 1)
    S.wait(tk)
    tk = S.activation(t.row_b[0:1, 0:width], t.row_a[0:1, 0:width], AF.Ln,
                      bias=eps_ap, scale=1.0).then_inc(sems["s"], 1)
    S.wait(tk)
    tk = S.activation(rstd_dst, t.row_b[0:1, 0:width], AF.Exp,
                      scale=-0.5).then_inc(sems["s"], 1)
    if rstd2_dst is None:
        return tk
    V.wait(tk)
    tk = V.tensor_mul(t.row_b[0:1, 0:width], t.row_a[0:1, 0:width], rstd_dst)\
        .then_inc(sems["v"], 1)
    V.wait(tk)
    tk = V.tensor_mul(t.row_b[0:1, 0:width], t.row_b[0:1, 0:width], rstd_dst)\
        .then_inc(sems["v"], 1)
    S.wait(tk)
    tk = S.activation(t.row_b[0:1, 0:width], t.row_b[0:1, 0:width], AF.Ln,
                      bias=eps_ap, scale=1.0).then_inc(sems["s"], 1)
    S.wait(tk)
    tk = S.activation(rstd2_dst, t.row_b[0:1, 0:width], AF.Exp,
                      scale=-0.5).then_inc(sems["s"], 1)
    return tk


def bcast2(P, t, sems, width, row_m, row_r, b_m, b_r, ready_tok):
    V, TE = P.on("vector"), P.on("tensor")
    TE.wait(ready_tok)
    P.bank_write(TE, b_m)
    tk_m = TE.matmul(t.pb[b_m][:, 0:width], t.ones_row[0:1, :], row_m,
                     start=True, stop=True).then_inc(sems["t"], 1)
    P.bank_written(b_m, tk_m)
    P.bank_write(TE, b_r)
    tk_r = TE.matmul(t.pb[b_r][:, 0:width], t.ones_row[0:1, :], row_r,
                     start=True, stop=True).then_inc(sems["t"], 1)
    P.bank_written(b_r, tk_r)
    P.bank_read(V, b_m)
    V.tensor_copy(t.bc_a[:, 0:width], t.pb[b_m][:, 0:width])
    P.bank_read(V, b_r)
    tk = V.tensor_copy(t.bc_b[:, 0:width], t.pb[b_r][:, 0:width]).then_inc(sems["v"], 1)
    P.bank_read_done(b_m, tk)
    P.bank_read_done(b_r, tk)
    return tk


def build_l0(P, t, aps, x_in_ap, x1_out_ap, sems, dram):
    V, S, TE, G, SY = (P.on("vector"), P.on("scalar"), P.on("tensor"),
                       P.on("gpsimd"), P.on("sync"))
    toks = P.toks

    tk_xq = [P.dma(SY, f"x{qg}", t.x[:, :, qg * QG:(qg + 1) * QG],
                   x_in_ap[:, :, qg * QG:(qg + 1) * QG]) for qg in range(NQG)]
    tk_x = tk_xq[-1]
    tk_c = None
    for nm, dst in [("eye", t.eye), ("ones_row", t.ones_row),
                    ("sel", t.sel),
                    ("gate_wt", t.gate_wt), ("beta", t.beta), ("gkv", t.gkv),
                    ("ones_bf", t.ones_bf)]:
        tk_c = P.dma(SY, "const", dst[:], aps[nm])
    V.memset(t.eps_row[:], 1e-5)
    V.memset(t.eps_row6[:], 1e-6)
    tk_ms = V.memset(t.neg10[:], -SHIFT).then_inc(sems["v"], 1)

    TE.wait(tk_c); TE.wait(tk_ms)
    xn_tok = None
    for qg in range(NQG):
        sl = slice(qg * QG, (qg + 1) * QG)
        for e in (G, S, V, TE):
            e.wait(tk_xq[qg])
        st_tok = ln_stats(P, t, sems, lambda kc: t.x[:, kc, sl], QG, 0, 1)
        row_tok = ln_rows(P, t, sems, QG, 0, 1,
                          t.m_row[0:1, 0:QG], t.rstd_row[0:1, 0:QG], 1.0 / 1024.0,
                          t.eps_row[0:1, :])
        bc_tok = bcast2(P, t, sems, QG, t.m_row[0:1, 0:QG], t.rstd_row[0:1, 0:QG],
                        2, 3, row_tok)
        V.wait(bc_tok)
        for kc in range(8):
            scr = t.xn_f32[kc % 2]
            V.wait(toks.get(("xnscr", kc % 2)))
            V.wait(toks.get(("xnscrg", kc % 2)))
            tk0 = V.tensor_sub(scr[:], t.x[:, kc, sl], t.bc_a[:, 0:QG])\
                .then_inc(sems["v"], 1)
            V.wait(tk0)
            tk = V.tensor_mul(scr[:], scr[:], t.bc_b[:, 0:QG]).then_inc(sems["v"], 1)
            G.wait(tk)
            tkg = G.tensor_copy(t.xn[:, kc, sl], scr[:]).then_inc(sems["g"], 1)
            toks[("xnscrg", kc % 2)] = tkg
            xn_tok = tkg
            toks[("xnq", qg)] = tkg
            if kc == 0:
                P.bank_write(TE, 4)
            TE.wait(tk)
            tmm = TE.matmul(t.pb[4][0:8, 0:QG], t.gate_wt[:, kc, :], scr[:],
                            start=(kc == 0), stop=(kc == 7)).then_inc(sems["t"], 1)
            toks[("xnscr", kc % 2)] = tmm
        P.bank_written(4, tmm)
        P.bank_read(V, 4)
        tk = V.tensor_copy(t.logits[:, sl], t.pb[4][0:8, 0:QG]).then_inc(sems["v"], 1)
        P.bank_read_done(4, tk)
        S.wait(tk)
        tk_sig = S.activation(t.sig[:, sl], t.logits[:, sl], AF.Sigmoid)\
            .then_inc(sems["s"], 1)
        V.wait(tk)
        tk_lb = V.tensor_scalar_add(t.lb[:, sl], t.logits[:, sl], t.beta[:])\
            .then_inc(sems["v"], 1)
        TE.wait(tk_lb)
        P.bank_write(TE, 5)
        for i in range(4):
            tkt = TE.transpose(t.pb[5][:, i * 8:(i + 1) * 8],
                               t.lb[:, qg * QG + i * 128:qg * QG + (i + 1) * 128],
                               t.eye[0:8, 0:8]).then_inc(sems["t"], 1)
        P.bank_written(5, tkt)
        P.bank_read(V, 5)
        tk = V.tensor_copy(t.lb_tok[:, 0:32], t.pb[5][:, 0:32]).then_inc(sems["v"], 1)
        P.bank_read_done(5, tk)
        V.wait(tk)
        for i in range(4):
            V.tensor_reduce(t.m1[:, i:i + 1], t.lb_tok[:, i * 8:(i + 1) * 8], AX, ALU.max)
        tk = V.memset(t.junk[:], 0.0).then_inc(sems["v"], 1)
        V.wait(tk)
        for i in range(4):
            V.tensor_scalar(t.ge1[:, i * 8:(i + 1) * 8], t.lb_tok[:, i * 8:(i + 1) * 8],
                            t.m1[:, i:i + 1], None, ALU.is_ge)
        tk = V.memset(t.junk[:], 0.0).then_inc(sems["v"], 1)
        V.wait(tk)
        for i in range(4):
            V.scalar_tensor_tensor(t.msk[:, i * 8:(i + 1) * 8], t.ge1[:, i * 8:(i + 1) * 8],
                                   -1e30, t.lb_tok[:, i * 8:(i + 1) * 8], ALU.mult, ALU.add)
        tk = V.memset(t.junk[:], 0.0).then_inc(sems["v"], 1)
        V.wait(tk)
        for i in range(4):
            V.tensor_reduce(t.m1[:, i:i + 1], t.msk[:, i * 8:(i + 1) * 8], AX, ALU.max)
        tk = V.memset(t.junk[:], 0.0).then_inc(sems["v"], 1)
        V.wait(tk)
        for i in range(4):
            V.tensor_scalar(t.msk[:, i * 8:(i + 1) * 8], t.msk[:, i * 8:(i + 1) * 8],
                            t.m1[:, i:i + 1], None, ALU.is_ge)
        tk = V.memset(t.junk[:], 0.0).then_inc(sems["v"], 1)
        V.wait(tk)
        tk_m2 = V.tensor_add(t.ge1[:, 0:32], t.ge1[:, 0:32], t.msk[:, 0:32])\
            .then_inc(sems["v"], 1)
        TE.wait(tk_m2)
        for i in range(4):
            P.bank_write(TE, 5)
            tkt = TE.transpose(t.pb[5][0:8, 0:128], t.ge1[:, i * 8:(i + 1) * 8], t.eye[:])\
                .then_inc(sems["t"], 1)
            P.bank_written(5, tkt)
            P.bank_read(V, 5)
            tk = V.tensor_copy(t.mask_fm[:, qg * QG + i * 128:qg * QG + (i + 1) * 128],
                               t.pb[5][0:8, 0:128]).then_inc(sems["v"], 1)
            P.bank_read_done(5, tk)
        V.wait(tk_sig); V.wait(tk)
        tk_cq = V.tensor_mul(t.c_sb[:, sl], t.mask_fm[:, sl], t.sig[:, sl])\
            .then_inc(sems["v"], 1)
        toks[("cqg", qg)] = tk_cq
    logits_tok = tk

    tk_csb = toks[("cqg", NQG - 1)]
    x_dead = [logits_tok, xn_tok, tk_csb]

    moe_acc_toks = []
    last_acc = None
    V.waits(x_dead)
    for e in range(9):
        tk_wd = None
        if e < 8:
            for qg in range(NQG):
                TE.wait(toks[("cqg", qg)])
                P.bank_write(TE, 6)
                tkb = TE.matmul(t.pb[6][:, 0:QG], t.sel[:, e, :],
                                t.c_sb[:, qg * QG:(qg + 1) * QG], start=True, stop=True)\
                    .then_inc(sems["t"], 1)
                P.bank_written(6, tkb)
                P.bank_read(V, 6)
                V.wait(toks.get(("cbcfree", qg)))
                tk = V.tensor_copy(t.cbc[qg][:], t.pb[6][:, 0:QG]).then_inc(sems["v"], 1)
                P.bank_read_done(6, tk)
                toks[("cbc", qg)] = tk
        for mc in range(16):
            bi = mc % 2
            G.wait(toks.get(("wgfree", bi)))
            tk_wg = P.dma(G, f"wg{bi}", t.wg_sb[bi][:], aps["wg"][e, mc])
            tk_wu = P.dma(G, f"wu{bi}", t.wu_sb[bi][:], aps["wu"][e, mc])
            if mc == 1:
                G.wait(toks.get(("wdfree",)))
                G.waits(x_dead)
                tk_wd = P.dma(G, "wd", t.wd_sb[:], aps["wd"][e])
            for qg in range(NQG):
                bg = qg % 2
                TE.wait(tk_wg); TE.wait(tk_wu)
                TE.wait(toks[("xnq", qg)])
                if e < 8:
                    TE.wait(toks[("cbc", qg)])
                xn_r = t.xn[:, :, qg * QG:(qg + 1) * QG]
                P.bank_write(TE, bg)
                for kc in range(8):
                    mh = TE.matmul(t.pb[bg][:, 0:QG], t.wg_sb[bi][:, kc, :], xn_r[:, kc, :],
                              start=(kc == 0), stop=(kc == 7))
                tk_g = mh.then_inc(sems["t"], 1)
                P.bank_written(bg, tk_g)
                P.bank_write(TE, 2 + bg)
                for kc in range(8):
                    mh = TE.matmul(t.pb[2 + bg][:, 0:QG], t.wu_sb[bi][:, kc, :], xn_r[:, kc, :],
                              start=(kc == 0), stop=(kc == 7))
                tk_u = mh.then_inc(sems["t"], 1)
                P.bank_written(2 + bg, tk_u)
                if qg == NQG - 1:
                    toks[("wgfree", bi)] = tk_u
                P.bank_read(S, bg)
                S.wait(toks.get(("sgfree", bg)))
                tk_sg = S.activation(t.sg[bg][:], t.pb[bg][:, 0:QG], AF.Silu)\
                    .then_inc(sems["s"], 1)
                P.bank_read_done(bg, tk_sg)
                V.wait(tk_sg)
                P.bank_read(V, 2 + bg)
                V.wait(toks.get(("hfree",)))
                if e < 8:
                    V.wait(toks[("cbc", qg)])
                    tk1 = V.tensor_mul(t.sg[bg][:], t.sg[bg][:], t.pb[2 + bg][:, 0:QG])\
                        .then_inc(sems["v"], 1)
                    V.wait(tk1)
                    tk_h = V.tensor_mul(t.hprime[:, mc, qg * QG:(qg + 1) * QG],
                                        t.sg[bg][:], t.cbc[qg][:]).then_inc(sems["v"], 1)
                else:
                    tk_h = V.tensor_mul(t.hprime[:, mc, qg * QG:(qg + 1) * QG],
                                        t.sg[bg][:], t.pb[2 + bg][:, 0:QG])\
                        .then_inc(sems["v"], 1)
                P.bank_read_done(2 + bg, tk_h)
                toks[("sgfree", bg)] = tk_h
                toks[("h", mc, qg)] = tk_h
        TE.wait(tk_wd)
        for hc in range(8):
            for qg in range(NQG):
                bd = 4 + (hc * NQG + qg) % 2
                P.bank_write(TE, bd)
                for mc in range(16):
                    TE.wait(toks[("h", mc, qg)])
                    mh = TE.matmul(t.pb[bd][:, 0:QG], t.wd_sb[:, mc, hc, :],
                              t.hprime[:, mc, qg * QG:(qg + 1) * QG],
                              start=(mc == 0), stop=(mc == 15))
                tk_d = mh.then_inc(sems["t"], 1)
                P.bank_written(bd, tk_d)
                if hc == 7 and qg == NQG - 1:
                    toks[("wdfree",)] = tk_d
                    toks[("hfree",)] = tk_d
                P.bank_read(V, bd)
                V.wait(toks.get(("eofree", (hc * NQG + qg) % 2)))
                tk_ev = V.tensor_copy(t.eo[(hc * NQG + qg) % 2][:], t.pb[bd][:, 0:QG])\
                    .then_inc(sems["v"], 1)
                P.bank_read_done(bd, tk_ev)
                G.wait(tk_ev)
                last_acc = P.dma(G, f"acc{(hc * NQG + qg) % 2}",
                                 dram["comb"][:, hc, qg * QG:(qg + 1) * QG],
                                 t.eo[(hc * NQG + qg) % 2][:],
                                 accum_op=(ALU.bypass if e == 0 else ALU.add))
                toks[("eofree", (hc * NQG + qg) % 2)] = last_acc
                if e == 8:
                    moe_acc_toks.append(last_acc)
        for qg in range(NQG):
            toks[("cbcfree", qg)] = toks[("h", 15, qg)]

    moe_bar = [(P.dma_sems["acc0"], P.dma_sems["acc0"].n),
               (P.dma_sems["acc1"], P.dma_sems["acc1"].n)]
    for E_ in (V, S, TE, G, SY):
        E_.waits(moe_bar)
    tk_wkv = P.dma(G, "wkv", t.wkv[:], aps["wkv"])
    tk_wq = P.dma(G, "wq", t.wq[:], aps["wq"])
    tk_m01 = P.dma(SY, "m01", t.mask01[:], aps["mask01"])
    tk_cos = P.dma(SY, "cos", t.cos2[:], aps["cos2"])
    tk_trig = P.dma(SY, "sin", t.sin2[:], aps["sin2"])
    V.wait(tk_cos); V.wait(tk_trig); V.wait(tk_m01)

    TE.wait(tk_wkv); TE.wait(tk_wq)
    kvn_tok = None
    for qg in range(NQG):
        sl = slice(qg * QG, (qg + 1) * QG)
        for oc in range(2):
            b = oc
            P.bank_write(TE, b)
            for kc in range(8):
                mh = TE.matmul(t.pb[b][:, 0:QG], t.wkv[:, kc, oc * 128:(oc + 1) * 128],
                          t.xn[:, kc, sl], start=(kc == 0), stop=(kc == 7))
            tk = mh.then_inc(sems["t"], 1)
            P.bank_written(b, tk)
            P.bank_read(V, b)
            V.wait(toks.get(("kvct", oc)))
            tk = V.tensor_copy(t.xn_f32[oc][:], t.pb[b][:, 0:QG]).then_inc(sems["v"], 1)
            P.bank_read_done(b, tk)
            toks[("kvc", oc)] = tk
        for oc in range(4):
            b = 2 + oc % 2
            P.bank_write(TE, b)
            for kc in range(8):
                mh = TE.matmul(t.pb[b][:, 0:QG], t.wq[:, kc, oc * 128:(oc + 1) * 128],
                          t.xn[:, kc, sl], start=(kc == 0), stop=(kc == 7))
            tk = mh.then_inc(sems["t"], 1)
            P.bank_written(b, tk)
            P.bank_read(V, b)
            tk = V.tensor_copy(t.q_c[:, oc, sl], t.pb[b][:, 0:QG]).then_inc(sems["v"], 1)
            P.bank_read_done(b, tk)
        qc_tok = tk
        for oc in range(2):
            S.wait(toks[("kvc", oc)])
            S.wait(toks.get(("lns", oc)))
            tk_s = S.activation(t.xsq_bf[oc][:, 0:QG], t.xn_f32[oc][:], AF.Square)\
                .then_inc(sems["s"], 1)
            if oc == 0:
                P.bank_write(TE, 4)
            TE.wait(tk_s)
            tmm = TE.matmul(t.pb[4][0:1, 0:QG], t.ones_bf[:], t.xsq_bf[oc][:, 0:QG],
                            start=(oc == 0), stop=(oc == 1)).then_inc(sems["t"], 1)
            toks[("lns", oc)] = tmm
        P.bank_written(4, tmm)
        P.bank_read(V, 4)
        tk = V.tensor_scalar_mul(t.row_a[0:1, 0:QG], t.pb[4][0:1, 0:QG], 1.0 / 256.0)\
            .then_inc(sems["v"], 1)
        P.bank_read_done(4, tk)
        S.wait(tk)
        tk = S.activation(t.row_b[0:1, 0:QG], t.row_a[0:1, 0:QG], AF.Ln,
                          bias=t.eps_row6[0:1, :], scale=1.0).then_inc(sems["s"], 1)
        S.wait(tk)
        tk = S.activation(t.row_a[0:1, 0:QG], t.row_b[0:1, 0:QG], AF.Exp,
                          scale=-0.5).then_inc(sems["s"], 1)
        TE.wait(tk)
        P.bank_write(TE, 5)
        tkb = TE.matmul(t.pb[5][:, 0:QG], t.ones_row[0:1, :], t.row_a[0:1, 0:QG],
                        start=True, stop=True).then_inc(sems["t"], 1)
        P.bank_written(5, tkb)
        P.bank_read(V, 5)
        tkc = V.tensor_copy(t.bc_a[:, 0:QG], t.pb[5][:, 0:QG]).then_inc(sems["v"], 1)
        P.bank_read_done(5, tkc)
        for oc in range(2):
            V.wait(tkc)
            V.wait(toks.get(("kvnscr", oc)))
            tk1 = V.tensor_mul(t.xn_f32[oc][:], t.xn_f32[oc][:], t.bc_a[:, 0:QG])\
                .then_inc(sems["v"], 1)
            toks[("kvct", oc)] = tk1
            V.wait(tk1)
            kvn_tok = V.tensor_scalar(t.kvn[:, oc, sl], t.xn_f32[oc][:],
                                      t.gkv[:, oc:oc + 1], None, ALU.mult)\
                .then_inc(sems["v"], 1)
            toks[("kvnscr", oc)] = kvn_tok

    pending_norm = [None]

    def emit_norm(hh):
        for qg in range(NQG):
            S.wait(toks[("pavdone", qg)])
            S.wait(toks.get(("rowa_nrm",)))
            tk_r = S.activation(t.row_a[0:1, 0:QG], t.pav_sb[qg][64:65, 0:QG],
                                AF.Ln).then_inc(sems["s"], 1)
            toks[("rowa_nrm",)] = tk_r
            b = qg % 2
            TE.wait(tk_r)
            P.bank_write(TE, b)
            tk_b = TE.matmul(t.pb[b][0:64, 0:QG], t.ones_row[0:1, 0:64],
                             t.row_a[0:1, 0:QG], start=True, stop=True)\
                .then_inc(sems["t"], 1)
            P.bank_written(b, tk_b)
            P.bank_read(S, b)
            S.wait(toks.get(("wsbfree",)))
            tk_w = S.activation(t.wsb[qg % 2][:], t.pb[b][0:64, 0:QG],
                                AF.Exp, scale=-1.0).then_inc(sems["s"], 1)
            P.bank_read_done(b, tk_w)
            V.wait(tk_w)
            V.wait(toks.get(("aofree", qg % 2)))
            tk_o = V.tensor_mul(t.ao_sb[qg % 2][:], t.pav_sb[qg][0:64, 0:QG],
                                t.wsb[qg % 2][:]).then_inc(sems["v"], 1)
            toks[("wsbfree",)] = tk_o
            toks[("pavsb", qg)] = tk_o
            G.wait(tk_o)
            tk_st = P.dma(G, f"ao{qg % 2}",
                          dram["aout"][(hh % 2) * 64:(hh % 2) * 64 + 64, hh // 2,
                                       qg * QG:(qg + 1) * QG],
                          t.ao_sb[qg % 2][:])
            toks[("aofree", qg % 2)] = tk_st
            toks[("norm_done",)] = tk_st

    for s3 in range(3):
        V.memset(t.kpk[s3][64:128, :, :], 0.0)
        tk_z = V.memset(t.qpk[s3][64:128, :, :], 0.0).then_inc(sems["v"], 1)
    TE.wait(tk_z)

    quad_toks = {}
    pair_toks = {}

    def rope_tasks(hg):
        qb = hg % 2
        tasks = []

        def t_dma():
            G.wait(toks.get(("ropewfree", qb)))
            quad_toks[("w", hg)] = (
                P.dma(G, f"krw{qb}", t.krw[qb][:], aps["krope"][hg]),
                P.dma(G, f"krwp{qb}", t.krwp[qb][:], aps["krope_p"][hg]),
                P.dma(G, f"qrw{qb}", t.qrw[qb][:], aps["qrope"][hg]),
                P.dma(G, f"qrwp{qb}", t.qrwp[qb][:], aps["qrope_p"][hg]))
        tasks.append(t_dma)

        def grp(widx, wsb_f, dst_f, trig_f, nkc, rhs_f, qg, war_id):
            def run():
                wsb = wsb_f(); dst = dst_f(); trig = trig_f(); rhs = rhs_f()
                TE.wait(quad_toks[("w", hg)][widx])
                b = qg % 2
                tsl = slice(qg * QG, (qg + 1) * QG)
                P.bank_write(TE, b)
                for kc in range(nkc):
                    mh = TE.matmul(t.pb[b][:, 0:QG], wsb[:, kc, :], rhs[:, kc, tsl],
                                   start=(kc == 0), stop=(kc == nkc - 1))
                tk = mh.then_inc(sems["t"], 1)
                P.bank_written(b, tk)
                P.bank_read(V, b)
                V.wait(toks.get(("rr", war_id)))
                V.tensor_mul(dst[0:64, tsl], t.pb[b][0:64, 0:QG], trig[:, tsl])
                tk = V.tensor_mul(dst[64:128, tsl], t.pb[b][64:128, 0:QG],
                                  trig[:, tsl]).then_inc(sems["v"], 1)
                P.bank_read_done(b, tk)
                toks[("rordone", war_id)] = tk
            return run

        for qg in range(NQG):
            tasks.append(grp(0, lambda: t.krw[qb], lambda: t.ror_a, lambda: t.cos2,
                             8, lambda: t.xn, qg, "a"))
        for qg in range(NQG):
            tasks.append(grp(1, lambda: t.krwp[qb], lambda: t.ror_b, lambda: t.sin2,
                             8, lambda: t.xn, qg, "b"))

        def comb(pk_f):
            def run():
                V.wait(toks[("rordone", "a")])
                V.wait(toks[("rordone", "b")])
                tk_all = None
                for hh in range(4):
                    h = hg * 4 + hh
                    hp = h // 2
                    s3 = hp % 3
                    pk = pk_f()
                    V.wait(toks.get(("packfree", s3)))
                    tk_all = V.tensor_add(pk[s3][64:96, h % 2, :],
                                          t.ror_a[hh * 32:(hh + 1) * 32, :],
                                          t.ror_b[hh * 32:(hh + 1) * 32, :])\
                        .then_inc(sems["v"], 1)
                toks[("rr", "a")] = tk_all
                toks[("rr", "b")] = tk_all
                quad_toks[("ropec", hg, pk_f is not kf)] = tk_all
            return run
        kf = lambda: t.kpk
        tasks.append(comb(kf))
        for qg in range(NQG):
            tasks.append(grp(2, lambda: t.qrw[qb], lambda: t.ror_a, lambda: t.cos2,
                             4, lambda: t.q_c, qg, "a"))
        for qg in range(NQG):
            tasks.append(grp(3, lambda: t.qrwp[qb], lambda: t.ror_b, lambda: t.sin2,
                             4, lambda: t.q_c, qg, "b"))

        def comb_q_wrap():
            comb(lambda: t.qpk)()
            toks[("ropewfree", qb)] = toks[("rr", "a")]
        tasks.append(comb_q_wrap)
        return tasks

    def pair_tasks(hp):
        pb_ = hp % 2
        s3 = hp % 3
        tasks = []

        def t_dma():
            G.wait(toks.get(("pairwfree", pb_)))
            pair_toks[("w", hp)] = (
                P.dma(G, f"kdw{pb_}", t.kdw[pb_][:], aps["kdec"][hp]),
                P.dma(G, f"vdw{pb_}", t.vdw[pb_][:], aps["vdec"][hp]),
                P.dma(G, f"qdw{pb_}", t.qdw[pb_][:], aps["qdec"][hp]))
        tasks.append(t_dma)

        def grp(widx, wsb_f, nrc, rhs_f, pk_f, qg):
            def run():
                wsb = wsb_f(); rhs = rhs_f(); pk = pk_f()
                TE.wait(pair_toks[("w", hp)][widx])
                TE.wait(kvn_tok); TE.wait(qc_tok)
                b = qg % 2
                P.bank_write(TE, b)
                for rc in range(nrc):
                    mh = TE.matmul(t.pb[b][:, 0:QG], wsb[:, rc, :],
                                   rhs[:, rc, qg * QG:(qg + 1) * QG],
                                   start=(rc == 0), stop=(rc == nrc - 1))
                tk = mh.then_inc(sems["t"], 1)
                P.bank_written(b, tk)
                P.bank_read(V, b)
                V.wait(toks.get(("packfree", s3)))
                V.tensor_copy(pk[s3][0:64, 0, qg * QG:(qg + 1) * QG],
                              t.pb[b][0:64, 0:QG])
                tkk = V.tensor_copy(pk[s3][0:64, 1, qg * QG:(qg + 1) * QG],
                                    t.pb[b][64:128, 0:QG]).then_inc(sems["v"], 1)
                P.bank_read_done(b, tkk)
                pair_toks[("prod", hp)] = tkk
            return run

        for qg in range(NQG):
            tasks.append(grp(0, lambda: t.kdw[pb_], 2, lambda: t.kvn,
                             lambda: t.kpk, qg))
        for qg in range(NQG):
            tasks.append(grp(2, lambda: t.qdw[pb_], 4, lambda: t.q_c,
                             lambda: t.qpk, qg))

        def t_ones():
            V.wait(toks.get(("vTfree", pb_)))
            V.memset(t.vT[pb_][:, :, 64:65], 1.0)
            pair_toks[("ones", hp)] = V.memset(t.vT[pb_][:, :, 129:130], 1.0)\
                .then_inc(sems["v"], 1)
        tasks.append(t_ones)

        def vt_grp(kt):
            def run():
                TE.wait(pair_toks[("w", hp)][1])
                TE.wait(kvn_tok)
                b = kt % 2
                P.bank_write(TE, b)
                for rc in range(2):
                    mh = TE.matmul(t.pb[b][:, 0:128],
                                   t.kvn[:, rc, kt * 128:(kt + 1) * 128],
                                   t.vdw[pb_][:, rc, :], start=(rc == 0), stop=(rc == 1))
                tk = mh.then_inc(sems["t"], 1)
                P.bank_written(b, tk)
                E_ = V if b == 0 else S
                sk = "v" if b == 0 else "s"
                P.bank_read(E_, b)
                E_.wait(pair_toks[("ones", hp)])
                if b == 0:
                    E_.tensor_copy(t.vT[pb_][:, kt, 0:64], t.pb[b][:, 0:64])
                    tkk = E_.tensor_copy(t.vT[pb_][:, kt, 65:129], t.pb[b][:, 64:128])\
                        .then_inc(sems[sk], 1)
                else:
                    E_.copy(t.vT[pb_][:, kt, 0:64], t.pb[b][:, 0:64])
                    tkk = E_.copy(t.vT[pb_][:, kt, 65:129], t.pb[b][:, 64:128])\
                        .then_inc(sems[sk], 1)
                P.bank_read_done(b, tkk)
                pair_toks[("vT", hp) if b == 0 else ("vTs", hp)] = tkk
            return run
        for kt in range(16):
            tasks.append(vt_grp(kt))
        return tasks

    def flash_head(h, tasks):
        hp = h // 2
        hl = h % 2
        hg = h // 4
        pb_ = hp % 2
        s3 = hp % 3
        if pending_norm[0] is not None:
            emit_norm(pending_norm[0])
        pending_norm[0] = h
        TE.wait(pair_toks[("prod", hp)])
        TE.wait(pair_toks.get(("prods", hp)))
        TE.wait(quad_toks[("ropec", hg, False)])
        TE.wait(quad_toks[("ropec", hg, True)])
        TE.wait(pair_toks[("vT", hp)])
        TE.wait(pair_toks.get(("vTs", hp)))

        def emit_scores(kt):
            ep = kt % 3
            sc = {}
            for qg in range(kt // 4, NQG):
                off = kt * 128 - qg * QG
                lo = max(0, off)
                bs = 2 + qg % 2
                P.bank_write(TE, bs)
                mh = TE.matmul(t.pb[bs][:, lo:QG],
                               t.kpk[s3][:, hl, kt * 128:(kt + 1) * 128],
                               t.qpk[s3][:, hl, qg * QG + lo:(qg + 1) * QG],
                               start=True, stop=True)
                tk_sc = mh.then_inc(sems["t"], 1)
                P.bank_written(bs, tk_sc)
                S.wait(toks.get(("expTfree", ep)))
                P.bank_read(S, bs)
                tk_e = S.activation(t.expT[ep][:, qg * QG + lo:(qg + 1) * QG],
                                    t.pb[bs][:, lo:QG], AF.Exp,
                                    bias=t.neg10[:], scale=SCALE).then_inc(sems["s"], 1)
                P.bank_read_done(bs, tk_e)
                if qg == kt // 4:
                    G.wait(tk_e)
                    dsl = slice(qg * QG + lo, qg * QG + lo + 128)
                    tk_e = G.tensor_mul(t.expT[ep][:, dsl], t.expT[ep][:, dsl],
                                        t.mask01[:]).then_inc(sems["g"], 1)
                sc[qg] = (lo, tk_e)
            return sc

        def emit_av(kt, sc):
            ep = kt % 3
            tk_av = None
            for qg in range(kt // 4, NQG):
                lo, tk_e = sc[qg]
                bav = 4 + qg
                if kt == 0:
                    P.bank_write(TE, bav)
                TE.wait(tk_e)
                mh = TE.matmul(t.pav[qg][0:65, lo:QG],
                               t.vT[pb_][:, kt, hl * 65:hl * 65 + 65],
                               t.expT[ep][:, qg * QG + lo:(qg + 1) * QG],
                               start=(kt == 0), stop=(kt == 4 * qg + 3))
                tk_av = mh.then_inc(sems["t"], 1)
                if kt == 4 * qg + 3:
                    P.bank_written(bav, tk_av)
                    P.bank_read(V, bav)
                    V.wait(toks.get(("pavsb", qg)))
                    tk_pv = V.tensor_copy(t.pav_sb[qg][:], t.pav[qg][0:65, 0:QG])\
                        .then_inc(sems["v"], 1)
                    P.bank_read_done(bav, tk_pv)
                    toks[("pavdone", qg)] = tk_pv
            toks[("expTfree", ep)] = tk_av
            return tk_av

        sc_q = {}
        tk_av = None
        for kt in range(16):
            sc_q[kt] = emit_scores(kt)
            for _ in range(2):
                if tasks:
                    tasks.pop(0)()
            if kt >= 2:
                tk_av = emit_av(kt - 2, sc_q.pop(kt - 2))
        tk_av = emit_av(14, sc_q.pop(14))
        tk_av = emit_av(15, sc_q.pop(15))
        toks[("vTfree", pb_)] = tk_av
        return tk_av

    for task in rope_tasks(0):
        task()
    for task in pair_tasks(0):
        task()
    for hp in range(8):
        nxt = []
        if hp < 7:
            if hp % 2 == 1:
                nxt += rope_tasks(hp // 2 + 1)
            nxt += pair_tasks(hp + 1)
        for hl in range(2):
            tk_av = flash_head(hp * 2 + hl, nxt)
        for task in nxt:
            task()
        toks[("pairfree", hp % 2)] = tk_av
        toks[("pairwfree", hp % 2)] = tk_av
        toks[("packfree", hp % 3)] = tk_av
        if hp == 7:
            for E_ in (SY, G):
                E_.wait(pair_toks[("prod", 7)])
                E_.wait(pair_toks.get(("prods", 7)))
                E_.wait(pair_toks[("vT", 7)])
                E_.wait(pair_toks.get(("vTs", 7)))
                E_.wait(quad_toks[("ropec", 3, False)])
                E_.wait(quad_toks[("ropec", 3, True)])
                E_.wait(toks[("vTfree", 0)])
            tk_wo = P.dma(G, "wo", t.wo_sb[:], aps["wo"])
            tk_xt0 = P.dma(SY, "xt0", t.xtile[0][:], x_in_ap[:, :, 0:QG])
            SY.waits(moe_acc_toks)
            tk_cb0 = P.dma(SY, "cb0", t.ctile[0][:], dram["comb"][:, :, 0:QG])
    if pending_norm[0] is not None:
        emit_norm(pending_norm[0])
    tk_o = toks[("norm_done",)]
    aout_done = [(P.dma_sems["ao0"], P.dma_sems["ao0"].n),
                 (P.dma_sems["ao1"], P.dma_sems["ao1"].n)]

    G.waits(aout_done); G.wait(tk_o)
    for E_ in (V, S, TE):
        E_.waits(aout_done)
        E_.wait(tk_o)
    SY.waits(aout_done)
    SY.waits(moe_acc_toks)
    tk_x1 = None
    for tt in range(NQG):
        bi = tt % 2
        x1dst = t.x1cb[tt] if tt < 2 else t.x1t[tt - 2]
        SY.wait(toks.get(("updfree", bi)))
        if tt == 0:
            tk_xt = tk_xt0
            tk_cb = tk_cb0
        else:
            tk_xt = P.dma(SY, f"xt{bi}", t.xtile[bi][:],
                          x_in_ap[:, :, tt * QG:(tt + 1) * QG])
            tk_cb = P.dma(SY, f"cb{bi}", t.ctile[bi][:],
                          dram["comb"][:, :, tt * QG:(tt + 1) * QG])
        tk_at = P.dma(SY, f"at{bi}", t.atile[bi][:],
                      dram["aout"][:, :, tt * QG:(tt + 1) * QG])
        for oc in range(8):
            b = oc % 2
            TE.wait(tk_at); TE.wait(tk_wo)
            P.bank_write(TE, b)
            for ic in range(8):
                mh = TE.matmul(t.pb[b][:, 0:QG], t.wo_sb[:, ic, oc, :],
                          t.atile[bi][:, ic, :], start=(ic == 0), stop=(ic == 7))
            tk_s = mh.then_inc(sems["t"], 1)
            P.bank_written(b, tk_s)
            S.wait(tk_cb)
            S.wait(toks.get(("gtfree", b)))
            tk_g = S.activation(t.gt[b][:, 0:QG], t.ctile[bi][:, oc, :], AF.Silu)\
                .then_inc(sems["s"], 1)
            P.bank_read(V, b)
            V.wait(tk_g); V.wait(tk_xt)
            tk1 = V.tensor_mul(t.gt[b][:, 0:QG], t.gt[b][:, 0:QG], t.pb[b][:, 0:QG])\
                .then_inc(sems["v"], 1)
            P.bank_read_done(b, tk1)
            V.wait(tk1)
            tk_x1 = V.tensor_add(x1dst[:, oc, :], t.gt[b][:, 0:QG],
                                 t.xtile[bi][:, oc, :]).then_inc(sems["v"], 1)
            toks[("gtfree", b)] = tk_x1
        if tt < 2:
            toks[("x1add", tt)] = tk_x1
        G.wait(tk_x1)
        tk_out = P.dma(G, f"x1o{bi}", x1_out_ap[:, :, tt * QG:(tt + 1) * QG],
                       x1dst[:])
        toks[("updfree", bi)] = tk_out
        toks[("x1out", bi)] = tk_out
        toks[("x1wb", tt)] = tk_out
    return [toks[("x1out", 0)], toks[("x1out", 1)]]


def build_l1(P, t, aps, x1_in_ap, out_ap, sems, entry_bar=None, wl1=None):
    V, S, TE, G, SY = (P.on("vector"), P.on("scalar"), P.on("tensor"),
                       P.on("gpsimd"), P.on("sync"))
    toks = P.toks
    NPOOL = len(t.pool)
    x1tok = {}

    def x1load(ci):
        if ci >= NTC:
            return
        b = ci % 2
        if ci < 2:
            if entry_bar is not None:
                x1tok[ci] = toks[("x1add", ci)]
            else:
                SY.wait(toks.get(("x1free", b)))
                x1tok[ci] = P.dma(SY, f"x1c{b}", t.x1cb[b][:],
                                  x1_in_ap[:, :, ci * TC:(ci + 1) * TC])
        else:
            G.wait(toks.get(("x1free", b)))
            x1tok[ci] = P.dma(G, f"x1g{b}", t.x1cb[b][:],
                              x1_in_ap[:, :, ci * TC:(ci + 1) * TC])
    if entry_bar is not None:
        for E_ in (V, S, TE, G, SY):
            E_.waits(entry_bar)
    x1load(0)
    x1load(1)
    pool_i = [0]

    def wtile(src_ap):
        i = pool_i[0]
        b = i % NPOOL
        pool_i[0] += 1
        ser = i % 96
        chunk = i // 96
        G.wait(toks.get(("poolfree", b)))
        G.wait(toks.get(("poolwb", b)))
        SY.wait(toks.get(("poolfree", b)))
        SY.wait(toks.get(("poolwb", b)))
        if wl1 is None:
            tk = P.dma(G, f"wt{b}", t.pool[b][:], src_ap)
        elif chunk == 0:
            tk = P.dma(G, f"wt{b}", t.pool[b][:], src_ap)
            SY.wait(tk)
            tkw = P.dma(SY, f"wls{b}", wl1[ser], t.pool[b][:])
            toks[("wl1w", ser)] = tkw
            toks[("poolwb", b)] = tkw
        else:
            SY.wait(toks[("wl1w", ser)])
            tk = P.dma(SY, f"wth{b}", t.pool[b][:], wl1[ser])
        return b, tk

    consts = [("l1vecs", t.l1vecs), ("b1", t.b1)]
    if entry_bar is None:
        consts += [("ones_bf", t.ones_bf), ("ones_row", t.ones_row)]
    tk_c2 = None
    for nm, dst in consts:
        tk_c2 = P.dma(SY, "l1const", dst[:], aps[nm])
    tk_vecs = tk_c2
    S.wait(tk_vecs)
    S.activation(t.decay[:], t.l1vecs[:, 3, :], AF.Sigmoid)
    tk_df = S.activation(t.first[:], t.l1vecs[:, 4, :], AF.Exp).then_inc(sems["s"], 1)
    V.memset(t.carry_s[:], 0.0)
    tk_cx = V.memset(t.carry_x[:], 0.0).then_inc(sems["v"], 1)
    tk_cs = tk_cx
    if entry_bar is None:
        V.memset(t.eps_row[:], 1e-5)

    MIXK, MIXV, MIXR, LNW, LNB, B2 = 0, 1, 2, 5, 6, 7
    tk_st = None
    heads = {}

    def ln_head(ci):
        b = ci % 2
        xc_ = t.x1cb[b]
        tk_x1 = x1tok[ci]
        for E_ in (G, S, V, TE):
            E_.wait(tk_x1)
        TE.wait(tk_c2)
        st_tok = ln_stats(P, t, sems, lambda kc: xc_[:, kc, :], TC, 2, 3,
                          cast_s=True)
        row_tok = ln_rows(P, t, sems, TC, 2, 3, t.m_row[0:1, 0:TC], t.rstd_row[0:1, 0:TC],
                          1.0 / 1024.0, t.eps_row[0:1, :],
                          rstd2_dst=t.rstd2_row[0:1, 0:TC])
        V.wait(row_tok)
        tk_r3 = V.tensor_mul(t.row_a[0:1, 0:TC], t.rstd_row[0:1, 0:TC],
                             t.rstd2_row[0:1, 0:TC]).then_inc(sems["v"], 1)
        bc_tok = bcast2(P, t, sems, TC, t.m_row[0:1, 0:TC], t.rstd_row[0:1, 0:TC],
                        2, 3, row_tok)
        TE.wait(tk_r3)
        P.bank_write(TE, 2)
        tk_b3 = TE.matmul(t.pb[2][:, 0:TC], t.ones_row[0:1, :], t.row_a[0:1, 0:TC],
                          start=True, stop=True).then_inc(sems["t"], 1)
        P.bank_written(2, tk_b3)
        P.bank_read(V, 2)
        tk_bc3 = V.tensor_copy(t.bc_c[:, 0:TC], t.pb[2][:, 0:TC]).then_inc(sems["v"], 1)
        P.bank_read_done(2, tk_bc3)
        V.wait(bc_tok)
        V.wait(toks.get(("xncfree",)))
        for kc in range(8):
            V.tensor_sub(t.xnc[:, kc, :], xc_[:, kc, :], t.bc_a[:, 0:TC])
        tk = V.memset(t.junk[:], 0.0).then_inc(sems["v"], 1)
        V.wait(tk)
        V.wait(toks.get(("xnbfree",)))
        for kc in range(8):
            V.tensor_mul(t.xnb_c[:, kc, :], t.xnc[:, kc, :], t.bc_b[:, 0:TC])
        heads[ci] = (V.memset(t.junk[:], 0.0).then_inc(sems["v"], 1), tk_bc3)

    mheads = {}
    cxtok = [None]

    def mix_head(ci):
        tk_xnb, tk_bc3 = heads[ci]
        dif = t.dif2[ci % 2]
        V.wait(tk_bc3); V.wait(tk_xnb); V.wait(tk_vecs)
        V.wait(toks.get(("xt1free",)))
        for kc in range(8):
            V.tensor_mul(t.xt1[:, kc, :], t.xnc[:, kc, :], t.bc_c[:, 0:TC])
        tk = V.memset(t.junk[:], 0.0).then_inc(sems["v"], 1)
        V.wait(tk)
        for kc in range(8):
            V.tensor_scalar(t.xt1[:, kc, :], t.xt1[:, kc, :],
                            t.l1vecs[:, LNW, kc:kc + 1], t.l1vecs[:, LNB, kc:kc + 1],
                            ALU.mult, ALU.add)
        tk_xt1 = V.memset(t.junk[:], 0.0).then_inc(sems["v"], 1)
        V.wait(tk_xt1); V.wait(cxtok[0])
        for kc in range(8):
            V.tensor_sub(dif[:, kc, 1:TC], t.xt1[:, kc, 1:TC], t.xt1[:, kc, 0:TC - 1])
            V.tensor_sub(dif[:, kc, 0:1], t.xt1[:, kc, 0:1], t.carry_x[:, kc:kc + 1])
        tk_dif = V.memset(t.junk[:], 0.0).then_inc(sems["v"], 1)
        V.wait(tk_dif)
        V.wait(toks.get(("mixfree", "mix0")))
        for kc in range(8):
            V.scalar_tensor_tensor(t.mix0[:, kc, 1:TC], dif[:, kc, 1:TC],
                                   t.l1vecs[:, MIXK, kc:kc + 1],
                                   t.xt1[:, kc, 0:TC - 1], ALU.mult, ALU.add)
            V.scalar_tensor_tensor(t.mix0[:, kc, 0:1], dif[:, kc, 0:1],
                                   t.l1vecs[:, MIXK, kc:kc + 1],
                                   t.carry_x[:, kc:kc + 1], ALU.mult, ALU.add)
        tk_mixk = V.memset(t.junk[:], 0.0).then_inc(sems["v"], 1)
        mheads[ci] = (tk_xt1, tk_dif, tk_mixk)

    ln_head(0)
    mix_head(0)
    for ci in range(NTC):
        bi = ci % 2
        xc = t.x1cb[bi]
        sl = slice(ci * TC, (ci + 1) * TC)
        tk_xnb, tk_bc3 = heads[ci]
        TE.wait(tk_xnb)
        for oc in range(32):
            wb, tk_w = wtile(aps["w1"][oc])
            b = 4 + oc % 2
            TE.wait(tk_w)
            P.bank_write(TE, b)
            for kc in range(8):
                mh = TE.matmul(t.pb[b][:, 0:TC], t.pool[wb][:, kc, :], t.xnb_c[:, kc, :],
                               start=(kc == 0), stop=(kc == 7))
            tk = mh.then_inc(sems["t"], 1)
            P.bank_written(b, tk)
            toks[("poolfree", wb)] = tk
            if oc == 31:
                toks[("xnbfree",)] = tk
            P.bank_read(S, b)
            S.wait(tk_vecs)
            tk2 = S.activation(t.h1[:, oc, :], t.pb[b][:, 0:TC], AF.Relu,
                               bias=t.b1[:, oc:oc + 1], scale=1.0).then_inc(sems["s"], 1)
            P.bank_read_done(b, tk2)
            toks[("h1", oc)] = tk2
        tk_xt1, tk_dif, tk_mixk = mheads[ci]
        dif = t.dif2[bi]
        Ss = t.Ss2[bi]
        Kt = t.K2[bi]
        s_c = t.s_c2[bi]
        mix_specs = [(MIXK, Kt, None, "K", aps["wk"], t.mix0, "mix0"),
                     (MIXV, t.Vv, None, "V", aps["wv"], t.mix1, "mix1"),
                     (MIXR, t.R, AF.Sigmoid, "R", aps["wr"], t.mix0, "mix0")]
        for mi, (mixcol, dst, act, key, wap, xin, xkey) in enumerate(mix_specs):
            if mi == 0:
                tk_mix = tk_mixk
            else:
                V.wait(tk_dif)
                V.wait(toks.get(("mixfree", xkey)))
                if xin is t.mix1:
                    V.waits(toks.get(("outfree",), []))
                for kc in range(8):
                    V.scalar_tensor_tensor(xin[:, kc, 1:TC], dif[:, kc, 1:TC],
                                           t.l1vecs[:, mixcol, kc:kc + 1],
                                           t.xt1[:, kc, 0:TC - 1], ALU.mult, ALU.add)
                    V.scalar_tensor_tensor(xin[:, kc, 0:1], dif[:, kc, 0:1],
                                           t.l1vecs[:, mixcol, kc:kc + 1],
                                           t.carry_x[:, kc:kc + 1], ALU.mult, ALU.add)
                tk_mix = V.memset(t.junk[:], 0.0).then_inc(sems["v"], 1)
            TE.wait(tk_mix)
            for oc in range(8):
                wb, tk_w = wtile(wap[oc])
                b = 4 + oc % 2
                TE.wait(tk_w)
                P.bank_write(TE, b)
                for kc in range(8):
                    mh = TE.matmul(t.pb[b][:, 0:TC], t.pool[wb][:, kc, :], xin[:, kc, :],
                                   start=(kc == 0), stop=(kc == 7))
                tk = mh.then_inc(sems["t"], 1)
                P.bank_written(b, tk)
                toks[("poolfree", wb)] = tk
                if oc == 7:
                    toks[("mixfree", xkey)] = tk
                if act is None:
                    P.bank_read(V, b)
                    if dst is t.Vv:
                        V.waits(toks.get(("outfree",), []))
                    tk2 = V.tensor_copy(dst[:, oc, :], t.pb[b][:, 0:TC])\
                        .then_inc(sems["v"], 1)
                else:
                    P.bank_read(S, b)
                    tk2 = S.activation(dst[:, oc, :], t.pb[b][:, 0:TC], act)\
                        .then_inc(sems["s"], 1)
                P.bank_read_done(b, tk2)
                toks[(key, oc)] = tk2
        for kc in range(8):
            V.tensor_mul(t.xnc[:, kc, :], t.xnc[:, kc, :], t.bc_b[:, 0:TC])
        tk_xnf = V.memset(t.junk[:], 0.0).then_inc(sems["v"], 1)
        V.wait(tk_df); V.wait(tk_cs)
        for kc in range(8):
            V.wait(toks[("K", kc)])
            V.tensor_tensor_scan(Ss[:, kc, :],
                                 t.decay[:, kc:kc + 1].broadcast_to([128, TC]),
                                 Kt[:, kc, :], t.carry_s[:, kc:kc + 1],
                                 ALU.mult, ALU.add)
        tk_scan = V.memset(t.junk[:], 0.0).then_inc(sems["v"], 1)
        V.wait(tk_scan)
        tk_cs = V.tensor_copy(t.carry_s[:], Ss[:, :, TC - 1]).then_inc(sems["v"], 1)
        V.wait(tk_cs)
        for kc in range(8):
            V.scalar_tensor_tensor(dif[:, kc, :], Ss[:, kc, :], t.first[:, kc:kc + 1],
                                   Kt[:, kc, :], ALU.mult, ALU.add)
        tk = V.memset(t.junk[:], 0.0).then_inc(sems["v"], 1)
        V.wait(tk)
        for kc in range(8):
            V.wait(toks[("V", kc)])
            V.tensor_mul(dif[:, kc, :], dif[:, kc, :], t.Vv[:, kc, :])
        tk = V.memset(t.junk[:], 0.0).then_inc(sems["v"], 1)
        V.wait(tk)
        V.wait(toks.get(("mixfree", "mix0")))
        for kc in range(8):
            V.wait(toks[("R", kc)])
            V.tensor_mul(t.rk[:, kc, :], dif[:, kc, :], t.R[:, kc, :])
        tk_rk = V.memset(t.junk[:], 0.0).then_inc(sems["v"], 1)
        V.wait(tk_rk)
        tk_cx = V.tensor_copy(t.carry_x[:], t.xt1[:, :, TC - 1]).then_inc(sems["v"], 1)
        toks[("xt1free",)] = tk_cx
        cxtok[0] = tk_cx
        TE.wait(tk_rk)
        for oc in range(8):
            wb, tk_w = wtile(aps["wo1"][oc])
            b = 4 + oc % 2
            TE.wait(tk_w)
            P.bank_write(TE, b)
            for kc in range(8):
                mh = TE.matmul(t.pb[b][:, 0:TC], t.pool[wb][:, kc, :], t.rk[:, kc, :],
                               start=(kc == 0), stop=(kc == 7))
            tk = mh.then_inc(sems["t"], 1)
            P.bank_written(b, tk)
            toks[("poolfree", wb)] = tk
            if oc == 7:
                toks[("mixfree", "mix0")] = tk
            P.bank_read(V, b)
            V.wait(tk_xnf)
            tk2 = V.tensor_add(s_c[:, oc, :], t.pb[b][:, 0:TC], t.xnc[:, oc, :])\
                .then_inc(sems["v"], 1)
            P.bank_read_done(b, tk2)
            toks[("s", oc)] = tk2
        toks[("xncfree",)] = tk2
        if ci + 1 < NTC:
            ln_head(ci + 1)
            mix_head(ci + 1)
        for ic in range(32):
            wb, tk_w = wtile(aps["w2"][ic])
            TE.wait(tk_w)
            TE.wait(toks[("h1", ic)])
            if ic == 0:
                for hc in range(8):
                    P.bank_write(TE, hc)
            for hc in range(8):
                mh = TE.matmul(t.pb[hc][:, 0:TC], t.pool[wb][:, hc, :], t.h1[:, ic, :],
                               start=(ic == 0), stop=(ic == 31))
            tk = mh.then_inc(sems["t"], 1)
            toks[("poolfree", wb)] = tk
        tk_f = tk
        toks[("h1free",)] = tk_f
        toks[("xnbfree",)] = tk_f
        for hc in range(8):
            P.bank_written(hc, tk_f)
        for hc in range(8):
            b = hc % 2
            P.bank_read(S, hc)
            S.wait(toks.get(("gtfree", b)))
            tk_g = S.activation(t.gt[b][:, 0:TC], t.pb[hc][:, 0:TC], AF.Silu,
                                bias=t.l1vecs[:, B2, hc:hc + 1], scale=1.0)\
                .then_inc(sems["s"], 1)
            P.bank_read_done(hc, tk_g)
            V.wait(tk_g)
            V.wait(toks[("s", hc)])
            V.waits(toks.get(("outfree",), []))
            tk1 = V.tensor_mul(t.gt[b][:, 0:TC], t.gt[b][:, 0:TC], s_c[:, hc, :])\
                .then_inc(sems["v"], 1)
            V.wait(tk1)
            tk_o = V.tensor_add(t.outc[:, hc, :], t.gt[b][:, 0:TC],
                                xc[:, hc, :]).then_inc(sems["v"], 1)
            toks[("gtfree", b)] = tk_o
            if hc == 7:
                toks[("x1free", bi)] = tk_o
            G.wait(tk_o)
            G.wait(toks.get(("ost", bi, hc % 2)))
            tk_st = P.dma(G, f"o{bi}{hc % 2}", out_ap[:, hc, sl],
                          t.outc[:, hc, :])
            toks[("ost", bi, hc % 2)] = tk_st
            if hc >= 6:
                toks[("outdone", bi, hc % 2)] = tk_st
        toks[("outfree",)] = [toks[("outdone", bi, 0)], toks[("outdone", bi, 1)]]
        x1load(ci + 2)
    return [toks[("outdone", 0, 0)], toks[("outdone", 0, 1)],
            toks[("outdone", 1, 0)], toks[("outdone", 1, 1)]]


def alloc_shared(nc, lay, t, w=QG):
    t.ones_bf = sb(nc, lay, "ones_bf", [128, 1], BF16)
    t.ones_row = sb(nc, lay, "ones_row", [1, 128], F32)
    t.neg10 = sb(nc, lay, "neg10", [128, 1], F32)
    t.eps_row = sb(nc, lay, "eps_row", [1, 1], F32)
    t.eps_row6 = sb(nc, lay, "eps_row6", [1, 1], F32)
    t.junk = sb(nc, lay, "junk", [1, 1], F32)
    t.m_row = sb(nc, lay, "m_row", [1, w], F32)
    t.rstd_row = sb(nc, lay, "rstd_row", [1, w], F32)
    t.rstd2_row = sb(nc, lay, "rstd2_row", [1, w], F32)
    t.row_a = sb(nc, lay, "row_a", [1, w], F32)
    t.row_b = sb(nc, lay, "row_b", [1, w], F32)
    t.bc_a = sb(nc, lay, "bc_a", [128, w], F32)
    t.bc_b = sb(nc, lay, "bc_b", [128, w], F32)
    t.xb_bf = [sb(nc, lay, f"xb_bf{i}", [128, w], BF16) for i in range(2)]
    t.xsq_bf = [sb(nc, lay, f"xsq_bf{i}", [128, w], BF16) for i in range(2)]
    t.gt = [sb(nc, lay, f"gt{i}", [128, w], F32) for i in range(2)]
    return t


def _region(lay):
    r = Bump(lay.limit)
    r.pos = lay.mark()
    r.peak = r.pos
    return r


def alloc_l0(nc, lay, t):
    t.eye = sb(nc, lay, "eye", [128, 128], F32)
    t.sel = sb(nc, lay, "sel", [8, 8, 128], F32)
    t.gate_wt = sb(nc, lay, "gate_wt", [128, 8, 8], F32)
    t.beta = sb(nc, lay, "beta", [8, 1], F32)
    t.gkv = sb(nc, lay, "gkv", [128, 2], F32)
    csb_off = lay.alloc(T * 4)
    t.c_sb = nc.alloc_sbuf_tensor_at("c_sb", [8, T], F32, offset=csb_off)
    t.csb_off = csb_off
    t.xn = sb(nc, lay, "xn", [128, 8, T], BF16)
    t.xn_f32 = [sb(nc, lay, f"xn_f32{i}", [128, QG], F32) for i in range(2)]
    t.lb_tok = sb(nc, lay, "lb_tok", [128, 128], F32)
    t.m1 = sb(nc, lay, "m1", [128, 16], F32)
    t.ge1 = sb(nc, lay, "ge1", [128, 128], F32)
    t.msk = sb(nc, lay, "msk", [128, 128], F32)

    rLN = _region(lay)
    t.x = sb(nc, rLN, "x", [128, 8, T], F32)
    t.logits = sb(nc, rLN, "logits", [8, T], F32)
    t.sig = sb(nc, rLN, "sig", [8, T], F32)
    t.lb = sb(nc, rLN, "lb", [8, T], F32)
    t.mask_fm = sb(nc, rLN, "mask_fm", [8, T], F32)

    rMOE = _region(lay)
    t.wd_sb = sb(nc, rMOE, "wd_sb", [128, 16, 8, 128], BF16)
    t.hprime = sb(nc, rMOE, "hprime", [128, 16, T], BF16)
    rMOE.pos = max(rMOE.pos, rLN.pos)
    t.wg_sb = [sb(nc, rMOE, f"wg_sb{i}", [128, 8, 128], BF16) for i in range(2)]
    t.wu_sb = [sb(nc, rMOE, f"wu_sb{i}", [128, 8, 128], BF16) for i in range(2)]
    t.sg = [sb(nc, rMOE, f"sg{i}", [128, QG], F32) for i in range(2)]
    t.cbc = [sb(nc, rMOE, f"cbc{i}", [128, QG], F32) for i in range(NQG)]
    t.eo = [sb(nc, rMOE, f"eo{i}", [128, QG], F32) for i in range(2)]

    rATT = _region(lay)
    t.wkv = sb(nc, rATT, "wkv", [128, 8, 256], BF16)
    wq_off = rATT.alloc(8 * 512 * 2)
    t.wq = nc.alloc_sbuf_tensor_at("wq", [128, 8, 512], BF16, offset=wq_off)
    t.pav_sb = [nc.alloc_sbuf_tensor_at(f"pav_sb{i}", [65, QG], F32,
                                        offset=wq_off + i * QG * 4)
                for i in range(NQG)]
    t.cos2 = sb(nc, rATT, "cos2", [64, T], BF16)
    t.sin2 = sb(nc, rATT, "sin2", [64, T], BF16)
    t.kvn = sb(nc, rATT, "kvn", [128, 2, T], BF16)
    t.q_c = sb(nc, rATT, "q_c", [128, 4, T], BF16)
    t.krw = [sb(nc, rATT, f"krw{i}", [128, 8, 128], BF16) for i in range(2)]
    t.krwp = [sb(nc, rATT, f"krwp{i}", [128, 8, 128], BF16) for i in range(2)]
    t.qrw = [sb(nc, rATT, f"qrw{i}", [128, 4, 128], BF16) for i in range(2)]
    t.qrwp = [sb(nc, rATT, f"qrwp{i}", [128, 4, 128], BF16) for i in range(2)]
    t.kdw = [sb(nc, rATT, f"kdw{i}", [128, 2, 128], BF16) for i in range(2)]
    t.vdw = [sb(nc, rATT, f"vdw{i}", [128, 2, 128], BF16) for i in range(2)]
    t.qdw = [sb(nc, rATT, f"qdw{i}", [128, 4, 128], BF16) for i in range(2)]
    t.vT = [sb(nc, rATT, f"vT{i}", [128, 16, 130], BF16) for i in range(2)]
    t.mask01 = sb(nc, rATT, "mask01", [128, 128], BF16)
    wsb1 = sb(nc, rATT, "wsb", [64, QG], F32)
    t.wsb = [wsb1, wsb1]
    t.ao_sb = [sb(nc, rATT, f"ao_sb{i}", [64, QG], BF16) for i in range(2)]
    t.ror_a = nc.alloc_sbuf_tensor_at("ror_a", [128, T], BF16, offset=t.csb_off)
    t.ror_b = nc.alloc_sbuf_tensor_at("ror_b", [128, T], BF16, offset=t.csb_off + T * 2)
    t.kpk = [sb(nc, rATT, f"kpk{i}", [128, 2, T], BF16) for i in range(3)]
    t.qpk = [sb(nc, rATT, f"qpk{i}", [128, 2, T], BF16) for i in range(3)]
    t.expT = [sb(nc, rATT, f"expT{i}", [128, T], BF16) for i in range(3)]

    rUPD = _region(lay)
    t.atile = [sb(nc, rUPD, f"atile{i}", [128, 8, QG], BF16) for i in range(2)]
    t.wo_sb = sb(nc, rUPD, "wo_sb", [128, 8, 8, 128], BF16)
    xt0 = sb(nc, rUPD, "xtile0", [128, 8, QG], F32)
    ct0 = sb(nc, rUPD, "ctile0", [128, 8, QG], F32)
    xt1 = sb(nc, rUPD, "xtile1", [128, 8, QG], F32)
    ct1 = sb(nc, rUPD, "ctile1", [128, 8, QG], F32)
    t.xtile = [xt0, xt1]
    t.ctile = [ct0, ct1]
    t.x1t = [sb(nc, rUPD, f"x1t{i}", [128, 8, QG], F32) for i in range(2)]
    lay.pos = max(rLN.peak, rMOE.peak, rATT.peak, rUPD.peak)
    lay.peak = max(lay.peak, lay.pos)
    assert lay.pos <= lay.limit, f"L0 SBUF overflow {lay.pos}"
    return t


def alloc_l1(nc, lay, t):
    NPOOL = 6
    t.l1vecs = sb(nc, lay, "l1vecs", [128, 8, 8], F32)
    t.b1 = sb(nc, lay, "b1", [128, 32], F32)
    t.decay = sb(nc, lay, "decay", [128, 8], F32)
    t.first = sb(nc, lay, "first", [128, 8], F32)
    t.carry_s = sb(nc, lay, "carry_s", [128, 8], F32)
    t.carry_x = sb(nc, lay, "carry_x", [128, 8], F32)
    t.pool = [sb(nc, lay, f"pool{i}", [128, 8, 128], BF16) for i in range(NPOOL)]
    t.x1cb = [sb(nc, lay, f"x1c{i}", [128, 8, TC], F32) for i in range(2)]
    t.bc_c = sb(nc, lay, "bc_c", [128, TC], F32)
    t.xnc = sb(nc, lay, "xnc", [128, 8, TC], F32)
    t.xnb_c = sb(nc, lay, "xnb_c", [128, 8, TC], BF16)
    t.xt1 = sb(nc, lay, "xt1", [128, 8, TC], F32)
    doff = lay.alloc(8 * TC * 4)
    koff = lay.alloc(8 * TC * 4)
    offs = [doff, koff]
    t.dif2 = [nc.alloc_sbuf_tensor_at(f"dif{i}", [128, 8, TC], F32, offset=offs[i])
              for i in range(2)]
    t.s_c2 = [nc.alloc_sbuf_tensor_at(f"s_c{i}", [128, 8, TC], F32, offset=offs[i])
              for i in range(2)]
    t.Ss2 = [nc.alloc_sbuf_tensor_at(f"Ss{i}", [128, 8, TC], F32, offset=offs[i])
             for i in range(2)]
    t.K2 = [nc.alloc_sbuf_tensor_at(f"K{i}", [128, 8, TC], F32, offset=offs[1 - i])
            for i in range(2)]
    t.h1 = sb(nc, lay, "h1", [128, 32, TC], BF16)
    moff = lay.alloc(8 * TC * 2)
    t.mix0 = nc.alloc_sbuf_tensor_at("mix0", [128, 8, TC], BF16, offset=moff)
    t.rk = nc.alloc_sbuf_tensor_at("rk", [128, 8, TC], BF16, offset=moff)
    m1off = lay.alloc(8 * TC * 2)
    t.mix1 = nc.alloc_sbuf_tensor_at("mix1", [128, 8, TC], BF16, offset=m1off)
    voff = lay.alloc(8 * TC * 2)
    assert voff == m1off + 8 * TC * 2
    t.Vv = nc.alloc_sbuf_tensor_at("Vv", [128, 8, TC], BF16, offset=voff)
    t.outc = nc.alloc_sbuf_tensor_at("outc", [128, 8, TC], F32, offset=m1off)
    t.mix = [t.mix0, t.mix1]
    t.R = t.mix1
    assert lay.pos <= lay.limit, f"L1 SBUF overflow {lay.pos}"
    return t


L0_INPUTS = ["wkv", "wq", "wo", "kdec", "vdec", "qdec", "krope", "krope_p",
             "qrope", "qrope_p", "gate_wt", "wg", "wu", "wd", "gkv", "beta",
             "eye", "mask01", "ones_bf", "ones_row", "cos2", "sin2", "sel"]
L1_INPUTS = ["wk", "wv", "wr", "wo1", "w1", "w2", "l1vecs", "b1",
             "ones_bf", "ones_row"]


def mk_sems(P):
    d = {k: P.sem(k) for k in ["dx", "dc", "dw", "dacc", "dao", "dx1", "dout"]}
    for k in ["t", "v", "s", "g"]:
        d[k] = P.sem(k, chained=True)
    return d


def build_program_l0(packed, const):
    nc = bass.Bass("TRN2", target_bir_lowering=False)
    x_in = nc.dram_tensor("x", [128, 8, T], F32, kind="ExternalInput").ap()
    x1_out = nc.dram_tensor("x1", [128, 8, T], F32, kind="ExternalOutput").ap()
    aps = declare_inputs(nc, packed, const, L0_INPUTS)
    dram = {
        "xspill": nc.dram_tensor("xspill", [128, 8, T], F32, kind="Internal").ap(),
        "comb": nc.dram_tensor("comb", [128, 8, T], F32, kind="Internal").ap(),
        "aout": nc.dram_tensor("aout", [128, 8, T], BF16, kind="Internal").ap(),
    }
    lay = Bump()
    t = NS()
    alloc_shared(nc, lay, t)
    alloc_l0(nc, lay, t)
    t.pb = [nc.alloc_psum_tensor(f"pb{i}", [128, QG], F32) for i in range(8)]
    t.pav = {qg: t.pb[4 + qg] for qg in range(NQG)}
    P = Prog(nc)
    sems = mk_sems(P)
    out_toks = build_l0(P, t, aps, x_in, x1_out, sems, dram)
    P.on("sync").waits(out_toks)
    with nc.Block() as block:
        P.finalize(block)
    return nc


def build_program_l1(packed, const):
    nc = bass.Bass("TRN2", target_bir_lowering=False)
    x1_in = nc.dram_tensor("x1", [128, 8, T], F32, kind="ExternalInput").ap()
    out = nc.dram_tensor("out", [128, 8, T], F32, kind="ExternalOutput").ap()
    aps = declare_inputs(nc, packed, const, L1_INPUTS)
    lay = Bump()
    t = NS()
    alloc_shared(nc, lay, t, TC)
    alloc_l1(nc, lay, t)
    t.pb = [nc.alloc_psum_tensor(f"pb{i}", [128, QG], F32) for i in range(8)]
    P = Prog(nc)
    sems = mk_sems(P)
    out_toks = build_l1(P, t, aps, x1_in, out, sems)
    P.on("sync").waits(out_toks)
    with nc.Block() as block:
        P.finalize(block)
    return nc


def build_program_fused(packed, const):
    nc = bass.Bass("TRN2", target_bir_lowering=False)
    x_in = nc.dram_tensor("x", [128, 8, T], F32, kind="ExternalInput").ap()
    out = nc.dram_tensor("out", [128, 8, T], F32, kind="ExternalOutput").ap()
    need = list(dict.fromkeys(L0_INPUTS + L1_INPUTS))
    aps = declare_inputs(nc, packed, const, need)
    dram = {
        "xspill": nc.dram_tensor("xspill", [128, 8, T], F32, kind="Internal").ap(),
        "comb": nc.dram_tensor("comb", [128, 8, T], F32, kind="Internal").ap(),
        "aout": nc.dram_tensor("aout", [128, 8, T], BF16, kind="Internal").ap(),
        "x1": nc.dram_tensor("x1", [128, 8, T], F32, kind="Internal").ap(),
        "wl1": nc.dram_tensor("wl1", [96, 128, 8, 128], BF16, kind="Internal").ap(),
    }
    lay = Bump()
    t = NS()
    alloc_shared(nc, lay, t)
    base0 = lay.mark()
    alloc_l0(nc, lay, t)
    l0_peak = lay.pos
    rL1 = Bump()
    rL1.pos = base0
    alloc_l1(nc, rL1, t)
    lay.pos = max(l0_peak, rL1.pos)
    t.pb = [nc.alloc_psum_tensor(f"pb{i}", [128, QG], F32) for i in range(8)]
    t.pav = {qg: t.pb[4 + qg] for qg in range(NQG)}
    P = Prog(nc)
    sems = mk_sems(P)
    x1_toks = build_l0(P, t, aps, x_in, dram["x1"], sems, dram)
    out_toks = build_l1(P, t, aps, dram["x1"], out, sems, entry_bar=x1_toks,
                        wl1=dram["wl1"])
    P.on("sync").waits(out_toks)
    with nc.Block() as block:
        P.finalize(block)
    return nc


def kernel(**inputs):
    ins = {k: np.asarray(v) for k, v in inputs.items()}
    packed, const = pack_host(ins)
    x = ins["L0_attn_x"]
    xs = [np.ascontiguousarray(x[b].T.reshape(8, 128, T).transpose(1, 0, 2))
          for b in range(NCORES)]

    nc = build_program_fused(packed, const)
    need = list(dict.fromkeys(L0_INPUTS + L1_INPUTS))
    in_maps = []
    for b in range(NCORES):
        m = {"x": xs[b]}
        for nm in need:
            m[nm] = packed[nm] if nm in packed else const[nm]
        in_maps.append(m)
    res = run_bass_kernel_spmd(nc, in_maps, core_ids=list(range(NCORES)))

    out = np.empty((NCORES, T, 1024), np.float32)
    for b in range(NCORES):
        o = res.results[b]["out"]
        out[b] = o.transpose(1, 0, 2).reshape(1024, T).T
    return out


# version 36
# speedup vs baseline: 1.1967x; 1.1967x over previous
import numpy as np
import ml_dtypes
import concourse.bass as bass
import concourse.mybir as mybir
from concourse.bass_utils import run_bass_kernel_spmd

F32 = mybir.dt.float32
BF16 = mybir.dt.bfloat16
AF = mybir.ActivationFunctionType
ALU = mybir.AluOpType
AX = mybir.AxisListType.X
BF = ml_dtypes.bfloat16

T = 2048
NCORES = 8
QG = 512
NQG = T // QG
TC = 512
NTC = T // TC
SHIFT = 10.0
SCALE = float(1.0 / np.sqrt(96.0))

ENGINES = ("sync", "tensor", "vector", "scalar", "gpsimd")


class Sem:
    def __init__(self, handle, name, chained=False):
        self.h = handle
        self.name = name
        self.n = 0
        self.chained = chained

    def tick(self, k=1):
        self.n += k
        return (self, self.n)


class IncH:
    __slots__ = ("incs", "prog", "eng", "idx")
    def __init__(self, incs, prog, eng, idx):
        self.incs = incs
        self.prog = prog
        self.eng = eng
        self.idx = idx

    def then_inc(self, sem, k):
        if sem.chained and sem.n > 0:
            w = self.prog.water[self.eng]
            if w.get(sem.name, 0) < sem.n:
                w[sem.name] = sem.n
                stream = self.prog.streams[self.eng]
                stream.insert(len(stream) - 1 - self.prog._ninserted(self.eng, self.idx),
                              ("wait_ge", (sem.h, sem.n), {}, []))
        self.incs.append((sem, k))
        return sem.tick(k)


class Rec:
    def __init__(self, prog, eng):
        self._p = prog
        self._e = eng

    def wait(self, tok):
        if tok is None:
            return
        sem, v = tok
        w = self._p.water[self._e]
        if w.get(sem.name, 0) >= v:
            return
        w[sem.name] = v
        self._p.streams[self._e].append(("wait_ge", (sem.h, v), {}, []))

    def waits(self, toks):
        for tk in toks:
            self.wait(tk)

    def __getattr__(self, name):
        def call(*a, **k):
            incs = []
            stream = self._p.streams[self._e]
            stream.append((name, a, k, incs))
            return IncH(incs, self._p, self._e, len(stream) - 1)
        return call


class Prog:
    def __init__(self, nc):
        self.nc = nc
        self.streams = {e: [] for e in ENGINES}
        self.water = {e: {} for e in ENGINES}
        self.recs = {e: Rec(self, e) for e in ENGINES}
        self._semid = 0
        self.toks = {}

    def sem(self, name=None, chained=False):
        self._semid += 1
        nm = name or f"s{self._semid}"
        return Sem(self.nc.alloc_semaphore(name=nm), nm, chained)

    def _ninserted(self, eng, idx):
        return 0

    def dma(self, rec, stream, out, in_, **kw):
        if not hasattr(self, "dma_sems"):
            self.dma_sems = {}
        s = self.dma_sems.get(stream)
        if s is None:
            s = self.sem("d_" + stream)
            self.dma_sems[stream] = s
        return rec.dma_start(out=out, in_=in_, **kw).then_inc(s, 16)

    def on(self, eng):
        return self.recs[eng]

    def bank_write(self, TE, bank):
        for tk in self.toks.get(("bankR", bank), []):
            TE.wait(tk)
        self.toks[("bankR", bank)] = []

    def bank_written(self, bank, tok):
        self.toks[("bankW", bank)] = tok

    def bank_read(self, rec, bank):
        rec.wait(self.toks.get(("bankW", bank)))

    def bank_read_done(self, bank, tok):
        self.toks.setdefault(("bankR", bank), []).append(tok)

    def finalize(self, block):
        def mk(stream):
            def body(eng):
                for (name, a, k, incs) in stream:
                    inst = getattr(eng, name)(*a, **k)
                    for (sem, kk) in incs:
                        inst.then_inc(sem.h, kk)
            return body
        for e in ENGINES:
            if self.streams[e]:
                getattr(block, e)(mk(self.streams[e]))

    def n_instructions(self):
        return sum(len(s) for s in self.streams.values())


class Bump:
    def __init__(self, limit=229376, base=16544):
        self.pos = base
        self.peak = base
        self.limit = limit

    def alloc(self, nbytes):
        off = (self.pos + 31) & ~31
        self.pos = off + nbytes
        self.peak = max(self.peak, self.pos)
        assert self.pos <= self.limit, f"SBUF overflow: {self.pos} > {self.limit}"
        return off

    def mark(self):
        return self.pos

    def reset(self, mark):
        self.pos = mark


def sb(nc, lay, name, shape, dtype):
    n = 1
    for s in shape[1:]:
        n *= s
    nbytes = n * (2 if dtype == BF16 else 4)
    off = lay.alloc(nbytes)
    return nc.alloc_sbuf_tensor_at(name, list(shape), dtype, offset=off)


def _fm(a):
    R = a.shape[0] // 128
    return np.ascontiguousarray(a.reshape(R, 128, a.shape[1]).transpose(1, 0, 2))


def _vec(a):
    R = a.shape[0] // 128
    return np.ascontiguousarray(a.reshape(R, 128).T)


def pack_host(ins):
    d = {}
    d["wkv"] = _fm(ins["L0_attn_W_kv"])
    d["wq"] = _fm(ins["L0_attn_W_q"])
    d["wo"] = np.ascontiguousarray(
        ins["L0_attn_W_o"].reshape(8, 128, 8, 128).transpose(1, 0, 2, 3))
    kd = ins["L0_attn_W_kdec"].reshape(2, 128, 8, 128).transpose(2, 1, 0, 3)
    vd = ins["L0_attn_W_vdec"].reshape(2, 128, 8, 128).transpose(2, 1, 0, 3)
    qd = ins["L0_attn_W_qdec"].reshape(4, 128, 8, 128).transpose(2, 1, 0, 3)
    d["kdec"] = np.ascontiguousarray(kd)
    d["vdec"] = np.ascontiguousarray(vd)
    d["qdec"] = np.ascontiguousarray(qd)
    perm = (np.arange(32) + 16) % 32
    def rope_pack(w, nkc):
        wp = w.reshape(nkc, 128, 4, 128).transpose(2, 1, 0, 3)
        return np.ascontiguousarray(wp)
    wkr = ins["L0_attn_W_krope"]
    wqr = ins["L0_attn_W_qrope"]
    permcols = np.concatenate([np.arange(512).reshape(16, 32)[h][perm] for h in range(16)])
    d["krope"] = rope_pack(wkr, 8)
    d["krope_p"] = rope_pack(wkr[:, permcols], 8)
    d["qrope"] = rope_pack(wqr, 4)
    d["qrope_p"] = rope_pack(wqr[:, permcols], 4)
    d["gate_wt"] = _fm(np.ascontiguousarray(ins["L0_ffn_gate_w"].T))
    gp = np.concatenate([ins["L0_ffn_gate_proj_w"], ins["L0_ffn_shared_gate_w"]], 0)
    up = np.concatenate([ins["L0_ffn_up_proj_w"], ins["L0_ffn_shared_up_w"]], 0)
    dn = np.concatenate([ins["L0_ffn_down_proj_w"], ins["L0_ffn_shared_down_w"]], 0)
    d["wg"] = np.ascontiguousarray(gp.reshape(9, 8, 128, 16, 128).transpose(0, 3, 2, 1, 4))
    d["wu"] = np.ascontiguousarray(up.reshape(9, 8, 128, 16, 128).transpose(0, 3, 2, 1, 4))
    d["wd"] = np.ascontiguousarray(dn.reshape(9, 16, 128, 8, 128).transpose(0, 2, 1, 3, 4))
    for nm, key in [("wk", "L1_attn_Wk"), ("wv", "L1_attn_Wv"),
                    ("wr", "L1_attn_Wr"), ("wo1", "L1_attn_Wo")]:
        wt = np.ascontiguousarray(ins[key].T)
        d[nm] = np.ascontiguousarray(wt.reshape(8, 128, 8, 128).transpose(2, 1, 0, 3))
    d["w1"] = np.ascontiguousarray(
        ins["L1_ffn_W1"].reshape(8, 128, 32, 128).transpose(2, 1, 0, 3))
    d["w2"] = np.ascontiguousarray(ins["L1_ffn_W2"].reshape(32, 128, 8, 128))
    d["l1vecs"] = np.stack([_vec(ins[k]) for k in
                            ["L1_attn_time_mix_k", "L1_attn_time_mix_v", "L1_attn_time_mix_r",
                             "L1_attn_time_decay", "L1_attn_time_first",
                             "L1_attn_ln_w", "L1_attn_ln_b", "L1_ffn_b2"]], 1)
    d["b1"] = _vec(ins["L1_ffn_b1"])
    d["gkv"] = _vec(ins["L0_attn_g_kv"])
    d["beta"] = np.ascontiguousarray(ins["L0_ffn_beta"].reshape(8, 1))
    c = {}
    c["eye"] = np.eye(128, dtype=np.float32)
    ki = np.arange(128)[:, None]; qi = np.arange(128)[None, :]
    c["maskdia"] = np.where(ki > qi, np.float32(-1e30), np.float32(0.0))
    c["mask01"] = np.where(ki <= qi, 1.0, 0.0).astype(BF)
    c["ones_bf"] = np.ones((128, 1), dtype=BF)
    c["ones_row"] = np.ones((1, 128), dtype=np.float32)
    inv_freq = 1.0 / 10000.0 ** (np.arange(0, 32, 2, dtype=np.float32) / 32)
    freqs = np.outer(np.arange(T, dtype=np.float32), inv_freq)
    emb = np.concatenate([freqs, freqs], -1)
    cos1 = np.cos(emb).astype(np.float32).T
    sin1 = np.sin(emb).astype(np.float32).T
    sins = sin1.copy(); sins[:16] = -sin1[:16]
    c["cos2"] = np.ascontiguousarray(np.concatenate([cos1, cos1], 0)).astype(BF)
    c["sin2"] = np.ascontiguousarray(np.concatenate([sins, sins], 0)).astype(BF)
    sel = np.zeros((8, 8, 128), dtype=np.float32)
    for e in range(8):
        sel[e, e, :] = 1.0
    c["sel"] = np.ascontiguousarray(sel.transpose(1, 0, 2))
    return d, c


def declare_inputs(nc, packed, const, need):
    aps = {}
    for nm in need:
        src = packed if nm in packed else const
        a = src[nm]
        dt = BF16 if a.dtype == BF else F32
        aps[nm] = nc.dram_tensor(nm, list(a.shape), dt, kind="ExternalInput").ap()
    return aps


class NS:
    pass


def ln_stats(P, t, sems, x_view_fn, width, b_sum, b_sq, cast_s=False):
    V, S, TE, G = P.on("vector"), P.on("scalar"), P.on("tensor"), P.on("gpsimd")
    toks = P.toks
    for kc in range(8):
        xt = x_view_fn(kc)
        if cast_s:
            S.wait(toks.get(("lnb", kc % 2)))
            tk_b = S.copy(t.xb_bf[kc % 2][:, 0:width], xt).then_inc(sems["s"], 1)
        else:
            G.wait(toks.get(("lnb", kc % 2)))
            tk_b = G.tensor_copy(t.xb_bf[kc % 2][:, 0:width], xt).then_inc(sems["g"], 1)
        S.wait(toks.get(("lns", kc % 2)))
        tk_s = S.activation(t.xsq_bf[kc % 2][:, 0:width], xt, AF.Square).then_inc(sems["s"], 1)
        if kc == 0:
            P.bank_write(TE, b_sum)
            P.bank_write(TE, b_sq)
        TE.wait(tk_b)
        tmm = TE.matmul(t.pb[b_sum][0:1, 0:width], t.ones_bf[:], t.xb_bf[kc % 2][:, 0:width],
                        start=(kc == 0), stop=(kc == 7)).then_inc(sems["t"], 1)
        toks[("lnb", kc % 2)] = tmm
        TE.wait(tk_s)
        tmm2 = TE.matmul(t.pb[b_sq][0:1, 0:width], t.ones_bf[:], t.xsq_bf[kc % 2][:, 0:width],
                         start=(kc == 0), stop=(kc == 7)).then_inc(sems["t"], 1)
        toks[("lns", kc % 2)] = tmm2
    P.bank_written(b_sum, tmm2)
    P.bank_written(b_sq, tmm2)
    return tmm2


def ln_rows(P, t, sems, width, b_sum, b_sq, m_dst, rstd_dst, inv_n, eps_ap,
            rstd2_dst=None):
    V, S = P.on("vector"), P.on("scalar")
    P.bank_read(V, b_sum)
    P.bank_read(V, b_sq)
    V.tensor_scalar_mul(m_dst, t.pb[b_sum][0:1, 0:width], inv_n)
    tk = V.tensor_scalar_mul(t.row_a[0:1, 0:width], t.pb[b_sq][0:1, 0:width], inv_n)\
        .then_inc(sems["v"], 1)
    P.bank_read_done(b_sum, tk)
    P.bank_read_done(b_sq, tk)
    V.wait(tk)
    tk = V.scalar_tensor_tensor(t.row_b[0:1, 0:width], m_dst, -1.0, m_dst,
                                ALU.mult, ALU.mult).then_inc(sems["v"], 1)
    V.wait(tk)
    tk = V.tensor_add(t.row_a[0:1, 0:width], t.row_a[0:1, 0:width],
                      t.row_b[0:1, 0:width]).then_inc(sems["v"], 1)
    S.wait(tk)
    tk = S.activation(t.row_b[0:1, 0:width], t.row_a[0:1, 0:width], AF.Ln,
                      bias=eps_ap, scale=1.0).then_inc(sems["s"], 1)
    S.wait(tk)
    tk = S.activation(rstd_dst, t.row_b[0:1, 0:width], AF.Exp,
                      scale=-0.5).then_inc(sems["s"], 1)
    if rstd2_dst is None:
        return tk
    V.wait(tk)
    tk = V.tensor_mul(t.row_b[0:1, 0:width], t.row_a[0:1, 0:width], rstd_dst)\
        .then_inc(sems["v"], 1)
    V.wait(tk)
    tk = V.tensor_mul(t.row_b[0:1, 0:width], t.row_b[0:1, 0:width], rstd_dst)\
        .then_inc(sems["v"], 1)
    S.wait(tk)
    tk = S.activation(t.row_b[0:1, 0:width], t.row_b[0:1, 0:width], AF.Ln,
                      bias=eps_ap, scale=1.0).then_inc(sems["s"], 1)
    S.wait(tk)
    tk = S.activation(rstd2_dst, t.row_b[0:1, 0:width], AF.Exp,
                      scale=-0.5).then_inc(sems["s"], 1)
    return tk


def bcast2(P, t, sems, width, row_m, row_r, b_m, b_r, ready_tok):
    V, TE = P.on("vector"), P.on("tensor")
    TE.wait(ready_tok)
    P.bank_write(TE, b_m)
    tk_m = TE.matmul(t.pb[b_m][:, 0:width], t.ones_row[0:1, :], row_m,
                     start=True, stop=True).then_inc(sems["t"], 1)
    P.bank_written(b_m, tk_m)
    P.bank_write(TE, b_r)
    tk_r = TE.matmul(t.pb[b_r][:, 0:width], t.ones_row[0:1, :], row_r,
                     start=True, stop=True).then_inc(sems["t"], 1)
    P.bank_written(b_r, tk_r)
    P.bank_read(V, b_m)
    V.tensor_copy(t.bc_a[:, 0:width], t.pb[b_m][:, 0:width])
    P.bank_read(V, b_r)
    tk = V.tensor_copy(t.bc_b[:, 0:width], t.pb[b_r][:, 0:width]).then_inc(sems["v"], 1)
    P.bank_read_done(b_m, tk)
    P.bank_read_done(b_r, tk)
    return tk


def build_l0(P, t, aps, x_in_ap, x1_out_ap, sems, dram):
    V, S, TE, G, SY = (P.on("vector"), P.on("scalar"), P.on("tensor"),
                       P.on("gpsimd"), P.on("sync"))
    toks = P.toks

    tk_xq = [P.dma(SY, f"x{qg}", t.x[:, :, qg * QG:(qg + 1) * QG],
                   x_in_ap[:, :, qg * QG:(qg + 1) * QG]) for qg in range(NQG)]
    tk_x = tk_xq[-1]
    tk_c = None
    for nm, dst in [("eye", t.eye), ("ones_row", t.ones_row),
                    ("sel", t.sel),
                    ("gate_wt", t.gate_wt), ("beta", t.beta), ("gkv", t.gkv),
                    ("ones_bf", t.ones_bf)]:
        tk_c = P.dma(SY, "const", dst[:], aps[nm])
    V.memset(t.eps_row[:], 1e-5)
    V.memset(t.eps_row6[:], 1e-6)
    tk_ms = V.memset(t.neg10[:], -SHIFT).then_inc(sems["v"], 1)

    TE.wait(tk_c); TE.wait(tk_ms)
    xn_tok = None
    for qg in range(NQG):
        sl = slice(qg * QG, (qg + 1) * QG)
        for e in (G, S, V, TE):
            e.wait(tk_xq[qg])
        st_tok = ln_stats(P, t, sems, lambda kc: t.x[:, kc, sl], QG, 0, 1)
        row_tok = ln_rows(P, t, sems, QG, 0, 1,
                          t.m_row[0:1, 0:QG], t.rstd_row[0:1, 0:QG], 1.0 / 1024.0,
                          t.eps_row[0:1, :])
        bc_tok = bcast2(P, t, sems, QG, t.m_row[0:1, 0:QG], t.rstd_row[0:1, 0:QG],
                        2, 3, row_tok)
        V.wait(bc_tok)
        for kc in range(8):
            scr = t.xn_f32[kc % 2]
            V.wait(toks.get(("xnscr", kc % 2)))
            V.wait(toks.get(("xnscrg", kc % 2)))
            tk0 = V.tensor_sub(scr[:], t.x[:, kc, sl], t.bc_a[:, 0:QG])\
                .then_inc(sems["v"], 1)
            V.wait(tk0)
            tk = V.tensor_mul(scr[:], scr[:], t.bc_b[:, 0:QG]).then_inc(sems["v"], 1)
            G.wait(tk)
            tkg = G.tensor_copy(t.xn[:, kc, sl], scr[:]).then_inc(sems["g"], 1)
            toks[("xnscrg", kc % 2)] = tkg
            xn_tok = tkg
            toks[("xnq", qg)] = tkg
            if kc == 0:
                P.bank_write(TE, 4)
            TE.wait(tk)
            tmm = TE.matmul(t.pb[4][0:8, 0:QG], t.gate_wt[:, kc, :], scr[:],
                            start=(kc == 0), stop=(kc == 7)).then_inc(sems["t"], 1)
            toks[("xnscr", kc % 2)] = tmm
        P.bank_written(4, tmm)
        P.bank_read(V, 4)
        tk = V.tensor_copy(t.logits[:, sl], t.pb[4][0:8, 0:QG]).then_inc(sems["v"], 1)
        P.bank_read_done(4, tk)
        S.wait(tk)
        tk_sig = S.activation(t.sig[:, sl], t.logits[:, sl], AF.Sigmoid)\
            .then_inc(sems["s"], 1)
        V.wait(tk)
        tk_lb = V.tensor_scalar_add(t.lb[:, sl], t.logits[:, sl], t.beta[:])\
            .then_inc(sems["v"], 1)
        TE.wait(tk_lb)
        P.bank_write(TE, 5)
        for i in range(4):
            tkt = TE.transpose(t.pb[5][:, i * 8:(i + 1) * 8],
                               t.lb[:, qg * QG + i * 128:qg * QG + (i + 1) * 128],
                               t.eye[0:8, 0:8]).then_inc(sems["t"], 1)
        P.bank_written(5, tkt)
        P.bank_read(V, 5)
        tk = V.tensor_copy(t.lb_tok[:, 0:32], t.pb[5][:, 0:32]).then_inc(sems["v"], 1)
        P.bank_read_done(5, tk)
        V.wait(tk)
        for i in range(4):
            V.tensor_reduce(t.m1[:, i:i + 1], t.lb_tok[:, i * 8:(i + 1) * 8], AX, ALU.max)
        tk = V.memset(t.junk[:], 0.0).then_inc(sems["v"], 1)
        V.wait(tk)
        for i in range(4):
            V.tensor_scalar(t.ge1[:, i * 8:(i + 1) * 8], t.lb_tok[:, i * 8:(i + 1) * 8],
                            t.m1[:, i:i + 1], None, ALU.is_ge)
        tk = V.memset(t.junk[:], 0.0).then_inc(sems["v"], 1)
        V.wait(tk)
        for i in range(4):
            V.scalar_tensor_tensor(t.msk[:, i * 8:(i + 1) * 8], t.ge1[:, i * 8:(i + 1) * 8],
                                   -1e30, t.lb_tok[:, i * 8:(i + 1) * 8], ALU.mult, ALU.add)
        tk = V.memset(t.junk[:], 0.0).then_inc(sems["v"], 1)
        V.wait(tk)
        for i in range(4):
            V.tensor_reduce(t.m1[:, i:i + 1], t.msk[:, i * 8:(i + 1) * 8], AX, ALU.max)
        tk = V.memset(t.junk[:], 0.0).then_inc(sems["v"], 1)
        V.wait(tk)
        for i in range(4):
            V.tensor_scalar(t.msk[:, i * 8:(i + 1) * 8], t.msk[:, i * 8:(i + 1) * 8],
                            t.m1[:, i:i + 1], None, ALU.is_ge)
        tk = V.memset(t.junk[:], 0.0).then_inc(sems["v"], 1)
        V.wait(tk)
        tk_m2 = V.tensor_add(t.ge1[:, 0:32], t.ge1[:, 0:32], t.msk[:, 0:32])\
            .then_inc(sems["v"], 1)
        TE.wait(tk_m2)
        for i in range(4):
            P.bank_write(TE, 5)
            tkt = TE.transpose(t.pb[5][0:8, 0:128], t.ge1[:, i * 8:(i + 1) * 8], t.eye[:])\
                .then_inc(sems["t"], 1)
            P.bank_written(5, tkt)
            P.bank_read(V, 5)
            tk = V.tensor_copy(t.mask_fm[:, qg * QG + i * 128:qg * QG + (i + 1) * 128],
                               t.pb[5][0:8, 0:128]).then_inc(sems["v"], 1)
            P.bank_read_done(5, tk)
        V.wait(tk_sig); V.wait(tk)
        tk_cq = V.tensor_mul(t.c_sb[:, sl], t.mask_fm[:, sl], t.sig[:, sl])\
            .then_inc(sems["v"], 1)
        toks[("cqg", qg)] = tk_cq
    logits_tok = tk

    tk_csb = toks[("cqg", NQG - 1)]
    x_dead = [logits_tok, xn_tok, tk_csb]

    moe_acc_toks = []
    last_acc = None
    V.waits(x_dead)
    for e in range(9):
        tk_wd = None
        if e < 8:
            for qg in range(NQG):
                TE.wait(toks[("cqg", qg)])
                P.bank_write(TE, 6)
                tkb = TE.matmul(t.pb[6][:, 0:QG], t.sel[:, e, :],
                                t.c_sb[:, qg * QG:(qg + 1) * QG], start=True, stop=True)\
                    .then_inc(sems["t"], 1)
                P.bank_written(6, tkb)
                P.bank_read(V, 6)
                V.wait(toks.get(("cbcfree", qg)))
                tk = V.tensor_copy(t.cbc[qg][:], t.pb[6][:, 0:QG]).then_inc(sems["v"], 1)
                P.bank_read_done(6, tk)
                toks[("cbc", qg)] = tk
        for mc in range(16):
            bi = mc % 2
            G.wait(toks.get(("wgfree", bi)))
            tk_wg = P.dma(G, f"wg{bi}", t.wg_sb[bi][:], aps["wg"][e, mc])
            tk_wu = P.dma(G, f"wu{bi}", t.wu_sb[bi][:], aps["wu"][e, mc])
            if mc == 1:
                G.wait(toks.get(("wdfree",)))
                G.waits(x_dead)
                tk_wd = P.dma(G, "wd", t.wd_sb[:], aps["wd"][e])
            for qg in range(NQG):
                bg = qg % 2
                TE.wait(tk_wg); TE.wait(tk_wu)
                TE.wait(toks[("xnq", qg)])
                if e < 8:
                    TE.wait(toks[("cbc", qg)])
                xn_r = t.xn[:, :, qg * QG:(qg + 1) * QG]
                P.bank_write(TE, bg)
                for kc in range(8):
                    mh = TE.matmul(t.pb[bg][:, 0:QG], t.wg_sb[bi][:, kc, :], xn_r[:, kc, :],
                              start=(kc == 0), stop=(kc == 7))
                tk_g = mh.then_inc(sems["t"], 1)
                P.bank_written(bg, tk_g)
                P.bank_write(TE, 2 + bg)
                for kc in range(8):
                    mh = TE.matmul(t.pb[2 + bg][:, 0:QG], t.wu_sb[bi][:, kc, :], xn_r[:, kc, :],
                              start=(kc == 0), stop=(kc == 7))
                tk_u = mh.then_inc(sems["t"], 1)
                P.bank_written(2 + bg, tk_u)
                if qg == NQG - 1:
                    toks[("wgfree", bi)] = tk_u
                P.bank_read(S, bg)
                S.wait(toks.get(("sgfree", bg)))
                tk_sg = S.activation(t.sg[bg][:], t.pb[bg][:, 0:QG], AF.Silu)\
                    .then_inc(sems["s"], 1)
                P.bank_read_done(bg, tk_sg)
                V.wait(tk_sg)
                P.bank_read(V, 2 + bg)
                V.wait(toks.get(("hfree",)))
                if e < 8:
                    V.wait(toks[("cbc", qg)])
                    tk1 = V.tensor_mul(t.sg[bg][:], t.sg[bg][:], t.pb[2 + bg][:, 0:QG])\
                        .then_inc(sems["v"], 1)
                    V.wait(tk1)
                    tk_h = V.tensor_mul(t.hprime[:, mc, qg * QG:(qg + 1) * QG],
                                        t.sg[bg][:], t.cbc[qg][:]).then_inc(sems["v"], 1)
                else:
                    tk_h = V.tensor_mul(t.hprime[:, mc, qg * QG:(qg + 1) * QG],
                                        t.sg[bg][:], t.pb[2 + bg][:, 0:QG])\
                        .then_inc(sems["v"], 1)
                P.bank_read_done(2 + bg, tk_h)
                toks[("sgfree", bg)] = tk_h
                toks[("h", mc, qg)] = tk_h
        TE.wait(tk_wd)
        for hc in range(8):
            for qg in range(NQG):
                bd = 4 + (hc * NQG + qg) % 2
                P.bank_write(TE, bd)
                for mc in range(16):
                    TE.wait(toks[("h", mc, qg)])
                    mh = TE.matmul(t.pb[bd][:, 0:QG], t.wd_sb[:, mc, hc, :],
                              t.hprime[:, mc, qg * QG:(qg + 1) * QG],
                              start=(mc == 0), stop=(mc == 15))
                tk_d = mh.then_inc(sems["t"], 1)
                P.bank_written(bd, tk_d)
                if hc == 7 and qg == NQG - 1:
                    toks[("wdfree",)] = tk_d
                    toks[("hfree",)] = tk_d
                P.bank_read(V, bd)
                V.wait(toks.get(("eofree", (hc * NQG + qg) % 2)))
                tk_ev = V.tensor_copy(t.eo[(hc * NQG + qg) % 2][:], t.pb[bd][:, 0:QG])\
                    .then_inc(sems["v"], 1)
                P.bank_read_done(bd, tk_ev)
                G.wait(tk_ev)
                last_acc = P.dma(G, f"acc{(hc * NQG + qg) % 2}",
                                 dram["comb"][:, hc, qg * QG:(qg + 1) * QG],
                                 t.eo[(hc * NQG + qg) % 2][:],
                                 accum_op=(ALU.bypass if e == 0 else ALU.add))
                toks[("eofree", (hc * NQG + qg) % 2)] = last_acc
                if e == 8:
                    moe_acc_toks.append(last_acc)
        for qg in range(NQG):
            toks[("cbcfree", qg)] = toks[("h", 15, qg)]

    moe_bar = [(P.dma_sems["acc0"], P.dma_sems["acc0"].n),
               (P.dma_sems["acc1"], P.dma_sems["acc1"].n)]
    for E_ in (V, S, TE, G, SY):
        E_.waits(moe_bar)
    tk_wkv = P.dma(G, "wkv", t.wkv[:], aps["wkv"])
    tk_wq = P.dma(G, "wq", t.wq[:], aps["wq"])
    tk_m01 = P.dma(SY, "m01", t.mask01[:], aps["mask01"])
    tk_cos = P.dma(SY, "cos", t.cos2[:], aps["cos2"])
    tk_trig = P.dma(SY, "sin", t.sin2[:], aps["sin2"])
    V.wait(tk_cos); V.wait(tk_trig); V.wait(tk_m01)

    TE.wait(tk_wkv); TE.wait(tk_wq)
    kvn_tok = None
    for qg in range(NQG):
        sl = slice(qg * QG, (qg + 1) * QG)
        for oc in range(2):
            b = oc
            P.bank_write(TE, b)
            for kc in range(8):
                mh = TE.matmul(t.pb[b][:, 0:QG], t.wkv[:, kc, oc * 128:(oc + 1) * 128],
                          t.xn[:, kc, sl], start=(kc == 0), stop=(kc == 7))
            tk = mh.then_inc(sems["t"], 1)
            P.bank_written(b, tk)
            P.bank_read(V, b)
            V.wait(toks.get(("kvct", oc)))
            tk = V.tensor_copy(t.xn_f32[oc][:], t.pb[b][:, 0:QG]).then_inc(sems["v"], 1)
            P.bank_read_done(b, tk)
            toks[("kvc", oc)] = tk
        for oc in range(4):
            b = 2 + oc % 2
            P.bank_write(TE, b)
            for kc in range(8):
                mh = TE.matmul(t.pb[b][:, 0:QG], t.wq[:, kc, oc * 128:(oc + 1) * 128],
                          t.xn[:, kc, sl], start=(kc == 0), stop=(kc == 7))
            tk = mh.then_inc(sems["t"], 1)
            P.bank_written(b, tk)
            P.bank_read(V, b)
            tk = V.tensor_copy(t.q_c[:, oc, sl], t.pb[b][:, 0:QG]).then_inc(sems["v"], 1)
            P.bank_read_done(b, tk)
        qc_tok = tk
        for oc in range(2):
            S.wait(toks[("kvc", oc)])
            S.wait(toks.get(("lns", oc)))
            tk_s = S.activation(t.xsq_bf[oc][:, 0:QG], t.xn_f32[oc][:], AF.Square)\
                .then_inc(sems["s"], 1)
            if oc == 0:
                P.bank_write(TE, 4)
            TE.wait(tk_s)
            tmm = TE.matmul(t.pb[4][0:1, 0:QG], t.ones_bf[:], t.xsq_bf[oc][:, 0:QG],
                            start=(oc == 0), stop=(oc == 1)).then_inc(sems["t"], 1)
            toks[("lns", oc)] = tmm
        P.bank_written(4, tmm)
        P.bank_read(V, 4)
        tk = V.tensor_scalar_mul(t.row_a[0:1, 0:QG], t.pb[4][0:1, 0:QG], 1.0 / 256.0)\
            .then_inc(sems["v"], 1)
        P.bank_read_done(4, tk)
        S.wait(tk)
        tk = S.activation(t.row_b[0:1, 0:QG], t.row_a[0:1, 0:QG], AF.Ln,
                          bias=t.eps_row6[0:1, :], scale=1.0).then_inc(sems["s"], 1)
        S.wait(tk)
        tk = S.activation(t.row_a[0:1, 0:QG], t.row_b[0:1, 0:QG], AF.Exp,
                          scale=-0.5).then_inc(sems["s"], 1)
        TE.wait(tk)
        P.bank_write(TE, 5)
        tkb = TE.matmul(t.pb[5][:, 0:QG], t.ones_row[0:1, :], t.row_a[0:1, 0:QG],
                        start=True, stop=True).then_inc(sems["t"], 1)
        P.bank_written(5, tkb)
        P.bank_read(V, 5)
        tkc = V.tensor_copy(t.bc_a[:, 0:QG], t.pb[5][:, 0:QG]).then_inc(sems["v"], 1)
        P.bank_read_done(5, tkc)
        for oc in range(2):
            V.wait(tkc)
            V.wait(toks.get(("kvnscr", oc)))
            tk1 = V.tensor_mul(t.xn_f32[oc][:], t.xn_f32[oc][:], t.bc_a[:, 0:QG])\
                .then_inc(sems["v"], 1)
            toks[("kvct", oc)] = tk1
            V.wait(tk1)
            kvn_tok = V.tensor_scalar(t.kvn[:, oc, sl], t.xn_f32[oc][:],
                                      t.gkv[:, oc:oc + 1], None, ALU.mult)\
                .then_inc(sems["v"], 1)
            toks[("kvnscr", oc)] = kvn_tok

    pending_norm = [None]

    def emit_norm(hh):
        for qg in range(NQG):
            S.wait(toks[("pavdone", qg)])
            S.wait(toks.get(("rowa_nrm",)))
            tk_r = S.activation(t.row_a[0:1, 0:QG], t.pav_sb[qg][64:65, 0:QG],
                                AF.Ln).then_inc(sems["s"], 1)
            toks[("rowa_nrm",)] = tk_r
            b = qg % 2
            TE.wait(tk_r)
            P.bank_write(TE, b)
            tk_b = TE.matmul(t.pb[b][0:64, 0:QG], t.ones_row[0:1, 0:64],
                             t.row_a[0:1, 0:QG], start=True, stop=True)\
                .then_inc(sems["t"], 1)
            P.bank_written(b, tk_b)
            P.bank_read(S, b)
            S.wait(toks.get(("wsbfree",)))
            tk_w = S.activation(t.wsb[qg % 2][:], t.pb[b][0:64, 0:QG],
                                AF.Exp, scale=-1.0).then_inc(sems["s"], 1)
            P.bank_read_done(b, tk_w)
            V.wait(tk_w)
            V.wait(toks.get(("aofree", qg % 2)))
            tk_o = V.tensor_mul(t.ao_sb[qg % 2][:], t.pav_sb[qg][0:64, 0:QG],
                                t.wsb[qg % 2][:]).then_inc(sems["v"], 1)
            toks[("wsbfree",)] = tk_o
            toks[("pavsb", qg)] = tk_o
            G.wait(tk_o)
            tk_st = P.dma(G, f"ao{qg % 2}",
                          dram["aout"][(hh % 2) * 64:(hh % 2) * 64 + 64, hh // 2,
                                       qg * QG:(qg + 1) * QG],
                          t.ao_sb[qg % 2][:])
            toks[("aofree", qg % 2)] = tk_st
            toks[("norm_done",)] = tk_st

    for s3 in range(3):
        V.memset(t.kpk[s3][64:128, :, :], 0.0)
        tk_z = V.memset(t.qpk[s3][64:128, :, :], 0.0).then_inc(sems["v"], 1)
    TE.wait(tk_z)

    quad_toks = {}
    pair_toks = {}

    def rope_tasks(hg):
        qb = hg % 2
        tasks = []

        def t_dma():
            G.wait(toks.get(("ropewfree", qb)))
            quad_toks[("w", hg)] = (
                P.dma(G, f"krw{qb}", t.krw[qb][:], aps["krope"][hg]),
                P.dma(G, f"krwp{qb}", t.krwp[qb][:], aps["krope_p"][hg]),
                P.dma(G, f"qrw{qb}", t.qrw[qb][:], aps["qrope"][hg]),
                P.dma(G, f"qrwp{qb}", t.qrwp[qb][:], aps["qrope_p"][hg]))
        tasks.append(t_dma)

        def grp(widx, wsb_f, dst_f, trig_f, nkc, rhs_f, qg, war_id):
            def run():
                wsb = wsb_f(); dst = dst_f(); trig = trig_f(); rhs = rhs_f()
                TE.wait(quad_toks[("w", hg)][widx])
                b = qg % 2
                tsl = slice(qg * QG, (qg + 1) * QG)
                P.bank_write(TE, b)
                for kc in range(nkc):
                    mh = TE.matmul(t.pb[b][:, 0:QG], wsb[:, kc, :], rhs[:, kc, tsl],
                                   start=(kc == 0), stop=(kc == nkc - 1))
                tk = mh.then_inc(sems["t"], 1)
                P.bank_written(b, tk)
                P.bank_read(V, b)
                V.wait(toks.get(("rr", war_id)))
                V.tensor_mul(dst[0:64, tsl], t.pb[b][0:64, 0:QG], trig[:, tsl])
                tk = V.tensor_mul(dst[64:128, tsl], t.pb[b][64:128, 0:QG],
                                  trig[:, tsl]).then_inc(sems["v"], 1)
                P.bank_read_done(b, tk)
                toks[("rordone", war_id)] = tk
            return run

        for qg in range(NQG):
            tasks.append(grp(0, lambda: t.krw[qb], lambda: t.ror_a, lambda: t.cos2,
                             8, lambda: t.xn, qg, "a"))
        for qg in range(NQG):
            tasks.append(grp(1, lambda: t.krwp[qb], lambda: t.ror_b, lambda: t.sin2,
                             8, lambda: t.xn, qg, "b"))

        def comb(pk_f):
            def run():
                V.wait(toks[("rordone", "a")])
                V.wait(toks[("rordone", "b")])
                tk_all = None
                for hh in range(4):
                    h = hg * 4 + hh
                    hp = h // 2
                    s3 = hp % 3
                    pk = pk_f()
                    V.wait(toks.get(("packfree", s3)))
                    tk_all = V.tensor_add(pk[s3][64:96, h % 2, :],
                                          t.ror_a[hh * 32:(hh + 1) * 32, :],
                                          t.ror_b[hh * 32:(hh + 1) * 32, :])\
                        .then_inc(sems["v"], 1)
                toks[("rr", "a")] = tk_all
                toks[("rr", "b")] = tk_all
                quad_toks[("ropec", hg, pk_f is not kf)] = tk_all
            return run
        kf = lambda: t.kpk
        tasks.append(comb(kf))
        for qg in range(NQG):
            tasks.append(grp(2, lambda: t.qrw[qb], lambda: t.ror_a, lambda: t.cos2,
                             4, lambda: t.q_c, qg, "a"))
        for qg in range(NQG):
            tasks.append(grp(3, lambda: t.qrwp[qb], lambda: t.ror_b, lambda: t.sin2,
                             4, lambda: t.q_c, qg, "b"))

        def comb_q_wrap():
            comb(lambda: t.qpk)()
            toks[("ropewfree", qb)] = toks[("rr", "a")]
        tasks.append(comb_q_wrap)
        return tasks

    def pair_tasks(hp):
        pb_ = hp % 2
        s3 = hp % 3
        tasks = []

        def t_dma():
            G.wait(toks.get(("pairwfree", pb_)))
            pair_toks[("w", hp)] = (
                P.dma(G, f"kdw{pb_}", t.kdw[pb_][:], aps["kdec"][hp]),
                P.dma(G, f"vdw{pb_}", t.vdw[pb_][:], aps["vdec"][hp]),
                P.dma(G, f"qdw{pb_}", t.qdw[pb_][:], aps["qdec"][hp]))
        tasks.append(t_dma)

        def grp(widx, wsb_f, nrc, rhs_f, pk_f, qg):
            def run():
                wsb = wsb_f(); rhs = rhs_f(); pk = pk_f()
                TE.wait(pair_toks[("w", hp)][widx])
                TE.wait(kvn_tok); TE.wait(qc_tok)
                b = qg % 2
                P.bank_write(TE, b)
                for rc in range(nrc):
                    mh = TE.matmul(t.pb[b][:, 0:QG], wsb[:, rc, :],
                                   rhs[:, rc, qg * QG:(qg + 1) * QG],
                                   start=(rc == 0), stop=(rc == nrc - 1))
                tk = mh.then_inc(sems["t"], 1)
                P.bank_written(b, tk)
                P.bank_read(V, b)
                V.wait(toks.get(("packfree", s3)))
                V.tensor_copy(pk[s3][0:64, 0, qg * QG:(qg + 1) * QG],
                              t.pb[b][0:64, 0:QG])
                tkk = V.tensor_copy(pk[s3][0:64, 1, qg * QG:(qg + 1) * QG],
                                    t.pb[b][64:128, 0:QG]).then_inc(sems["v"], 1)
                P.bank_read_done(b, tkk)
                pair_toks[("prod", hp)] = tkk
            return run

        for qg in range(NQG):
            tasks.append(grp(0, lambda: t.kdw[pb_], 2, lambda: t.kvn,
                             lambda: t.kpk, qg))
        for qg in range(NQG):
            tasks.append(grp(2, lambda: t.qdw[pb_], 4, lambda: t.q_c,
                             lambda: t.qpk, qg))

        def t_ones():
            V.wait(toks.get(("vTfree", pb_)))
            V.memset(t.vT[pb_][:, :, 64:65], 1.0)
            pair_toks[("ones", hp)] = V.memset(t.vT[pb_][:, :, 129:130], 1.0)\
                .then_inc(sems["v"], 1)
        tasks.append(t_ones)

        def vt_grp(kt):
            def run():
                TE.wait(pair_toks[("w", hp)][1])
                TE.wait(kvn_tok)
                b = kt % 2
                P.bank_write(TE, b)
                for rc in range(2):
                    mh = TE.matmul(t.pb[b][:, 0:128],
                                   t.kvn[:, rc, kt * 128:(kt + 1) * 128],
                                   t.vdw[pb_][:, rc, :], start=(rc == 0), stop=(rc == 1))
                tk = mh.then_inc(sems["t"], 1)
                P.bank_written(b, tk)
                E_ = V if b == 0 else S
                sk = "v" if b == 0 else "s"
                P.bank_read(E_, b)
                E_.wait(pair_toks[("ones", hp)])
                if b == 0:
                    E_.tensor_copy(t.vT[pb_][:, kt, 0:64], t.pb[b][:, 0:64])
                    tkk = E_.tensor_copy(t.vT[pb_][:, kt, 65:129], t.pb[b][:, 64:128])\
                        .then_inc(sems[sk], 1)
                else:
                    E_.copy(t.vT[pb_][:, kt, 0:64], t.pb[b][:, 0:64])
                    tkk = E_.copy(t.vT[pb_][:, kt, 65:129], t.pb[b][:, 64:128])\
                        .then_inc(sems[sk], 1)
                P.bank_read_done(b, tkk)
                pair_toks[("vT", hp) if b == 0 else ("vTs", hp)] = tkk
            return run
        for kt in range(16):
            tasks.append(vt_grp(kt))
        return tasks

    def flash_head(h, tasks):
        hp = h // 2
        hl = h % 2
        hg = h // 4
        pb_ = hp % 2
        s3 = hp % 3
        if pending_norm[0] is not None:
            emit_norm(pending_norm[0])
        pending_norm[0] = h
        TE.wait(pair_toks[("prod", hp)])
        TE.wait(pair_toks.get(("prods", hp)))
        TE.wait(quad_toks[("ropec", hg, False)])
        TE.wait(quad_toks[("ropec", hg, True)])
        TE.wait(pair_toks[("vT", hp)])
        TE.wait(pair_toks.get(("vTs", hp)))

        def emit_scores(kt):
            ep = kt % 3
            sc = {}
            for qg in range(kt // 4, NQG):
                off = kt * 128 - qg * QG
                lo = max(0, off)
                bs = 2 + qg % 2
                P.bank_write(TE, bs)
                mh = TE.matmul(t.pb[bs][:, lo:QG],
                               t.kpk[s3][:, hl, kt * 128:(kt + 1) * 128],
                               t.qpk[s3][:, hl, qg * QG + lo:(qg + 1) * QG],
                               start=True, stop=True)
                tk_sc = mh.then_inc(sems["t"], 1)
                P.bank_written(bs, tk_sc)
                S.wait(toks.get(("expTfree", ep)))
                P.bank_read(S, bs)
                tk_e = S.activation(t.expT[ep][:, qg * QG + lo:(qg + 1) * QG],
                                    t.pb[bs][:, lo:QG], AF.Exp,
                                    bias=t.neg10[:], scale=SCALE).then_inc(sems["s"], 1)
                P.bank_read_done(bs, tk_e)
                if qg == kt // 4:
                    G.wait(tk_e)
                    dsl = slice(qg * QG + lo, qg * QG + lo + 128)
                    tk_e = G.tensor_mul(t.expT[ep][:, dsl], t.expT[ep][:, dsl],
                                        t.mask01[:]).then_inc(sems["g"], 1)
                sc[qg] = (lo, tk_e)
            return sc

        def emit_av(kt, sc):
            ep = kt % 3
            tk_av = None
            for qg in range(kt // 4, NQG):
                lo, tk_e = sc[qg]
                bav = 4 + qg
                if kt == 0:
                    P.bank_write(TE, bav)
                TE.wait(tk_e)
                mh = TE.matmul(t.pav[qg][0:65, lo:QG],
                               t.vT[pb_][:, kt, hl * 65:hl * 65 + 65],
                               t.expT[ep][:, qg * QG + lo:(qg + 1) * QG],
                               start=(kt == 0), stop=(kt == 4 * qg + 3))
                tk_av = mh.then_inc(sems["t"], 1)
                if kt == 4 * qg + 3:
                    P.bank_written(bav, tk_av)
                    P.bank_read(V, bav)
                    V.wait(toks.get(("pavsb", qg)))
                    tk_pv = V.tensor_copy(t.pav_sb[qg][:], t.pav[qg][0:65, 0:QG])\
                        .then_inc(sems["v"], 1)
                    P.bank_read_done(bav, tk_pv)
                    toks[("pavdone", qg)] = tk_pv
            toks[("expTfree", ep)] = tk_av
            return tk_av

        sc_q = {}
        tk_av = None
        for kt in range(16):
            sc_q[kt] = emit_scores(kt)
            for _ in range(2):
                if tasks:
                    tasks.pop(0)()
            if kt >= 2:
                tk_av = emit_av(kt - 2, sc_q.pop(kt - 2))
        tk_av = emit_av(14, sc_q.pop(14))
        tk_av = emit_av(15, sc_q.pop(15))
        toks[("vTfree", pb_)] = tk_av
        return tk_av

    for task in rope_tasks(0):
        task()
    for task in pair_tasks(0):
        task()
    for hp in range(8):
        nxt = []
        if hp < 7:
            if hp % 2 == 1:
                nxt += rope_tasks(hp // 2 + 1)
            nxt += pair_tasks(hp + 1)
        for hl in range(2):
            tk_av = flash_head(hp * 2 + hl, nxt)
        for task in nxt:
            task()
        toks[("pairfree", hp % 2)] = tk_av
        toks[("pairwfree", hp % 2)] = tk_av
        toks[("packfree", hp % 3)] = tk_av
        if hp == 7:
            for E_ in (SY, G):
                E_.wait(pair_toks[("prod", 7)])
                E_.wait(pair_toks.get(("prods", 7)))
                E_.wait(pair_toks[("vT", 7)])
                E_.wait(pair_toks.get(("vTs", 7)))
                E_.wait(quad_toks[("ropec", 3, False)])
                E_.wait(quad_toks[("ropec", 3, True)])
                E_.wait(toks[("vTfree", 0)])
            tk_wo = P.dma(G, "wo", t.wo_sb[:], aps["wo"])
            tk_xt0 = P.dma(SY, "xt0", t.xtile[0][:], x_in_ap[:, :, 0:QG])
            SY.waits(moe_acc_toks)
            tk_cb0 = P.dma(SY, "cb0", t.ctile[0][:], dram["comb"][:, :, 0:QG])
    if pending_norm[0] is not None:
        emit_norm(pending_norm[0])
    tk_o = toks[("norm_done",)]
    aout_done = [(P.dma_sems["ao0"], P.dma_sems["ao0"].n),
                 (P.dma_sems["ao1"], P.dma_sems["ao1"].n)]

    G.waits(aout_done); G.wait(tk_o)
    for E_ in (V, S, TE):
        E_.waits(aout_done)
        E_.wait(tk_o)
    SY.waits(aout_done)
    SY.waits(moe_acc_toks)
    tk_x1 = None
    for tt in range(NQG):
        bi = tt % 2
        x1dst = t.x1cb[tt] if tt < 2 else t.x1t[tt - 2]
        SY.wait(toks.get(("updfree", bi)))
        if tt == 0:
            tk_xt = tk_xt0
            tk_cb = tk_cb0
        else:
            tk_xt = P.dma(SY, f"xt{bi}", t.xtile[bi][:],
                          x_in_ap[:, :, tt * QG:(tt + 1) * QG])
            tk_cb = P.dma(SY, f"cb{bi}", t.ctile[bi][:],
                          dram["comb"][:, :, tt * QG:(tt + 1) * QG])
        tk_at = P.dma(SY, f"at{bi}", t.atile[bi][:],
                      dram["aout"][:, :, tt * QG:(tt + 1) * QG])
        for oc in range(8):
            b = oc % 2
            TE.wait(tk_at); TE.wait(tk_wo)
            P.bank_write(TE, b)
            for ic in range(8):
                mh = TE.matmul(t.pb[b][:, 0:QG], t.wo_sb[:, ic, oc, :],
                          t.atile[bi][:, ic, :], start=(ic == 0), stop=(ic == 7))
            tk_s = mh.then_inc(sems["t"], 1)
            P.bank_written(b, tk_s)
            S.wait(tk_cb)
            S.wait(toks.get(("gtfree", b)))
            tk_g = S.activation(t.gt[b][:, 0:QG], t.ctile[bi][:, oc, :], AF.Silu)\
                .then_inc(sems["s"], 1)
            P.bank_read(V, b)
            V.wait(tk_g); V.wait(tk_xt)
            tk1 = V.tensor_mul(t.gt[b][:, 0:QG], t.gt[b][:, 0:QG], t.pb[b][:, 0:QG])\
                .then_inc(sems["v"], 1)
            P.bank_read_done(b, tk1)
            V.wait(tk1)
            tk_x1 = V.tensor_add(x1dst[:, oc, :], t.gt[b][:, 0:QG],
                                 t.xtile[bi][:, oc, :]).then_inc(sems["v"], 1)
            toks[("gtfree", b)] = tk_x1
        if tt < 2:
            toks[("x1add", tt)] = tk_x1
        G.wait(tk_x1)
        tk_out = P.dma(G, f"x1o{bi}", x1_out_ap[:, :, tt * QG:(tt + 1) * QG],
                       x1dst[:])
        toks[("updfree", bi)] = tk_out
        toks[("x1out", bi)] = tk_out
        toks[("x1wb", tt)] = tk_out
    return [toks[("x1out", 0)], toks[("x1out", 1)]]


def build_l1(P, t, aps, x1_in_ap, out_ap, sems, entry_bar=None, wl1=None):
    V, S, TE, G, SY = (P.on("vector"), P.on("scalar"), P.on("tensor"),
                       P.on("gpsimd"), P.on("sync"))
    toks = P.toks
    NPOOL = len(t.pool)
    x1tok = {}

    def x1load(ci):
        if ci >= NTC:
            return
        b = ci % 2
        if ci < 2:
            if entry_bar is not None:
                x1tok[ci] = toks[("x1add", ci)]
            else:
                SY.wait(toks.get(("x1free", b)))
                x1tok[ci] = P.dma(SY, f"x1c{b}", t.x1cb[b][:],
                                  x1_in_ap[:, :, ci * TC:(ci + 1) * TC])
        else:
            G.wait(toks.get(("x1free", b)))
            x1tok[ci] = P.dma(G, f"x1g{b}", t.x1cb[b][:],
                              x1_in_ap[:, :, ci * TC:(ci + 1) * TC])
    if entry_bar is not None:
        for E_ in (V, S, TE, G, SY):
            E_.waits(entry_bar)
    x1load(0)
    x1load(1)
    pool_i = [0]

    def wtile(src_ap):
        i = pool_i[0]
        b = i % NPOOL
        pool_i[0] += 1
        ser = i % 96
        chunk = i // 96
        G.wait(toks.get(("poolfree", b)))
        G.wait(toks.get(("poolwb", b)))
        SY.wait(toks.get(("poolfree", b)))
        SY.wait(toks.get(("poolwb", b)))
        if wl1 is None:
            tk = P.dma(G, f"wt{b}", t.pool[b][:], src_ap)
        elif chunk == 0:
            tk = P.dma(G, f"wt{b}", t.pool[b][:], src_ap)
            SY.wait(tk)
            tkw = P.dma(SY, f"wls{b}", wl1[ser], t.pool[b][:])
            toks[("wl1w", ser)] = tkw
            toks[("poolwb", b)] = tkw
        else:
            SY.wait(toks[("wl1w", ser)])
            tk = P.dma(SY, f"wth{b}", t.pool[b][:], wl1[ser])
        return b, tk

    consts = [("l1vecs", t.l1vecs), ("b1", t.b1)]
    if entry_bar is None:
        consts += [("ones_bf", t.ones_bf), ("ones_row", t.ones_row)]
    tk_c2 = None
    for nm, dst in consts:
        tk_c2 = P.dma(SY, "l1const", dst[:], aps[nm])
    tk_vecs = tk_c2
    S.wait(tk_vecs)
    S.activation(t.decay[:], t.l1vecs[:, 3, :], AF.Sigmoid)
    tk_df = S.activation(t.first[:], t.l1vecs[:, 4, :], AF.Exp).then_inc(sems["s"], 1)
    V.memset(t.carry_s[:], 0.0)
    tk_cx = V.memset(t.carry_x[:], 0.0).then_inc(sems["v"], 1)
    tk_cs = tk_cx
    if entry_bar is None:
        V.memset(t.eps_row[:], 1e-5)

    MIXK, MIXV, MIXR, LNW, LNB, B2 = 0, 1, 2, 5, 6, 7
    tk_st = None
    heads = {}

    def ln_head(ci):
        b = ci % 2
        xc_ = t.x1cb[b]
        tk_x1 = x1tok[ci]
        for E_ in (G, S, V, TE):
            E_.wait(tk_x1)
        TE.wait(tk_c2)
        st_tok = ln_stats(P, t, sems, lambda kc: xc_[:, kc, :], TC, 2, 3,
                          cast_s=True)
        row_tok = ln_rows(P, t, sems, TC, 2, 3, t.m_row[0:1, 0:TC], t.rstd_row[0:1, 0:TC],
                          1.0 / 1024.0, t.eps_row[0:1, :],
                          rstd2_dst=t.rstd2_row[0:1, 0:TC])
        V.wait(row_tok)
        tk_r3 = V.tensor_mul(t.row_a[0:1, 0:TC], t.rstd_row[0:1, 0:TC],
                             t.rstd2_row[0:1, 0:TC]).then_inc(sems["v"], 1)
        bc_tok = bcast2(P, t, sems, TC, t.m_row[0:1, 0:TC], t.rstd_row[0:1, 0:TC],
                        2, 3, row_tok)
        TE.wait(tk_r3)
        P.bank_write(TE, 2)
        tk_b3 = TE.matmul(t.pb[2][:, 0:TC], t.ones_row[0:1, :], t.row_a[0:1, 0:TC],
                          start=True, stop=True).then_inc(sems["t"], 1)
        P.bank_written(2, tk_b3)
        P.bank_read(V, 2)
        tk_bc3 = V.tensor_copy(t.bc_c[:, 0:TC], t.pb[2][:, 0:TC]).then_inc(sems["v"], 1)
        P.bank_read_done(2, tk_bc3)
        V.wait(bc_tok)
        V.wait(toks.get(("xncfree",)))
        for kc in range(8):
            V.tensor_sub(t.xnc[:, kc, :], xc_[:, kc, :], t.bc_a[:, 0:TC])
        tk = V.memset(t.junk[:], 0.0).then_inc(sems["v"], 1)
        V.wait(tk)
        V.wait(toks.get(("xnbfree",)))
        xnb_toks = []
        for kc in range(8):
            xnb_toks.append(V.tensor_mul(t.xnb_c[:, kc, :], t.xnc[:, kc, :],
                                         t.bc_b[:, 0:TC]).then_inc(sems["v"], 1))
        heads[ci] = (xnb_toks, tk_bc3)

    mheads = {}
    cxtok = [None]

    def mix_head(ci):
        tk_xnb, tk_bc3 = heads[ci]
        dif = t.dif2[ci % 2]
        V.wait(tk_bc3); V.waits(tk_xnb); V.wait(tk_vecs)
        V.wait(toks.get(("xt1free",)))
        for kc in range(8):
            V.tensor_mul(t.xt1[:, kc, :], t.xnc[:, kc, :], t.bc_c[:, 0:TC])
        tk = V.memset(t.junk[:], 0.0).then_inc(sems["v"], 1)
        V.wait(tk)
        for kc in range(8):
            V.tensor_scalar(t.xt1[:, kc, :], t.xt1[:, kc, :],
                            t.l1vecs[:, LNW, kc:kc + 1], t.l1vecs[:, LNB, kc:kc + 1],
                            ALU.mult, ALU.add)
        tk_xt1 = V.memset(t.junk[:], 0.0).then_inc(sems["v"], 1)
        V.wait(tk_xt1); V.wait(cxtok[0])
        for kc in range(8):
            V.tensor_sub(dif[:, kc, 1:TC], t.xt1[:, kc, 1:TC], t.xt1[:, kc, 0:TC - 1])
            V.tensor_sub(dif[:, kc, 0:1], t.xt1[:, kc, 0:1], t.carry_x[:, kc:kc + 1])
        tk_dif = V.memset(t.junk[:], 0.0).then_inc(sems["v"], 1)
        V.wait(tk_dif)
        V.wait(toks.get(("mixfree", "mix0")))
        for kc in range(8):
            V.scalar_tensor_tensor(t.mix0[:, kc, 1:TC], dif[:, kc, 1:TC],
                                   t.l1vecs[:, MIXK, kc:kc + 1],
                                   t.xt1[:, kc, 0:TC - 1], ALU.mult, ALU.add)
            V.scalar_tensor_tensor(t.mix0[:, kc, 0:1], dif[:, kc, 0:1],
                                   t.l1vecs[:, MIXK, kc:kc + 1],
                                   t.carry_x[:, kc:kc + 1], ALU.mult, ALU.add)
        tk_mixk = V.memset(t.junk[:], 0.0).then_inc(sems["v"], 1)
        mheads[ci] = (tk_xt1, tk_dif, tk_mixk)

    ln_head(0)
    mix_head(0)
    for ci in range(NTC):
        bi = ci % 2
        xc = t.x1cb[bi]
        sl = slice(ci * TC, (ci + 1) * TC)
        tk_xnb, tk_bc3 = heads[ci]
        for oc in range(32):
            wb, tk_w = wtile(aps["w1"][oc])
            b = 4 + oc % 2
            TE.wait(tk_w)
            P.bank_write(TE, b)
            for kc in range(8):
                TE.wait(tk_xnb[kc])
                mh = TE.matmul(t.pb[b][:, 0:TC], t.pool[wb][:, kc, :], t.xnb_c[:, kc, :],
                               start=(kc == 0), stop=(kc == 7))
            tk = mh.then_inc(sems["t"], 1)
            P.bank_written(b, tk)
            toks[("poolfree", wb)] = tk
            if oc == 31:
                toks[("xnbfree",)] = tk
            P.bank_read(S, b)
            S.wait(tk_vecs)
            tk2 = S.activation(t.h1[:, oc, :], t.pb[b][:, 0:TC], AF.Relu,
                               bias=t.b1[:, oc:oc + 1], scale=1.0).then_inc(sems["s"], 1)
            P.bank_read_done(b, tk2)
            toks[("h1", oc)] = tk2
        tk_xt1, tk_dif, tk_mixk = mheads[ci]
        dif = t.dif2[bi]
        Ss = t.Ss2[bi]
        Kt = t.K2[bi]
        s_c = t.s_c2[bi]
        mix_specs = [(MIXK, Kt, None, "K", aps["wk"], t.mix0, "mix0"),
                     (MIXV, t.Vv, None, "V", aps["wv"], t.mix1, "mix1"),
                     (MIXR, t.R, AF.Sigmoid, "R", aps["wr"], t.mix0, "mix0")]
        for mi, (mixcol, dst, act, key, wap, xin, xkey) in enumerate(mix_specs):
            if mi == 0:
                tk_mix = tk_mixk
            else:
                V.wait(tk_dif)
                V.wait(toks.get(("mixfree", xkey)))
                if xin is t.mix1:
                    V.waits(toks.get(("outfree",), []))
                for kc in range(8):
                    V.scalar_tensor_tensor(xin[:, kc, 1:TC], dif[:, kc, 1:TC],
                                           t.l1vecs[:, mixcol, kc:kc + 1],
                                           t.xt1[:, kc, 0:TC - 1], ALU.mult, ALU.add)
                    V.scalar_tensor_tensor(xin[:, kc, 0:1], dif[:, kc, 0:1],
                                           t.l1vecs[:, mixcol, kc:kc + 1],
                                           t.carry_x[:, kc:kc + 1], ALU.mult, ALU.add)
                tk_mix = V.memset(t.junk[:], 0.0).then_inc(sems["v"], 1)
            TE.wait(tk_mix)
            for oc in range(8):
                wb, tk_w = wtile(wap[oc])
                b = 4 + oc % 2
                TE.wait(tk_w)
                P.bank_write(TE, b)
                for kc in range(8):
                    mh = TE.matmul(t.pb[b][:, 0:TC], t.pool[wb][:, kc, :], xin[:, kc, :],
                                   start=(kc == 0), stop=(kc == 7))
                tk = mh.then_inc(sems["t"], 1)
                P.bank_written(b, tk)
                toks[("poolfree", wb)] = tk
                if oc == 7:
                    toks[("mixfree", xkey)] = tk
                if act is None:
                    P.bank_read(V, b)
                    if dst is t.Vv:
                        V.waits(toks.get(("outfree",), []))
                    tk2 = V.tensor_copy(dst[:, oc, :], t.pb[b][:, 0:TC])\
                        .then_inc(sems["v"], 1)
                else:
                    P.bank_read(S, b)
                    tk2 = S.activation(dst[:, oc, :], t.pb[b][:, 0:TC], act)\
                        .then_inc(sems["s"], 1)
                P.bank_read_done(b, tk2)
                toks[(key, oc)] = tk2
        for kc in range(8):
            V.tensor_mul(t.xnc[:, kc, :], t.xnc[:, kc, :], t.bc_b[:, 0:TC])
        tk_xnf = V.memset(t.junk[:], 0.0).then_inc(sems["v"], 1)
        V.wait(tk_df); V.wait(tk_cs)
        for kc in range(8):
            V.wait(toks[("K", kc)])
            V.tensor_tensor_scan(Ss[:, kc, :],
                                 t.decay[:, kc:kc + 1].broadcast_to([128, TC]),
                                 Kt[:, kc, :], t.carry_s[:, kc:kc + 1],
                                 ALU.mult, ALU.add)
        tk_scan = V.memset(t.junk[:], 0.0).then_inc(sems["v"], 1)
        V.wait(tk_scan)
        tk_cs = V.tensor_copy(t.carry_s[:], Ss[:, :, TC - 1]).then_inc(sems["v"], 1)
        V.wait(tk_cs)
        for kc in range(8):
            V.scalar_tensor_tensor(dif[:, kc, :], Ss[:, kc, :], t.first[:, kc:kc + 1],
                                   Kt[:, kc, :], ALU.mult, ALU.add)
        tk = V.memset(t.junk[:], 0.0).then_inc(sems["v"], 1)
        V.wait(tk)
        for kc in range(8):
            V.wait(toks[("V", kc)])
            V.tensor_mul(dif[:, kc, :], dif[:, kc, :], t.Vv[:, kc, :])
        tk = V.memset(t.junk[:], 0.0).then_inc(sems["v"], 1)
        V.wait(tk)
        V.wait(toks.get(("mixfree", "mix0")))
        for kc in range(8):
            V.wait(toks[("R", kc)])
            V.tensor_mul(t.rk[:, kc, :], dif[:, kc, :], t.R[:, kc, :])
        tk_rk = V.memset(t.junk[:], 0.0).then_inc(sems["v"], 1)
        V.wait(tk_rk)
        tk_cx = V.tensor_copy(t.carry_x[:], t.xt1[:, :, TC - 1]).then_inc(sems["v"], 1)
        toks[("xt1free",)] = tk_cx
        cxtok[0] = tk_cx
        TE.wait(tk_rk)
        for oc in range(8):
            wb, tk_w = wtile(aps["wo1"][oc])
            b = 4 + oc % 2
            TE.wait(tk_w)
            P.bank_write(TE, b)
            for kc in range(8):
                mh = TE.matmul(t.pb[b][:, 0:TC], t.pool[wb][:, kc, :], t.rk[:, kc, :],
                               start=(kc == 0), stop=(kc == 7))
            tk = mh.then_inc(sems["t"], 1)
            P.bank_written(b, tk)
            toks[("poolfree", wb)] = tk
            if oc == 7:
                toks[("mixfree", "mix0")] = tk
            P.bank_read(V, b)
            V.wait(tk_xnf)
            tk2 = V.tensor_add(s_c[:, oc, :], t.pb[b][:, 0:TC], t.xnc[:, oc, :])\
                .then_inc(sems["v"], 1)
            P.bank_read_done(b, tk2)
            toks[("s", oc)] = tk2
        toks[("xncfree",)] = tk2
        if ci + 1 < NTC:
            ln_head(ci + 1)
            mix_head(ci + 1)
        for ic in range(32):
            wb, tk_w = wtile(aps["w2"][ic])
            TE.wait(tk_w)
            TE.wait(toks[("h1", ic)])
            if ic == 0:
                for hc in range(8):
                    P.bank_write(TE, hc)
            for hc in range(8):
                mh = TE.matmul(t.pb[hc][:, 0:TC], t.pool[wb][:, hc, :], t.h1[:, ic, :],
                               start=(ic == 0), stop=(ic == 31))
            tk = mh.then_inc(sems["t"], 1)
            toks[("poolfree", wb)] = tk
        tk_f = tk
        toks[("h1free",)] = tk_f
        toks[("xnbfree",)] = tk_f
        for hc in range(8):
            P.bank_written(hc, tk_f)
        for hc in range(8):
            b = hc % 2
            P.bank_read(S, hc)
            S.wait(toks.get(("gtfree", b)))
            tk_g = S.activation(t.gt[b][:, 0:TC], t.pb[hc][:, 0:TC], AF.Silu,
                                bias=t.l1vecs[:, B2, hc:hc + 1], scale=1.0)\
                .then_inc(sems["s"], 1)
            P.bank_read_done(hc, tk_g)
            V.wait(tk_g)
            V.wait(toks[("s", hc)])
            V.waits(toks.get(("outfree",), []))
            tk1 = V.tensor_mul(t.gt[b][:, 0:TC], t.gt[b][:, 0:TC], s_c[:, hc, :])\
                .then_inc(sems["v"], 1)
            V.wait(tk1)
            tk_o = V.tensor_add(t.outc[:, hc, :], t.gt[b][:, 0:TC],
                                xc[:, hc, :]).then_inc(sems["v"], 1)
            toks[("gtfree", b)] = tk_o
            if hc == 7:
                toks[("x1free", bi)] = tk_o
            G.wait(tk_o)
            G.wait(toks.get(("ost", bi, hc % 2)))
            tk_st = P.dma(G, f"o{bi}{hc % 2}", out_ap[:, hc, sl],
                          t.outc[:, hc, :])
            toks[("ost", bi, hc % 2)] = tk_st
            if hc >= 6:
                toks[("outdone", bi, hc % 2)] = tk_st
        toks[("outfree",)] = [toks[("outdone", bi, 0)], toks[("outdone", bi, 1)]]
        x1load(ci + 2)
    return [toks[("outdone", 0, 0)], toks[("outdone", 0, 1)],
            toks[("outdone", 1, 0)], toks[("outdone", 1, 1)]]


def alloc_shared(nc, lay, t, w=QG):
    t.ones_bf = sb(nc, lay, "ones_bf", [128, 1], BF16)
    t.ones_row = sb(nc, lay, "ones_row", [1, 128], F32)
    t.neg10 = sb(nc, lay, "neg10", [128, 1], F32)
    t.eps_row = sb(nc, lay, "eps_row", [1, 1], F32)
    t.eps_row6 = sb(nc, lay, "eps_row6", [1, 1], F32)
    t.junk = sb(nc, lay, "junk", [1, 1], F32)
    t.m_row = sb(nc, lay, "m_row", [1, w], F32)
    t.rstd_row = sb(nc, lay, "rstd_row", [1, w], F32)
    t.rstd2_row = sb(nc, lay, "rstd2_row", [1, w], F32)
    t.row_a = sb(nc, lay, "row_a", [1, w], F32)
    t.row_b = sb(nc, lay, "row_b", [1, w], F32)
    t.bc_a = sb(nc, lay, "bc_a", [128, w], F32)
    t.bc_b = sb(nc, lay, "bc_b", [128, w], F32)
    t.xb_bf = [sb(nc, lay, f"xb_bf{i}", [128, w], BF16) for i in range(2)]
    t.xsq_bf = [sb(nc, lay, f"xsq_bf{i}", [128, w], BF16) for i in range(2)]
    t.gt = [sb(nc, lay, f"gt{i}", [128, w], F32) for i in range(2)]
    return t


def _region(lay):
    r = Bump(lay.limit)
    r.pos = lay.mark()
    r.peak = r.pos
    return r


def alloc_l0(nc, lay, t):
    t.eye = sb(nc, lay, "eye", [128, 128], F32)
    t.sel = sb(nc, lay, "sel", [8, 8, 128], F32)
    t.gate_wt = sb(nc, lay, "gate_wt", [128, 8, 8], F32)
    t.beta = sb(nc, lay, "beta", [8, 1], F32)
    t.gkv = sb(nc, lay, "gkv", [128, 2], F32)
    csb_off = lay.alloc(T * 4)
    t.c_sb = nc.alloc_sbuf_tensor_at("c_sb", [8, T], F32, offset=csb_off)
    t.csb_off = csb_off
    t.xn = sb(nc, lay, "xn", [128, 8, T], BF16)
    t.xn_f32 = [sb(nc, lay, f"xn_f32{i}", [128, QG], F32) for i in range(2)]
    t.lb_tok = sb(nc, lay, "lb_tok", [128, 128], F32)
    t.m1 = sb(nc, lay, "m1", [128, 16], F32)
    t.ge1 = sb(nc, lay, "ge1", [128, 128], F32)
    t.msk = sb(nc, lay, "msk", [128, 128], F32)

    rLN = _region(lay)
    t.x = sb(nc, rLN, "x", [128, 8, T], F32)
    t.logits = sb(nc, rLN, "logits", [8, T], F32)
    t.sig = sb(nc, rLN, "sig", [8, T], F32)
    t.lb = sb(nc, rLN, "lb", [8, T], F32)
    t.mask_fm = sb(nc, rLN, "mask_fm", [8, T], F32)

    rMOE = _region(lay)
    t.wd_sb = sb(nc, rMOE, "wd_sb", [128, 16, 8, 128], BF16)
    t.hprime = sb(nc, rMOE, "hprime", [128, 16, T], BF16)
    rMOE.pos = max(rMOE.pos, rLN.pos)
    t.wg_sb = [sb(nc, rMOE, f"wg_sb{i}", [128, 8, 128], BF16) for i in range(2)]
    t.wu_sb = [sb(nc, rMOE, f"wu_sb{i}", [128, 8, 128], BF16) for i in range(2)]
    t.sg = [sb(nc, rMOE, f"sg{i}", [128, QG], F32) for i in range(2)]
    t.cbc = [sb(nc, rMOE, f"cbc{i}", [128, QG], F32) for i in range(NQG)]
    t.eo = [sb(nc, rMOE, f"eo{i}", [128, QG], F32) for i in range(2)]

    rATT = _region(lay)
    t.wkv = sb(nc, rATT, "wkv", [128, 8, 256], BF16)
    wq_off = rATT.alloc(8 * 512 * 2)
    t.wq = nc.alloc_sbuf_tensor_at("wq", [128, 8, 512], BF16, offset=wq_off)
    t.pav_sb = [nc.alloc_sbuf_tensor_at(f"pav_sb{i}", [65, QG], F32,
                                        offset=wq_off + i * QG * 4)
                for i in range(NQG)]
    t.cos2 = sb(nc, rATT, "cos2", [64, T], BF16)
    t.sin2 = sb(nc, rATT, "sin2", [64, T], BF16)
    t.kvn = sb(nc, rATT, "kvn", [128, 2, T], BF16)
    t.q_c = sb(nc, rATT, "q_c", [128, 4, T], BF16)
    t.krw = [sb(nc, rATT, f"krw{i}", [128, 8, 128], BF16) for i in range(2)]
    t.krwp = [sb(nc, rATT, f"krwp{i}", [128, 8, 128], BF16) for i in range(2)]
    t.qrw = [sb(nc, rATT, f"qrw{i}", [128, 4, 128], BF16) for i in range(2)]
    t.qrwp = [sb(nc, rATT, f"qrwp{i}", [128, 4, 128], BF16) for i in range(2)]
    t.kdw = [sb(nc, rATT, f"kdw{i}", [128, 2, 128], BF16) for i in range(2)]
    t.vdw = [sb(nc, rATT, f"vdw{i}", [128, 2, 128], BF16) for i in range(2)]
    t.qdw = [sb(nc, rATT, f"qdw{i}", [128, 4, 128], BF16) for i in range(2)]
    t.vT = [sb(nc, rATT, f"vT{i}", [128, 16, 130], BF16) for i in range(2)]
    t.mask01 = sb(nc, rATT, "mask01", [128, 128], BF16)
    wsb1 = sb(nc, rATT, "wsb", [64, QG], F32)
    t.wsb = [wsb1, wsb1]
    t.ao_sb = [sb(nc, rATT, f"ao_sb{i}", [64, QG], BF16) for i in range(2)]
    t.ror_a = nc.alloc_sbuf_tensor_at("ror_a", [128, T], BF16, offset=t.csb_off)
    t.ror_b = nc.alloc_sbuf_tensor_at("ror_b", [128, T], BF16, offset=t.csb_off + T * 2)
    t.kpk = [sb(nc, rATT, f"kpk{i}", [128, 2, T], BF16) for i in range(3)]
    t.qpk = [sb(nc, rATT, f"qpk{i}", [128, 2, T], BF16) for i in range(3)]
    t.expT = [sb(nc, rATT, f"expT{i}", [128, T], BF16) for i in range(3)]

    rUPD = _region(lay)
    t.atile = [sb(nc, rUPD, f"atile{i}", [128, 8, QG], BF16) for i in range(2)]
    t.wo_sb = sb(nc, rUPD, "wo_sb", [128, 8, 8, 128], BF16)
    xt0 = sb(nc, rUPD, "xtile0", [128, 8, QG], F32)
    ct0 = sb(nc, rUPD, "ctile0", [128, 8, QG], F32)
    xt1 = sb(nc, rUPD, "xtile1", [128, 8, QG], F32)
    ct1 = sb(nc, rUPD, "ctile1", [128, 8, QG], F32)
    t.xtile = [xt0, xt1]
    t.ctile = [ct0, ct1]
    t.x1t = [sb(nc, rUPD, f"x1t{i}", [128, 8, QG], F32) for i in range(2)]
    lay.pos = max(rLN.peak, rMOE.peak, rATT.peak, rUPD.peak)
    lay.peak = max(lay.peak, lay.pos)
    assert lay.pos <= lay.limit, f"L0 SBUF overflow {lay.pos}"
    return t


def alloc_l1(nc, lay, t):
    NPOOL = 6
    t.l1vecs = sb(nc, lay, "l1vecs", [128, 8, 8], F32)
    t.b1 = sb(nc, lay, "b1", [128, 32], F32)
    t.decay = sb(nc, lay, "decay", [128, 8], F32)
    t.first = sb(nc, lay, "first", [128, 8], F32)
    t.carry_s = sb(nc, lay, "carry_s", [128, 8], F32)
    t.carry_x = sb(nc, lay, "carry_x", [128, 8], F32)
    t.pool = [sb(nc, lay, f"pool{i}", [128, 8, 128], BF16) for i in range(NPOOL)]
    t.x1cb = [sb(nc, lay, f"x1c{i}", [128, 8, TC], F32) for i in range(2)]
    t.bc_c = sb(nc, lay, "bc_c", [128, TC], F32)
    t.xnc = sb(nc, lay, "xnc", [128, 8, TC], F32)
    t.xnb_c = sb(nc, lay, "xnb_c", [128, 8, TC], BF16)
    t.xt1 = sb(nc, lay, "xt1", [128, 8, TC], F32)
    doff = lay.alloc(8 * TC * 4)
    koff = lay.alloc(8 * TC * 4)
    offs = [doff, koff]
    t.dif2 = [nc.alloc_sbuf_tensor_at(f"dif{i}", [128, 8, TC], F32, offset=offs[i])
              for i in range(2)]
    t.s_c2 = [nc.alloc_sbuf_tensor_at(f"s_c{i}", [128, 8, TC], F32, offset=offs[i])
              for i in range(2)]
    t.Ss2 = [nc.alloc_sbuf_tensor_at(f"Ss{i}", [128, 8, TC], F32, offset=offs[i])
             for i in range(2)]
    t.K2 = [nc.alloc_sbuf_tensor_at(f"K{i}", [128, 8, TC], F32, offset=offs[1 - i])
            for i in range(2)]
    t.h1 = sb(nc, lay, "h1", [128, 32, TC], BF16)
    moff = lay.alloc(8 * TC * 2)
    t.mix0 = nc.alloc_sbuf_tensor_at("mix0", [128, 8, TC], BF16, offset=moff)
    t.rk = nc.alloc_sbuf_tensor_at("rk", [128, 8, TC], BF16, offset=moff)
    m1off = lay.alloc(8 * TC * 2)
    t.mix1 = nc.alloc_sbuf_tensor_at("mix1", [128, 8, TC], BF16, offset=m1off)
    voff = lay.alloc(8 * TC * 2)
    assert voff == m1off + 8 * TC * 2
    t.Vv = nc.alloc_sbuf_tensor_at("Vv", [128, 8, TC], BF16, offset=voff)
    t.outc = nc.alloc_sbuf_tensor_at("outc", [128, 8, TC], F32, offset=m1off)
    t.mix = [t.mix0, t.mix1]
    t.R = t.mix1
    t.pool += [sb(nc, lay, f"pool{6 + i}", [128, 8, 128], BF16) for i in range(4)]
    assert lay.pos <= lay.limit, f"L1 SBUF overflow {lay.pos}"
    return t


L0_INPUTS = ["wkv", "wq", "wo", "kdec", "vdec", "qdec", "krope", "krope_p",
             "qrope", "qrope_p", "gate_wt", "wg", "wu", "wd", "gkv", "beta",
             "eye", "mask01", "ones_bf", "ones_row", "cos2", "sin2", "sel"]
L1_INPUTS = ["wk", "wv", "wr", "wo1", "w1", "w2", "l1vecs", "b1",
             "ones_bf", "ones_row"]


def mk_sems(P):
    d = {k: P.sem(k) for k in ["dx", "dc", "dw", "dacc", "dao", "dx1", "dout"]}
    for k in ["t", "v", "s", "g"]:
        d[k] = P.sem(k, chained=True)
    return d


def build_program_l0(packed, const):
    nc = bass.Bass("TRN2", target_bir_lowering=False)
    x_in = nc.dram_tensor("x", [128, 8, T], F32, kind="ExternalInput").ap()
    x1_out = nc.dram_tensor("x1", [128, 8, T], F32, kind="ExternalOutput").ap()
    aps = declare_inputs(nc, packed, const, L0_INPUTS)
    dram = {
        "xspill": nc.dram_tensor("xspill", [128, 8, T], F32, kind="Internal").ap(),
        "comb": nc.dram_tensor("comb", [128, 8, T], F32, kind="Internal").ap(),
        "aout": nc.dram_tensor("aout", [128, 8, T], BF16, kind="Internal").ap(),
    }
    lay = Bump()
    t = NS()
    alloc_shared(nc, lay, t)
    alloc_l0(nc, lay, t)
    t.pb = [nc.alloc_psum_tensor(f"pb{i}", [128, QG], F32) for i in range(8)]
    t.pav = {qg: t.pb[4 + qg] for qg in range(NQG)}
    P = Prog(nc)
    sems = mk_sems(P)
    out_toks = build_l0(P, t, aps, x_in, x1_out, sems, dram)
    P.on("sync").waits(out_toks)
    with nc.Block() as block:
        P.finalize(block)
    return nc


def build_program_l1(packed, const):
    nc = bass.Bass("TRN2", target_bir_lowering=False)
    x1_in = nc.dram_tensor("x1", [128, 8, T], F32, kind="ExternalInput").ap()
    out = nc.dram_tensor("out", [128, 8, T], F32, kind="ExternalOutput").ap()
    aps = declare_inputs(nc, packed, const, L1_INPUTS)
    lay = Bump()
    t = NS()
    alloc_shared(nc, lay, t, TC)
    alloc_l1(nc, lay, t)
    t.pb = [nc.alloc_psum_tensor(f"pb{i}", [128, QG], F32) for i in range(8)]
    P = Prog(nc)
    sems = mk_sems(P)
    out_toks = build_l1(P, t, aps, x1_in, out, sems)
    P.on("sync").waits(out_toks)
    with nc.Block() as block:
        P.finalize(block)
    return nc


def build_program_fused(packed, const):
    nc = bass.Bass("TRN2", target_bir_lowering=False)
    x_in = nc.dram_tensor("x", [128, 8, T], F32, kind="ExternalInput").ap()
    out = nc.dram_tensor("out", [128, 8, T], F32, kind="ExternalOutput").ap()
    need = list(dict.fromkeys(L0_INPUTS + L1_INPUTS))
    aps = declare_inputs(nc, packed, const, need)
    dram = {
        "xspill": nc.dram_tensor("xspill", [128, 8, T], F32, kind="Internal").ap(),
        "comb": nc.dram_tensor("comb", [128, 8, T], F32, kind="Internal").ap(),
        "aout": nc.dram_tensor("aout", [128, 8, T], BF16, kind="Internal").ap(),
        "x1": nc.dram_tensor("x1", [128, 8, T], F32, kind="Internal").ap(),
        "wl1": nc.dram_tensor("wl1", [96, 128, 8, 128], BF16, kind="Internal").ap(),
    }
    lay = Bump()
    t = NS()
    alloc_shared(nc, lay, t)
    base0 = lay.mark()
    alloc_l0(nc, lay, t)
    l0_peak = lay.pos
    rL1 = Bump()
    rL1.pos = base0
    alloc_l1(nc, rL1, t)
    lay.pos = max(l0_peak, rL1.pos)
    t.pb = [nc.alloc_psum_tensor(f"pb{i}", [128, QG], F32) for i in range(8)]
    t.pav = {qg: t.pb[4 + qg] for qg in range(NQG)}
    P = Prog(nc)
    sems = mk_sems(P)
    x1_toks = build_l0(P, t, aps, x_in, dram["x1"], sems, dram)
    out_toks = build_l1(P, t, aps, dram["x1"], out, sems, entry_bar=x1_toks,
                        wl1=dram["wl1"])
    P.on("sync").waits(out_toks)
    with nc.Block() as block:
        P.finalize(block)
    return nc


def kernel(**inputs):
    ins = {k: np.asarray(v) for k, v in inputs.items()}
    packed, const = pack_host(ins)
    x = ins["L0_attn_x"]
    xs = [np.ascontiguousarray(x[b].T.reshape(8, 128, T).transpose(1, 0, 2))
          for b in range(NCORES)]

    nc = build_program_fused(packed, const)
    need = list(dict.fromkeys(L0_INPUTS + L1_INPUTS))
    in_maps = []
    for b in range(NCORES):
        m = {"x": xs[b]}
        for nm in need:
            m[nm] = packed[nm] if nm in packed else const[nm]
        in_maps.append(m)
    res = run_bass_kernel_spmd(nc, in_maps, core_ids=list(range(NCORES)))

    out = np.empty((NCORES, T, 1024), np.float32)
    for b in range(NCORES):
        o = res.results[b]["out"]
        out[b] = o.transpose(1, 0, 2).reshape(1024, T).T
    return out


# version 37 (final)
# speedup vs baseline: 1.1972x; 1.0004x over previous
import numpy as np
import ml_dtypes
import concourse.bass as bass
import concourse.mybir as mybir
from concourse.bass_utils import run_bass_kernel_spmd

F32 = mybir.dt.float32
BF16 = mybir.dt.bfloat16
AF = mybir.ActivationFunctionType
ALU = mybir.AluOpType
AX = mybir.AxisListType.X
BF = ml_dtypes.bfloat16

T = 2048
NCORES = 8
QG = 512
NQG = T // QG
TC = 512
NTC = T // TC
SHIFT = 10.0
SCALE = float(1.0 / np.sqrt(96.0))

ENGINES = ("sync", "tensor", "vector", "scalar", "gpsimd")


class Sem:
    def __init__(self, handle, name, chained=False):
        self.h = handle
        self.name = name
        self.n = 0
        self.chained = chained

    def tick(self, k=1):
        self.n += k
        return (self, self.n)


class IncH:
    __slots__ = ("incs", "prog", "eng", "idx")
    def __init__(self, incs, prog, eng, idx):
        self.incs = incs
        self.prog = prog
        self.eng = eng
        self.idx = idx

    def then_inc(self, sem, k):
        if sem.chained and sem.n > 0:
            w = self.prog.water[self.eng]
            if w.get(sem.name, 0) < sem.n:
                w[sem.name] = sem.n
                stream = self.prog.streams[self.eng]
                stream.insert(len(stream) - 1 - self.prog._ninserted(self.eng, self.idx),
                              ("wait_ge", (sem.h, sem.n), {}, []))
        self.incs.append((sem, k))
        return sem.tick(k)


class Rec:
    def __init__(self, prog, eng):
        self._p = prog
        self._e = eng

    def wait(self, tok):
        if tok is None:
            return
        sem, v = tok
        w = self._p.water[self._e]
        if w.get(sem.name, 0) >= v:
            return
        w[sem.name] = v
        self._p.streams[self._e].append(("wait_ge", (sem.h, v), {}, []))

    def waits(self, toks):
        for tk in toks:
            self.wait(tk)

    def __getattr__(self, name):
        def call(*a, **k):
            incs = []
            stream = self._p.streams[self._e]
            stream.append((name, a, k, incs))
            return IncH(incs, self._p, self._e, len(stream) - 1)
        return call


class Prog:
    def __init__(self, nc):
        self.nc = nc
        self.streams = {e: [] for e in ENGINES}
        self.water = {e: {} for e in ENGINES}
        self.recs = {e: Rec(self, e) for e in ENGINES}
        self._semid = 0
        self.toks = {}

    def sem(self, name=None, chained=False):
        self._semid += 1
        nm = name or f"s{self._semid}"
        return Sem(self.nc.alloc_semaphore(name=nm), nm, chained)

    def _ninserted(self, eng, idx):
        return 0

    def dma(self, rec, stream, out, in_, **kw):
        if not hasattr(self, "dma_sems"):
            self.dma_sems = {}
        s = self.dma_sems.get(stream)
        if s is None:
            s = self.sem("d_" + stream)
            self.dma_sems[stream] = s
        return rec.dma_start(out=out, in_=in_, **kw).then_inc(s, 16)

    def on(self, eng):
        return self.recs[eng]

    def bank_write(self, TE, bank):
        for tk in self.toks.get(("bankR", bank), []):
            TE.wait(tk)
        self.toks[("bankR", bank)] = []

    def bank_written(self, bank, tok):
        self.toks[("bankW", bank)] = tok

    def bank_read(self, rec, bank):
        rec.wait(self.toks.get(("bankW", bank)))

    def bank_read_done(self, bank, tok):
        self.toks.setdefault(("bankR", bank), []).append(tok)

    def finalize(self, block):
        def mk(stream):
            def body(eng):
                for (name, a, k, incs) in stream:
                    inst = getattr(eng, name)(*a, **k)
                    for (sem, kk) in incs:
                        inst.then_inc(sem.h, kk)
            return body
        for e in ENGINES:
            if self.streams[e]:
                getattr(block, e)(mk(self.streams[e]))

    def n_instructions(self):
        return sum(len(s) for s in self.streams.values())


class Bump:
    def __init__(self, limit=229376, base=16544):
        self.pos = base
        self.peak = base
        self.limit = limit

    def alloc(self, nbytes):
        off = (self.pos + 31) & ~31
        self.pos = off + nbytes
        self.peak = max(self.peak, self.pos)
        assert self.pos <= self.limit, f"SBUF overflow: {self.pos} > {self.limit}"
        return off

    def mark(self):
        return self.pos

    def reset(self, mark):
        self.pos = mark


def sb(nc, lay, name, shape, dtype):
    n = 1
    for s in shape[1:]:
        n *= s
    nbytes = n * (2 if dtype == BF16 else 4)
    off = lay.alloc(nbytes)
    return nc.alloc_sbuf_tensor_at(name, list(shape), dtype, offset=off)


def _fm(a):
    R = a.shape[0] // 128
    return np.ascontiguousarray(a.reshape(R, 128, a.shape[1]).transpose(1, 0, 2))


def _vec(a):
    R = a.shape[0] // 128
    return np.ascontiguousarray(a.reshape(R, 128).T)


def pack_host(ins):
    d = {}
    d["wkv"] = _fm(ins["L0_attn_W_kv"])
    d["wq"] = _fm(ins["L0_attn_W_q"])
    d["wo"] = np.ascontiguousarray(
        ins["L0_attn_W_o"].reshape(8, 128, 8, 128).transpose(1, 0, 2, 3))
    kd = ins["L0_attn_W_kdec"].reshape(2, 128, 8, 128).transpose(2, 1, 0, 3)
    vd = ins["L0_attn_W_vdec"].reshape(2, 128, 8, 128).transpose(2, 1, 0, 3)
    qd = ins["L0_attn_W_qdec"].reshape(4, 128, 8, 128).transpose(2, 1, 0, 3)
    d["kdec"] = np.ascontiguousarray(kd)
    d["vdec"] = np.ascontiguousarray(vd)
    d["qdec"] = np.ascontiguousarray(qd)
    perm = (np.arange(32) + 16) % 32
    def rope_pack(w, nkc):
        wp = w.reshape(nkc, 128, 4, 128).transpose(2, 1, 0, 3)
        return np.ascontiguousarray(wp)
    wkr = ins["L0_attn_W_krope"]
    wqr = ins["L0_attn_W_qrope"]
    permcols = np.concatenate([np.arange(512).reshape(16, 32)[h][perm] for h in range(16)])
    d["krope"] = rope_pack(wkr, 8)
    d["krope_p"] = rope_pack(wkr[:, permcols], 8)
    d["qrope"] = rope_pack(wqr, 4)
    d["qrope_p"] = rope_pack(wqr[:, permcols], 4)
    d["gate_wt"] = _fm(np.ascontiguousarray(ins["L0_ffn_gate_w"].T))
    gp = np.concatenate([ins["L0_ffn_gate_proj_w"], ins["L0_ffn_shared_gate_w"]], 0)
    up = np.concatenate([ins["L0_ffn_up_proj_w"], ins["L0_ffn_shared_up_w"]], 0)
    dn = np.concatenate([ins["L0_ffn_down_proj_w"], ins["L0_ffn_shared_down_w"]], 0)
    d["wg"] = np.ascontiguousarray(gp.reshape(9, 8, 128, 16, 128).transpose(0, 3, 2, 1, 4))
    d["wu"] = np.ascontiguousarray(up.reshape(9, 8, 128, 16, 128).transpose(0, 3, 2, 1, 4))
    d["wd"] = np.ascontiguousarray(dn.reshape(9, 16, 128, 8, 128).transpose(0, 2, 1, 3, 4))
    for nm, key in [("wk", "L1_attn_Wk"), ("wv", "L1_attn_Wv"),
                    ("wr", "L1_attn_Wr"), ("wo1", "L1_attn_Wo")]:
        wt = np.ascontiguousarray(ins[key].T)
        d[nm] = np.ascontiguousarray(wt.reshape(8, 128, 8, 128).transpose(2, 1, 0, 3))
    d["w1"] = np.ascontiguousarray(
        ins["L1_ffn_W1"].reshape(8, 128, 32, 128).transpose(2, 1, 0, 3))
    d["w2"] = np.ascontiguousarray(ins["L1_ffn_W2"].reshape(32, 128, 8, 128))
    d["l1vecs"] = np.stack([_vec(ins[k]) for k in
                            ["L1_attn_time_mix_k", "L1_attn_time_mix_v", "L1_attn_time_mix_r",
                             "L1_attn_time_decay", "L1_attn_time_first",
                             "L1_attn_ln_w", "L1_attn_ln_b", "L1_ffn_b2"]], 1)
    d["b1"] = _vec(ins["L1_ffn_b1"])
    d["gkv"] = _vec(ins["L0_attn_g_kv"])
    d["beta"] = np.ascontiguousarray(ins["L0_ffn_beta"].reshape(8, 1))
    c = {}
    c["eye"] = np.eye(128, dtype=np.float32)
    ki = np.arange(128)[:, None]; qi = np.arange(128)[None, :]
    c["maskdia"] = np.where(ki > qi, np.float32(-1e30), np.float32(0.0))
    c["mask01"] = np.where(ki <= qi, 1.0, 0.0).astype(BF)
    c["ones_bf"] = np.ones((128, 1), dtype=BF)
    c["ones_row"] = np.ones((1, 128), dtype=np.float32)
    inv_freq = 1.0 / 10000.0 ** (np.arange(0, 32, 2, dtype=np.float32) / 32)
    freqs = np.outer(np.arange(T, dtype=np.float32), inv_freq)
    emb = np.concatenate([freqs, freqs], -1)
    cos1 = np.cos(emb).astype(np.float32).T
    sin1 = np.sin(emb).astype(np.float32).T
    sins = sin1.copy(); sins[:16] = -sin1[:16]
    c["cos2"] = np.ascontiguousarray(np.concatenate([cos1, cos1], 0)).astype(BF)
    c["sin2"] = np.ascontiguousarray(np.concatenate([sins, sins], 0)).astype(BF)
    sel = np.zeros((8, 8, 128), dtype=np.float32)
    for e in range(8):
        sel[e, e, :] = 1.0
    c["sel"] = np.ascontiguousarray(sel.transpose(1, 0, 2))
    return d, c


def declare_inputs(nc, packed, const, need):
    aps = {}
    for nm in need:
        src = packed if nm in packed else const
        a = src[nm]
        dt = BF16 if a.dtype == BF else F32
        aps[nm] = nc.dram_tensor(nm, list(a.shape), dt, kind="ExternalInput").ap()
    return aps


class NS:
    pass


def ln_stats(P, t, sems, x_view_fn, width, b_sum, b_sq, cast_s=False):
    V, S, TE, G = P.on("vector"), P.on("scalar"), P.on("tensor"), P.on("gpsimd")
    toks = P.toks
    for kc in range(8):
        xt = x_view_fn(kc)
        if cast_s:
            S.wait(toks.get(("lnb", kc % 2)))
            tk_b = S.copy(t.xb_bf[kc % 2][:, 0:width], xt).then_inc(sems["s"], 1)
        else:
            G.wait(toks.get(("lnb", kc % 2)))
            tk_b = G.tensor_copy(t.xb_bf[kc % 2][:, 0:width], xt).then_inc(sems["g"], 1)
        S.wait(toks.get(("lns", kc % 2)))
        tk_s = S.activation(t.xsq_bf[kc % 2][:, 0:width], xt, AF.Square).then_inc(sems["s"], 1)
        if kc == 0:
            P.bank_write(TE, b_sum)
            P.bank_write(TE, b_sq)
        TE.wait(tk_b)
        tmm = TE.matmul(t.pb[b_sum][0:1, 0:width], t.ones_bf[:], t.xb_bf[kc % 2][:, 0:width],
                        start=(kc == 0), stop=(kc == 7)).then_inc(sems["t"], 1)
        toks[("lnb", kc % 2)] = tmm
        TE.wait(tk_s)
        tmm2 = TE.matmul(t.pb[b_sq][0:1, 0:width], t.ones_bf[:], t.xsq_bf[kc % 2][:, 0:width],
                         start=(kc == 0), stop=(kc == 7)).then_inc(sems["t"], 1)
        toks[("lns", kc % 2)] = tmm2
    P.bank_written(b_sum, tmm2)
    P.bank_written(b_sq, tmm2)
    return tmm2


def ln_rows(P, t, sems, width, b_sum, b_sq, m_dst, rstd_dst, inv_n, eps_ap,
            rstd2_dst=None):
    V, S = P.on("vector"), P.on("scalar")
    P.bank_read(V, b_sum)
    P.bank_read(V, b_sq)
    V.tensor_scalar_mul(m_dst, t.pb[b_sum][0:1, 0:width], inv_n)
    tk = V.tensor_scalar_mul(t.row_a[0:1, 0:width], t.pb[b_sq][0:1, 0:width], inv_n)\
        .then_inc(sems["v"], 1)
    P.bank_read_done(b_sum, tk)
    P.bank_read_done(b_sq, tk)
    V.wait(tk)
    tk = V.scalar_tensor_tensor(t.row_b[0:1, 0:width], m_dst, -1.0, m_dst,
                                ALU.mult, ALU.mult).then_inc(sems["v"], 1)
    V.wait(tk)
    tk = V.tensor_add(t.row_a[0:1, 0:width], t.row_a[0:1, 0:width],
                      t.row_b[0:1, 0:width]).then_inc(sems["v"], 1)
    S.wait(tk)
    tk = S.activation(t.row_b[0:1, 0:width], t.row_a[0:1, 0:width], AF.Ln,
                      bias=eps_ap, scale=1.0).then_inc(sems["s"], 1)
    S.wait(tk)
    tk = S.activation(rstd_dst, t.row_b[0:1, 0:width], AF.Exp,
                      scale=-0.5).then_inc(sems["s"], 1)
    if rstd2_dst is None:
        return tk
    V.wait(tk)
    tk = V.tensor_mul(t.row_b[0:1, 0:width], t.row_a[0:1, 0:width], rstd_dst)\
        .then_inc(sems["v"], 1)
    V.wait(tk)
    tk = V.tensor_mul(t.row_b[0:1, 0:width], t.row_b[0:1, 0:width], rstd_dst)\
        .then_inc(sems["v"], 1)
    S.wait(tk)
    tk = S.activation(t.row_b[0:1, 0:width], t.row_b[0:1, 0:width], AF.Ln,
                      bias=eps_ap, scale=1.0).then_inc(sems["s"], 1)
    S.wait(tk)
    tk = S.activation(rstd2_dst, t.row_b[0:1, 0:width], AF.Exp,
                      scale=-0.5).then_inc(sems["s"], 1)
    return tk


def bcast2(P, t, sems, width, row_m, row_r, b_m, b_r, ready_tok):
    V, TE = P.on("vector"), P.on("tensor")
    TE.wait(ready_tok)
    P.bank_write(TE, b_m)
    tk_m = TE.matmul(t.pb[b_m][:, 0:width], t.ones_row[0:1, :], row_m,
                     start=True, stop=True).then_inc(sems["t"], 1)
    P.bank_written(b_m, tk_m)
    P.bank_write(TE, b_r)
    tk_r = TE.matmul(t.pb[b_r][:, 0:width], t.ones_row[0:1, :], row_r,
                     start=True, stop=True).then_inc(sems["t"], 1)
    P.bank_written(b_r, tk_r)
    P.bank_read(V, b_m)
    V.tensor_copy(t.bc_a[:, 0:width], t.pb[b_m][:, 0:width])
    P.bank_read(V, b_r)
    tk = V.tensor_copy(t.bc_b[:, 0:width], t.pb[b_r][:, 0:width]).then_inc(sems["v"], 1)
    P.bank_read_done(b_m, tk)
    P.bank_read_done(b_r, tk)
    return tk


def build_l0(P, t, aps, x_in_ap, x1_out_ap, sems, dram):
    V, S, TE, G, SY = (P.on("vector"), P.on("scalar"), P.on("tensor"),
                       P.on("gpsimd"), P.on("sync"))
    toks = P.toks

    tk_xq = [P.dma(SY, f"x{qg}", t.x[:, :, qg * QG:(qg + 1) * QG],
                   x_in_ap[:, :, qg * QG:(qg + 1) * QG]) for qg in range(NQG)]
    tk_x = tk_xq[-1]
    tk_c = None
    for nm, dst in [("eye", t.eye), ("ones_row", t.ones_row),
                    ("sel", t.sel),
                    ("gate_wt", t.gate_wt), ("beta", t.beta), ("gkv", t.gkv),
                    ("ones_bf", t.ones_bf)]:
        tk_c = P.dma(SY, "const", dst[:], aps[nm])
    V.memset(t.eps_row[:], 1e-5)
    V.memset(t.eps_row6[:], 1e-6)
    tk_ms = V.memset(t.neg10[:], -SHIFT).then_inc(sems["v"], 1)

    TE.wait(tk_c); TE.wait(tk_ms)
    xn_tok = None
    for qg in range(NQG):
        sl = slice(qg * QG, (qg + 1) * QG)
        for e in (G, S, V, TE):
            e.wait(tk_xq[qg])
        st_tok = ln_stats(P, t, sems, lambda kc: t.x[:, kc, sl], QG, 0, 1)
        row_tok = ln_rows(P, t, sems, QG, 0, 1,
                          t.m_row[0:1, 0:QG], t.rstd_row[0:1, 0:QG], 1.0 / 1024.0,
                          t.eps_row[0:1, :])
        bc_tok = bcast2(P, t, sems, QG, t.m_row[0:1, 0:QG], t.rstd_row[0:1, 0:QG],
                        2, 3, row_tok)
        V.wait(bc_tok)
        for kc in range(8):
            scr = t.xn_f32[kc % 2]
            V.wait(toks.get(("xnscr", kc % 2)))
            V.wait(toks.get(("xnscrg", kc % 2)))
            tk0 = V.tensor_sub(scr[:], t.x[:, kc, sl], t.bc_a[:, 0:QG])\
                .then_inc(sems["v"], 1)
            V.wait(tk0)
            tk = V.tensor_mul(scr[:], scr[:], t.bc_b[:, 0:QG]).then_inc(sems["v"], 1)
            G.wait(tk)
            tkg = G.tensor_copy(t.xn[:, kc, sl], scr[:]).then_inc(sems["g"], 1)
            toks[("xnscrg", kc % 2)] = tkg
            xn_tok = tkg
            toks[("xnq", qg)] = tkg
            if kc == 0:
                P.bank_write(TE, 4)
            TE.wait(tk)
            tmm = TE.matmul(t.pb[4][0:8, 0:QG], t.gate_wt[:, kc, :], scr[:],
                            start=(kc == 0), stop=(kc == 7)).then_inc(sems["t"], 1)
            toks[("xnscr", kc % 2)] = tmm
        P.bank_written(4, tmm)
        P.bank_read(V, 4)
        tk = V.tensor_copy(t.logits[:, sl], t.pb[4][0:8, 0:QG]).then_inc(sems["v"], 1)
        P.bank_read_done(4, tk)
        S.wait(tk)
        tk_sig = S.activation(t.sig[:, sl], t.logits[:, sl], AF.Sigmoid)\
            .then_inc(sems["s"], 1)
        V.wait(tk)
        tk_lb = V.tensor_scalar_add(t.lb[:, sl], t.logits[:, sl], t.beta[:])\
            .then_inc(sems["v"], 1)
        TE.wait(tk_lb)
        P.bank_write(TE, 5)
        for i in range(4):
            tkt = TE.transpose(t.pb[5][:, i * 8:(i + 1) * 8],
                               t.lb[:, qg * QG + i * 128:qg * QG + (i + 1) * 128],
                               t.eye[0:8, 0:8]).then_inc(sems["t"], 1)
        P.bank_written(5, tkt)
        P.bank_read(V, 5)
        tk = V.tensor_copy(t.lb_tok[:, 0:32], t.pb[5][:, 0:32]).then_inc(sems["v"], 1)
        P.bank_read_done(5, tk)
        V.wait(tk)
        for i in range(4):
            V.tensor_reduce(t.m1[:, i:i + 1], t.lb_tok[:, i * 8:(i + 1) * 8], AX, ALU.max)
        tk = V.memset(t.junk[:], 0.0).then_inc(sems["v"], 1)
        V.wait(tk)
        for i in range(4):
            V.tensor_scalar(t.ge1[:, i * 8:(i + 1) * 8], t.lb_tok[:, i * 8:(i + 1) * 8],
                            t.m1[:, i:i + 1], None, ALU.is_ge)
        tk = V.memset(t.junk[:], 0.0).then_inc(sems["v"], 1)
        V.wait(tk)
        for i in range(4):
            V.scalar_tensor_tensor(t.msk[:, i * 8:(i + 1) * 8], t.ge1[:, i * 8:(i + 1) * 8],
                                   -1e30, t.lb_tok[:, i * 8:(i + 1) * 8], ALU.mult, ALU.add)
        tk = V.memset(t.junk[:], 0.0).then_inc(sems["v"], 1)
        V.wait(tk)
        for i in range(4):
            V.tensor_reduce(t.m1[:, i:i + 1], t.msk[:, i * 8:(i + 1) * 8], AX, ALU.max)
        tk = V.memset(t.junk[:], 0.0).then_inc(sems["v"], 1)
        V.wait(tk)
        for i in range(4):
            V.tensor_scalar(t.msk[:, i * 8:(i + 1) * 8], t.msk[:, i * 8:(i + 1) * 8],
                            t.m1[:, i:i + 1], None, ALU.is_ge)
        tk = V.memset(t.junk[:], 0.0).then_inc(sems["v"], 1)
        V.wait(tk)
        tk_m2 = V.tensor_add(t.ge1[:, 0:32], t.ge1[:, 0:32], t.msk[:, 0:32])\
            .then_inc(sems["v"], 1)
        TE.wait(tk_m2)
        for i in range(4):
            P.bank_write(TE, 5)
            tkt = TE.transpose(t.pb[5][0:8, 0:128], t.ge1[:, i * 8:(i + 1) * 8], t.eye[:])\
                .then_inc(sems["t"], 1)
            P.bank_written(5, tkt)
            P.bank_read(V, 5)
            tk = V.tensor_copy(t.mask_fm[:, qg * QG + i * 128:qg * QG + (i + 1) * 128],
                               t.pb[5][0:8, 0:128]).then_inc(sems["v"], 1)
            P.bank_read_done(5, tk)
        V.wait(tk_sig); V.wait(tk)
        tk_cq = V.tensor_mul(t.c_sb[:, sl], t.mask_fm[:, sl], t.sig[:, sl])\
            .then_inc(sems["v"], 1)
        toks[("cqg", qg)] = tk_cq
    logits_tok = tk

    tk_csb = toks[("cqg", NQG - 1)]
    x_dead = [logits_tok, xn_tok, tk_csb]

    moe_acc_toks = []
    last_acc = None
    V.waits(x_dead)
    for e in range(9):
        tk_wd = None
        if e < 8:
            for qg in range(NQG):
                TE.wait(toks[("cqg", qg)])
                P.bank_write(TE, 6)
                tkb = TE.matmul(t.pb[6][:, 0:QG], t.sel[:, e, :],
                                t.c_sb[:, qg * QG:(qg + 1) * QG], start=True, stop=True)\
                    .then_inc(sems["t"], 1)
                P.bank_written(6, tkb)
                P.bank_read(V, 6)
                V.wait(toks.get(("cbcfree", qg)))
                tk = V.tensor_copy(t.cbc[qg][:], t.pb[6][:, 0:QG]).then_inc(sems["v"], 1)
                P.bank_read_done(6, tk)
                toks[("cbc", qg)] = tk
        for mc in range(16):
            bi = mc % 2
            G.wait(toks.get(("wgfree", bi)))
            tk_wg = P.dma(G, f"wg{bi}", t.wg_sb[bi][:], aps["wg"][e, mc])
            tk_wu = P.dma(G, f"wu{bi}", t.wu_sb[bi][:], aps["wu"][e, mc])
            if mc == 1:
                G.wait(toks.get(("wdfree",)))
                G.waits(x_dead)
                tk_wd = P.dma(G, "wd", t.wd_sb[:], aps["wd"][e])
            for qg in range(NQG):
                bg = qg % 2
                TE.wait(tk_wg); TE.wait(tk_wu)
                TE.wait(toks[("xnq", qg)])
                if e < 8:
                    TE.wait(toks[("cbc", qg)])
                xn_r = t.xn[:, :, qg * QG:(qg + 1) * QG]
                P.bank_write(TE, bg)
                for kc in range(8):
                    mh = TE.matmul(t.pb[bg][:, 0:QG], t.wg_sb[bi][:, kc, :], xn_r[:, kc, :],
                              start=(kc == 0), stop=(kc == 7))
                tk_g = mh.then_inc(sems["t"], 1)
                P.bank_written(bg, tk_g)
                P.bank_write(TE, 2 + bg)
                for kc in range(8):
                    mh = TE.matmul(t.pb[2 + bg][:, 0:QG], t.wu_sb[bi][:, kc, :], xn_r[:, kc, :],
                              start=(kc == 0), stop=(kc == 7))
                tk_u = mh.then_inc(sems["t"], 1)
                P.bank_written(2 + bg, tk_u)
                if qg == NQG - 1:
                    toks[("wgfree", bi)] = tk_u
                P.bank_read(S, bg)
                S.wait(toks.get(("sgfree", bg)))
                tk_sg = S.activation(t.sg[bg][:], t.pb[bg][:, 0:QG], AF.Silu)\
                    .then_inc(sems["s"], 1)
                P.bank_read_done(bg, tk_sg)
                V.wait(tk_sg)
                P.bank_read(V, 2 + bg)
                V.wait(toks.get(("hfree",)))
                if e < 8:
                    V.wait(toks[("cbc", qg)])
                    tk1 = V.tensor_mul(t.sg[bg][:], t.sg[bg][:], t.pb[2 + bg][:, 0:QG])\
                        .then_inc(sems["v"], 1)
                    V.wait(tk1)
                    tk_h = V.tensor_mul(t.hprime[:, mc, qg * QG:(qg + 1) * QG],
                                        t.sg[bg][:], t.cbc[qg][:]).then_inc(sems["v"], 1)
                else:
                    tk_h = V.tensor_mul(t.hprime[:, mc, qg * QG:(qg + 1) * QG],
                                        t.sg[bg][:], t.pb[2 + bg][:, 0:QG])\
                        .then_inc(sems["v"], 1)
                P.bank_read_done(2 + bg, tk_h)
                toks[("sgfree", bg)] = tk_h
                toks[("h", mc, qg)] = tk_h
        TE.wait(tk_wd)
        for hc in range(8):
            for qg in range(NQG):
                bd = 4 + (hc * NQG + qg) % 2
                P.bank_write(TE, bd)
                for mc in range(16):
                    TE.wait(toks[("h", mc, qg)])
                    mh = TE.matmul(t.pb[bd][:, 0:QG], t.wd_sb[:, mc, hc, :],
                              t.hprime[:, mc, qg * QG:(qg + 1) * QG],
                              start=(mc == 0), stop=(mc == 15))
                tk_d = mh.then_inc(sems["t"], 1)
                P.bank_written(bd, tk_d)
                if hc == 7 and qg == NQG - 1:
                    toks[("wdfree",)] = tk_d
                    toks[("hfree",)] = tk_d
                P.bank_read(V, bd)
                V.wait(toks.get(("eofree", (hc * NQG + qg) % 2)))
                tk_ev = V.tensor_copy(t.eo[(hc * NQG + qg) % 2][:], t.pb[bd][:, 0:QG])\
                    .then_inc(sems["v"], 1)
                P.bank_read_done(bd, tk_ev)
                G.wait(tk_ev)
                last_acc = P.dma(G, f"acc{(hc * NQG + qg) % 2}",
                                 dram["comb"][:, hc, qg * QG:(qg + 1) * QG],
                                 t.eo[(hc * NQG + qg) % 2][:],
                                 accum_op=(ALU.bypass if e == 0 else ALU.add))
                toks[("eofree", (hc * NQG + qg) % 2)] = last_acc
                if e == 8:
                    moe_acc_toks.append(last_acc)
        for qg in range(NQG):
            toks[("cbcfree", qg)] = toks[("h", 15, qg)]

    moe_bar = [(P.dma_sems["acc0"], P.dma_sems["acc0"].n),
               (P.dma_sems["acc1"], P.dma_sems["acc1"].n)]
    for E_ in (V, S, TE, G, SY):
        E_.waits(moe_bar)
    tk_wkv = P.dma(G, "wkv", t.wkv[:], aps["wkv"])
    tk_wq = P.dma(G, "wq", t.wq[:], aps["wq"])
    tk_m01 = P.dma(SY, "m01", t.mask01[:], aps["mask01"])
    tk_cos = P.dma(SY, "cos", t.cos2[:], aps["cos2"])
    tk_trig = P.dma(SY, "sin", t.sin2[:], aps["sin2"])
    V.wait(tk_cos); V.wait(tk_trig); V.wait(tk_m01)

    TE.wait(tk_wkv); TE.wait(tk_wq)
    kvn_tok = None
    for qg in range(NQG):
        sl = slice(qg * QG, (qg + 1) * QG)
        for oc in range(2):
            b = oc
            P.bank_write(TE, b)
            for kc in range(8):
                mh = TE.matmul(t.pb[b][:, 0:QG], t.wkv[:, kc, oc * 128:(oc + 1) * 128],
                          t.xn[:, kc, sl], start=(kc == 0), stop=(kc == 7))
            tk = mh.then_inc(sems["t"], 1)
            P.bank_written(b, tk)
            P.bank_read(V, b)
            V.wait(toks.get(("kvct", oc)))
            tk = V.tensor_copy(t.xn_f32[oc][:], t.pb[b][:, 0:QG]).then_inc(sems["v"], 1)
            P.bank_read_done(b, tk)
            toks[("kvc", oc)] = tk
        for oc in range(4):
            b = 2 + oc % 2
            P.bank_write(TE, b)
            for kc in range(8):
                mh = TE.matmul(t.pb[b][:, 0:QG], t.wq[:, kc, oc * 128:(oc + 1) * 128],
                          t.xn[:, kc, sl], start=(kc == 0), stop=(kc == 7))
            tk = mh.then_inc(sems["t"], 1)
            P.bank_written(b, tk)
            P.bank_read(V, b)
            tk = V.tensor_copy(t.q_c[:, oc, sl], t.pb[b][:, 0:QG]).then_inc(sems["v"], 1)
            P.bank_read_done(b, tk)
        qc_tok = tk
        for oc in range(2):
            S.wait(toks[("kvc", oc)])
            S.wait(toks.get(("lns", oc)))
            tk_s = S.activation(t.xsq_bf[oc][:, 0:QG], t.xn_f32[oc][:], AF.Square)\
                .then_inc(sems["s"], 1)
            if oc == 0:
                P.bank_write(TE, 4)
            TE.wait(tk_s)
            tmm = TE.matmul(t.pb[4][0:1, 0:QG], t.ones_bf[:], t.xsq_bf[oc][:, 0:QG],
                            start=(oc == 0), stop=(oc == 1)).then_inc(sems["t"], 1)
            toks[("lns", oc)] = tmm
        P.bank_written(4, tmm)
        P.bank_read(V, 4)
        tk = V.tensor_scalar_mul(t.row_a[0:1, 0:QG], t.pb[4][0:1, 0:QG], 1.0 / 256.0)\
            .then_inc(sems["v"], 1)
        P.bank_read_done(4, tk)
        S.wait(tk)
        tk = S.activation(t.row_b[0:1, 0:QG], t.row_a[0:1, 0:QG], AF.Ln,
                          bias=t.eps_row6[0:1, :], scale=1.0).then_inc(sems["s"], 1)
        S.wait(tk)
        tk = S.activation(t.row_a[0:1, 0:QG], t.row_b[0:1, 0:QG], AF.Exp,
                          scale=-0.5).then_inc(sems["s"], 1)
        TE.wait(tk)
        P.bank_write(TE, 5)
        tkb = TE.matmul(t.pb[5][:, 0:QG], t.ones_row[0:1, :], t.row_a[0:1, 0:QG],
                        start=True, stop=True).then_inc(sems["t"], 1)
        P.bank_written(5, tkb)
        P.bank_read(V, 5)
        tkc = V.tensor_copy(t.bc_a[:, 0:QG], t.pb[5][:, 0:QG]).then_inc(sems["v"], 1)
        P.bank_read_done(5, tkc)
        for oc in range(2):
            V.wait(tkc)
            V.wait(toks.get(("kvnscr", oc)))
            tk1 = V.tensor_mul(t.xn_f32[oc][:], t.xn_f32[oc][:], t.bc_a[:, 0:QG])\
                .then_inc(sems["v"], 1)
            toks[("kvct", oc)] = tk1
            V.wait(tk1)
            kvn_tok = V.tensor_scalar(t.kvn[:, oc, sl], t.xn_f32[oc][:],
                                      t.gkv[:, oc:oc + 1], None, ALU.mult)\
                .then_inc(sems["v"], 1)
            toks[("kvnscr", oc)] = kvn_tok

    pending_norm = [None]

    def emit_norm(hh):
        for qg in range(NQG):
            S.wait(toks[("pavdone", qg)])
            S.wait(toks.get(("rowa_nrm",)))
            tk_r = S.activation(t.row_a[0:1, 0:QG], t.pav_sb[qg][64:65, 0:QG],
                                AF.Ln).then_inc(sems["s"], 1)
            toks[("rowa_nrm",)] = tk_r
            b = qg % 2
            TE.wait(tk_r)
            P.bank_write(TE, b)
            tk_b = TE.matmul(t.pb[b][0:64, 0:QG], t.ones_row[0:1, 0:64],
                             t.row_a[0:1, 0:QG], start=True, stop=True)\
                .then_inc(sems["t"], 1)
            P.bank_written(b, tk_b)
            P.bank_read(S, b)
            S.wait(toks.get(("wsbfree",)))
            tk_w = S.activation(t.wsb[qg % 2][:], t.pb[b][0:64, 0:QG],
                                AF.Exp, scale=-1.0).then_inc(sems["s"], 1)
            P.bank_read_done(b, tk_w)
            V.wait(tk_w)
            V.wait(toks.get(("aofree", qg % 2)))
            tk_o = V.tensor_mul(t.ao_sb[qg % 2][:], t.pav_sb[qg][0:64, 0:QG],
                                t.wsb[qg % 2][:]).then_inc(sems["v"], 1)
            toks[("wsbfree",)] = tk_o
            toks[("pavsb", qg)] = tk_o
            G.wait(tk_o)
            tk_st = P.dma(G, f"ao{qg % 2}",
                          dram["aout"][(hh % 2) * 64:(hh % 2) * 64 + 64, hh // 2,
                                       qg * QG:(qg + 1) * QG],
                          t.ao_sb[qg % 2][:])
            toks[("aofree", qg % 2)] = tk_st
            toks[("norm_done",)] = tk_st

    for s3 in range(3):
        V.memset(t.kpk[s3][64:128, :, :], 0.0)
        tk_z = V.memset(t.qpk[s3][64:128, :, :], 0.0).then_inc(sems["v"], 1)
    TE.wait(tk_z)

    quad_toks = {}
    pair_toks = {}

    def rope_tasks(hg):
        qb = hg % 2
        tasks = []

        def t_dma():
            G.wait(toks.get(("ropewfree", qb)))
            quad_toks[("w", hg)] = (
                P.dma(G, f"krw{qb}", t.krw[qb][:], aps["krope"][hg]),
                P.dma(G, f"krwp{qb}", t.krwp[qb][:], aps["krope_p"][hg]),
                P.dma(G, f"qrw{qb}", t.qrw[qb][:], aps["qrope"][hg]),
                P.dma(G, f"qrwp{qb}", t.qrwp[qb][:], aps["qrope_p"][hg]))
        tasks.append(t_dma)

        def grp(widx, wsb_f, dst_f, trig_f, nkc, rhs_f, qg, war_id):
            def run():
                wsb = wsb_f(); dst = dst_f(); trig = trig_f(); rhs = rhs_f()
                TE.wait(quad_toks[("w", hg)][widx])
                b = qg % 2
                tsl = slice(qg * QG, (qg + 1) * QG)
                P.bank_write(TE, b)
                for kc in range(nkc):
                    mh = TE.matmul(t.pb[b][:, 0:QG], wsb[:, kc, :], rhs[:, kc, tsl],
                                   start=(kc == 0), stop=(kc == nkc - 1))
                tk = mh.then_inc(sems["t"], 1)
                P.bank_written(b, tk)
                P.bank_read(V, b)
                V.wait(toks.get(("rr", war_id)))
                V.tensor_mul(dst[0:64, tsl], t.pb[b][0:64, 0:QG], trig[:, tsl])
                tk = V.tensor_mul(dst[64:128, tsl], t.pb[b][64:128, 0:QG],
                                  trig[:, tsl]).then_inc(sems["v"], 1)
                P.bank_read_done(b, tk)
                toks[("rordone", war_id)] = tk
            return run

        for qg in range(NQG):
            tasks.append(grp(0, lambda: t.krw[qb], lambda: t.ror_a, lambda: t.cos2,
                             8, lambda: t.xn, qg, "a"))
        for qg in range(NQG):
            tasks.append(grp(1, lambda: t.krwp[qb], lambda: t.ror_b, lambda: t.sin2,
                             8, lambda: t.xn, qg, "b"))

        def comb(pk_f):
            def run():
                V.wait(toks[("rordone", "a")])
                V.wait(toks[("rordone", "b")])
                tk_all = None
                for hh in range(4):
                    h = hg * 4 + hh
                    hp = h // 2
                    s3 = hp % 3
                    pk = pk_f()
                    V.wait(toks.get(("packfree", s3)))
                    tk_all = V.tensor_add(pk[s3][64:96, h % 2, :],
                                          t.ror_a[hh * 32:(hh + 1) * 32, :],
                                          t.ror_b[hh * 32:(hh + 1) * 32, :])\
                        .then_inc(sems["v"], 1)
                toks[("rr", "a")] = tk_all
                toks[("rr", "b")] = tk_all
                quad_toks[("ropec", hg, pk_f is not kf)] = tk_all
            return run
        kf = lambda: t.kpk
        tasks.append(comb(kf))
        for qg in range(NQG):
            tasks.append(grp(2, lambda: t.qrw[qb], lambda: t.ror_a, lambda: t.cos2,
                             4, lambda: t.q_c, qg, "a"))
        for qg in range(NQG):
            tasks.append(grp(3, lambda: t.qrwp[qb], lambda: t.ror_b, lambda: t.sin2,
                             4, lambda: t.q_c, qg, "b"))

        def comb_q_wrap():
            comb(lambda: t.qpk)()
            toks[("ropewfree", qb)] = toks[("rr", "a")]
        tasks.append(comb_q_wrap)
        return tasks

    def pair_tasks(hp):
        pb_ = hp % 2
        s3 = hp % 3
        tasks = []

        def t_dma():
            G.wait(toks.get(("pairwfree", pb_)))
            pair_toks[("w", hp)] = (
                P.dma(G, f"kdw{pb_}", t.kdw[pb_][:], aps["kdec"][hp]),
                P.dma(G, f"vdw{pb_}", t.vdw[pb_][:], aps["vdec"][hp]),
                P.dma(G, f"qdw{pb_}", t.qdw[pb_][:], aps["qdec"][hp]))
        tasks.append(t_dma)

        def grp(widx, wsb_f, nrc, rhs_f, pk_f, qg):
            def run():
                wsb = wsb_f(); rhs = rhs_f(); pk = pk_f()
                TE.wait(pair_toks[("w", hp)][widx])
                TE.wait(kvn_tok); TE.wait(qc_tok)
                b = qg % 2
                P.bank_write(TE, b)
                for rc in range(nrc):
                    mh = TE.matmul(t.pb[b][:, 0:QG], wsb[:, rc, :],
                                   rhs[:, rc, qg * QG:(qg + 1) * QG],
                                   start=(rc == 0), stop=(rc == nrc - 1))
                tk = mh.then_inc(sems["t"], 1)
                P.bank_written(b, tk)
                P.bank_read(V, b)
                V.wait(toks.get(("packfree", s3)))
                V.tensor_copy(pk[s3][0:64, 0, qg * QG:(qg + 1) * QG],
                              t.pb[b][0:64, 0:QG])
                tkk = V.tensor_copy(pk[s3][0:64, 1, qg * QG:(qg + 1) * QG],
                                    t.pb[b][64:128, 0:QG]).then_inc(sems["v"], 1)
                P.bank_read_done(b, tkk)
                pair_toks[("prod", hp)] = tkk
            return run

        for qg in range(NQG):
            tasks.append(grp(0, lambda: t.kdw[pb_], 2, lambda: t.kvn,
                             lambda: t.kpk, qg))
        for qg in range(NQG):
            tasks.append(grp(2, lambda: t.qdw[pb_], 4, lambda: t.q_c,
                             lambda: t.qpk, qg))

        def t_ones():
            V.wait(toks.get(("vTfree", pb_)))
            V.memset(t.vT[pb_][:, :, 64:65], 1.0)
            pair_toks[("ones", hp)] = V.memset(t.vT[pb_][:, :, 129:130], 1.0)\
                .then_inc(sems["v"], 1)
        tasks.append(t_ones)

        def vt_grp(kt):
            def run():
                TE.wait(pair_toks[("w", hp)][1])
                TE.wait(kvn_tok)
                b = kt % 2
                P.bank_write(TE, b)
                for rc in range(2):
                    mh = TE.matmul(t.pb[b][:, 0:128],
                                   t.kvn[:, rc, kt * 128:(kt + 1) * 128],
                                   t.vdw[pb_][:, rc, :], start=(rc == 0), stop=(rc == 1))
                tk = mh.then_inc(sems["t"], 1)
                P.bank_written(b, tk)
                E_ = V if b == 0 else S
                sk = "v" if b == 0 else "s"
                P.bank_read(E_, b)
                E_.wait(pair_toks[("ones", hp)])
                if b == 0:
                    E_.tensor_copy(t.vT[pb_][:, kt, 0:64], t.pb[b][:, 0:64])
                    tkk = E_.tensor_copy(t.vT[pb_][:, kt, 65:129], t.pb[b][:, 64:128])\
                        .then_inc(sems[sk], 1)
                else:
                    E_.copy(t.vT[pb_][:, kt, 0:64], t.pb[b][:, 0:64])
                    tkk = E_.copy(t.vT[pb_][:, kt, 65:129], t.pb[b][:, 64:128])\
                        .then_inc(sems[sk], 1)
                P.bank_read_done(b, tkk)
                pair_toks[("vT", hp) if b == 0 else ("vTs", hp)] = tkk
            return run
        for kt in range(16):
            tasks.append(vt_grp(kt))
        return tasks

    def flash_head(h, tasks):
        hp = h // 2
        hl = h % 2
        hg = h // 4
        pb_ = hp % 2
        s3 = hp % 3
        if pending_norm[0] is not None:
            emit_norm(pending_norm[0])
        pending_norm[0] = h
        TE.wait(pair_toks[("prod", hp)])
        TE.wait(pair_toks.get(("prods", hp)))
        TE.wait(quad_toks[("ropec", hg, False)])
        TE.wait(quad_toks[("ropec", hg, True)])
        TE.wait(pair_toks[("vT", hp)])
        TE.wait(pair_toks.get(("vTs", hp)))

        def emit_scores(kt):
            ep = kt % 3
            sc = {}
            for qg in range(kt // 4, NQG):
                off = kt * 128 - qg * QG
                lo = max(0, off)
                bs = 2 + qg % 2
                P.bank_write(TE, bs)
                mh = TE.matmul(t.pb[bs][:, lo:QG],
                               t.kpk[s3][:, hl, kt * 128:(kt + 1) * 128],
                               t.qpk[s3][:, hl, qg * QG + lo:(qg + 1) * QG],
                               start=True, stop=True)
                tk_sc = mh.then_inc(sems["t"], 1)
                P.bank_written(bs, tk_sc)
                S.wait(toks.get(("expTfree", ep)))
                P.bank_read(S, bs)
                tk_e = S.activation(t.expT[ep][:, qg * QG + lo:(qg + 1) * QG],
                                    t.pb[bs][:, lo:QG], AF.Exp,
                                    bias=t.neg10[:], scale=SCALE).then_inc(sems["s"], 1)
                P.bank_read_done(bs, tk_e)
                if qg == kt // 4:
                    G.wait(tk_e)
                    dsl = slice(qg * QG + lo, qg * QG + lo + 128)
                    tk_e = G.tensor_mul(t.expT[ep][:, dsl], t.expT[ep][:, dsl],
                                        t.mask01[:]).then_inc(sems["g"], 1)
                sc[qg] = (lo, tk_e)
            return sc

        def emit_av(kt, sc):
            ep = kt % 3
            tk_av = None
            for qg in range(kt // 4, NQG):
                lo, tk_e = sc[qg]
                bav = 4 + qg
                if kt == 0:
                    P.bank_write(TE, bav)
                TE.wait(tk_e)
                mh = TE.matmul(t.pav[qg][0:65, lo:QG],
                               t.vT[pb_][:, kt, hl * 65:hl * 65 + 65],
                               t.expT[ep][:, qg * QG + lo:(qg + 1) * QG],
                               start=(kt == 0), stop=(kt == 4 * qg + 3))
                tk_av = mh.then_inc(sems["t"], 1)
                if kt == 4 * qg + 3:
                    P.bank_written(bav, tk_av)
                    P.bank_read(V, bav)
                    V.wait(toks.get(("pavsb", qg)))
                    tk_pv = V.tensor_copy(t.pav_sb[qg][:], t.pav[qg][0:65, 0:QG])\
                        .then_inc(sems["v"], 1)
                    P.bank_read_done(bav, tk_pv)
                    toks[("pavdone", qg)] = tk_pv
            toks[("expTfree", ep)] = tk_av
            return tk_av

        sc_q = {}
        tk_av = None
        for kt in range(16):
            sc_q[kt] = emit_scores(kt)
            for _ in range(2):
                if tasks:
                    tasks.pop(0)()
            if kt >= 2:
                tk_av = emit_av(kt - 2, sc_q.pop(kt - 2))
        tk_av = emit_av(14, sc_q.pop(14))
        tk_av = emit_av(15, sc_q.pop(15))
        toks[("vTfree", pb_)] = tk_av
        return tk_av

    for task in rope_tasks(0):
        task()
    for task in pair_tasks(0):
        task()
    for hp in range(8):
        nxt = []
        if hp < 7:
            if hp % 2 == 1:
                nxt += rope_tasks(hp // 2 + 1)
            nxt += pair_tasks(hp + 1)
        for hl in range(2):
            tk_av = flash_head(hp * 2 + hl, nxt)
        for task in nxt:
            task()
        toks[("pairfree", hp % 2)] = tk_av
        toks[("pairwfree", hp % 2)] = tk_av
        toks[("packfree", hp % 3)] = tk_av
        if hp == 7:
            for E_ in (SY, G):
                E_.wait(pair_toks[("prod", 7)])
                E_.wait(pair_toks.get(("prods", 7)))
                E_.wait(pair_toks[("vT", 7)])
                E_.wait(pair_toks.get(("vTs", 7)))
                E_.wait(quad_toks[("ropec", 3, False)])
                E_.wait(quad_toks[("ropec", 3, True)])
                E_.wait(toks[("vTfree", 0)])
            tk_wo = P.dma(G, "wo", t.wo_sb[:], aps["wo"])
            tk_xt0 = P.dma(SY, "xt0", t.xtile[0][:], x_in_ap[:, :, 0:QG])
            SY.waits(moe_acc_toks)
            tk_cb0 = P.dma(SY, "cb0", t.ctile[0][:], dram["comb"][:, :, 0:QG])
    if pending_norm[0] is not None:
        emit_norm(pending_norm[0])
    tk_o = toks[("norm_done",)]
    aout_done = [(P.dma_sems["ao0"], P.dma_sems["ao0"].n),
                 (P.dma_sems["ao1"], P.dma_sems["ao1"].n)]

    G.waits(aout_done); G.wait(tk_o)
    for E_ in (V, S, TE):
        E_.waits(aout_done)
        E_.wait(tk_o)
    SY.waits(aout_done)
    SY.waits(moe_acc_toks)
    tk_x1 = None
    for tt in range(NQG):
        bi = tt % 2
        x1dst = t.x1cb[tt] if tt < 2 else t.x1t[tt - 2]
        SY.wait(toks.get(("updfree", bi)))
        if tt == 0:
            tk_xt = tk_xt0
            tk_cb = tk_cb0
        else:
            tk_xt = P.dma(SY, f"xt{bi}", t.xtile[bi][:],
                          x_in_ap[:, :, tt * QG:(tt + 1) * QG])
            tk_cb = P.dma(SY, f"cb{bi}", t.ctile[bi][:],
                          dram["comb"][:, :, tt * QG:(tt + 1) * QG])
        tk_at = P.dma(SY, f"at{bi}", t.atile[bi][:],
                      dram["aout"][:, :, tt * QG:(tt + 1) * QG])
        for oc in range(8):
            b = oc % 2
            TE.wait(tk_at); TE.wait(tk_wo)
            P.bank_write(TE, b)
            for ic in range(8):
                mh = TE.matmul(t.pb[b][:, 0:QG], t.wo_sb[:, ic, oc, :],
                          t.atile[bi][:, ic, :], start=(ic == 0), stop=(ic == 7))
            tk_s = mh.then_inc(sems["t"], 1)
            P.bank_written(b, tk_s)
            S.wait(tk_cb)
            S.wait(toks.get(("gtfree", b)))
            tk_g = S.activation(t.gt[b][:, 0:QG], t.ctile[bi][:, oc, :], AF.Silu)\
                .then_inc(sems["s"], 1)
            P.bank_read(V, b)
            V.wait(tk_g); V.wait(tk_xt)
            tk1 = V.tensor_mul(t.gt[b][:, 0:QG], t.gt[b][:, 0:QG], t.pb[b][:, 0:QG])\
                .then_inc(sems["v"], 1)
            P.bank_read_done(b, tk1)
            V.wait(tk1)
            tk_x1 = V.tensor_add(x1dst[:, oc, :], t.gt[b][:, 0:QG],
                                 t.xtile[bi][:, oc, :]).then_inc(sems["v"], 1)
            toks[("gtfree", b)] = tk_x1
        if tt < 2:
            toks[("x1add", tt)] = tk_x1
        G.wait(tk_x1)
        tk_out = P.dma(G, f"x1o{bi}", x1_out_ap[:, :, tt * QG:(tt + 1) * QG],
                       x1dst[:])
        toks[("updfree", bi)] = tk_out
        toks[("x1out", bi)] = tk_out
        toks[("x1wb", tt)] = tk_out
    return [toks[("x1out", 0)], toks[("x1out", 1)]]


def build_l1(P, t, aps, x1_in_ap, out_ap, sems, entry_bar=None, wl1=None):
    V, S, TE, G, SY = (P.on("vector"), P.on("scalar"), P.on("tensor"),
                       P.on("gpsimd"), P.on("sync"))
    toks = P.toks
    NPOOL = len(t.pool)
    x1tok = {}

    def x1load(ci):
        if ci >= NTC:
            return
        b = ci % 2
        if ci < 2:
            if entry_bar is not None:
                x1tok[ci] = toks[("x1add", ci)]
            else:
                SY.wait(toks.get(("x1free", b)))
                x1tok[ci] = P.dma(SY, f"x1c{b}", t.x1cb[b][:],
                                  x1_in_ap[:, :, ci * TC:(ci + 1) * TC])
        else:
            G.wait(toks.get(("x1free", b)))
            x1tok[ci] = P.dma(G, f"x1g{b}", t.x1cb[b][:],
                              x1_in_ap[:, :, ci * TC:(ci + 1) * TC])
    if entry_bar is not None:
        for E_ in (V, S, TE, G, SY):
            E_.waits(entry_bar)
    x1load(0)
    x1load(1)
    pool_i = [0]

    def wtile(src_ap):
        i = pool_i[0]
        b = i % NPOOL
        pool_i[0] += 1
        ser = i % 96
        chunk = i // 96
        G.wait(toks.get(("poolfree", b)))
        G.wait(toks.get(("poolwb", b)))
        SY.wait(toks.get(("poolfree", b)))
        SY.wait(toks.get(("poolwb", b)))
        if wl1 is None:
            tk = P.dma(G, f"wt{b}", t.pool[b][:], src_ap)
        elif chunk == 0:
            tk = P.dma(G, f"wt{b}", t.pool[b][:], src_ap)
            SY.wait(tk)
            tkw = P.dma(SY, f"wls{b}", wl1[ser], t.pool[b][:])
            toks[("wl1w", ser)] = tkw
            toks[("poolwb", b)] = tkw
        else:
            SY.wait(toks[("wl1w", ser)])
            tk = P.dma(SY, f"wth{b}", t.pool[b][:], wl1[ser])
        return b, tk

    consts = [("l1vecs", t.l1vecs), ("b1", t.b1)]
    if entry_bar is None:
        consts += [("ones_bf", t.ones_bf), ("ones_row", t.ones_row)]
    tk_c2 = None
    for nm, dst in consts:
        tk_c2 = P.dma(SY, "l1const", dst[:], aps[nm])
    tk_vecs = tk_c2
    S.wait(tk_vecs)
    S.activation(t.decay[:], t.l1vecs[:, 3, :], AF.Sigmoid)
    tk_df = S.activation(t.first[:], t.l1vecs[:, 4, :], AF.Exp).then_inc(sems["s"], 1)
    V.memset(t.carry_s[:], 0.0)
    tk_cx = V.memset(t.carry_x[:], 0.0).then_inc(sems["v"], 1)
    tk_cs = tk_cx
    if entry_bar is None:
        V.memset(t.eps_row[:], 1e-5)

    MIXK, MIXV, MIXR, LNW, LNB, B2 = 0, 1, 2, 5, 6, 7
    tk_st = None
    heads = {}

    def ln_head(ci):
        b = ci % 2
        xc_ = t.x1cb[b]
        tk_x1 = x1tok[ci]
        for E_ in (G, S, V, TE):
            E_.wait(tk_x1)
        TE.wait(tk_c2)
        st_tok = ln_stats(P, t, sems, lambda kc: xc_[:, kc, :], TC, 2, 3,
                          cast_s=True)
        row_tok = ln_rows(P, t, sems, TC, 2, 3, t.m_row[0:1, 0:TC], t.rstd_row[0:1, 0:TC],
                          1.0 / 1024.0, t.eps_row[0:1, :],
                          rstd2_dst=t.rstd2_row[0:1, 0:TC])
        V.wait(row_tok)
        tk_r3 = V.tensor_mul(t.row_a[0:1, 0:TC], t.rstd_row[0:1, 0:TC],
                             t.rstd2_row[0:1, 0:TC]).then_inc(sems["v"], 1)
        bc_tok = bcast2(P, t, sems, TC, t.m_row[0:1, 0:TC], t.rstd_row[0:1, 0:TC],
                        2, 3, row_tok)
        TE.wait(tk_r3)
        P.bank_write(TE, 2)
        tk_b3 = TE.matmul(t.pb[2][:, 0:TC], t.ones_row[0:1, :], t.row_a[0:1, 0:TC],
                          start=True, stop=True).then_inc(sems["t"], 1)
        P.bank_written(2, tk_b3)
        P.bank_read(V, 2)
        tk_bc3 = V.tensor_copy(t.bc_c[:, 0:TC], t.pb[2][:, 0:TC]).then_inc(sems["v"], 1)
        P.bank_read_done(2, tk_bc3)
        V.wait(bc_tok)
        V.wait(toks.get(("xncfree",)))
        for kc in range(8):
            V.tensor_sub(t.xnc[:, kc, :], xc_[:, kc, :], t.bc_a[:, 0:TC])
        tk = V.memset(t.junk[:], 0.0).then_inc(sems["v"], 1)
        V.wait(tk)
        V.wait(toks.get(("xnbfree",)))
        xnb_toks = []
        for kc in range(8):
            xnb_toks.append(V.tensor_mul(t.xnb_c[:, kc, :], t.xnc[:, kc, :],
                                         t.bc_b[:, 0:TC]).then_inc(sems["v"], 1))
        heads[ci] = (xnb_toks, tk_bc3)

    mheads = {}
    cxtok = [None]

    def mix_head(ci):
        tk_xnb, tk_bc3 = heads[ci]
        dif = t.dif2[ci % 2]
        V.wait(tk_bc3); V.waits(tk_xnb); V.wait(tk_vecs)
        V.wait(toks.get(("xt1free",)))
        for kc in range(8):
            V.tensor_mul(t.xt1[:, kc, :], t.xnc[:, kc, :], t.bc_c[:, 0:TC])
        tk = V.memset(t.junk[:], 0.0).then_inc(sems["v"], 1)
        V.wait(tk)
        for kc in range(8):
            V.tensor_scalar(t.xt1[:, kc, :], t.xt1[:, kc, :],
                            t.l1vecs[:, LNW, kc:kc + 1], t.l1vecs[:, LNB, kc:kc + 1],
                            ALU.mult, ALU.add)
        tk_xt1 = V.memset(t.junk[:], 0.0).then_inc(sems["v"], 1)
        V.wait(tk_xt1); V.wait(cxtok[0])
        for kc in range(8):
            V.tensor_sub(dif[:, kc, 1:TC], t.xt1[:, kc, 1:TC], t.xt1[:, kc, 0:TC - 1])
            V.tensor_sub(dif[:, kc, 0:1], t.xt1[:, kc, 0:1], t.carry_x[:, kc:kc + 1])
        tk_dif = V.memset(t.junk[:], 0.0).then_inc(sems["v"], 1)
        V.wait(tk_dif)
        V.wait(toks.get(("mixfree", "mix0")))
        mixk_toks = []
        for kc in range(8):
            V.scalar_tensor_tensor(t.mix0[:, kc, 1:TC], dif[:, kc, 1:TC],
                                   t.l1vecs[:, MIXK, kc:kc + 1],
                                   t.xt1[:, kc, 0:TC - 1], ALU.mult, ALU.add)
            mixk_toks.append(
                V.scalar_tensor_tensor(t.mix0[:, kc, 0:1], dif[:, kc, 0:1],
                                       t.l1vecs[:, MIXK, kc:kc + 1],
                                       t.carry_x[:, kc:kc + 1], ALU.mult, ALU.add)
                .then_inc(sems["v"], 1))
        mheads[ci] = (tk_xt1, tk_dif, mixk_toks)

    ln_head(0)
    mix_head(0)
    for ci in range(NTC):
        bi = ci % 2
        xc = t.x1cb[bi]
        sl = slice(ci * TC, (ci + 1) * TC)
        tk_xnb, tk_bc3 = heads[ci]
        for oc in range(32):
            wb, tk_w = wtile(aps["w1"][oc])
            b = 4 + oc % 2
            TE.wait(tk_w)
            P.bank_write(TE, b)
            for kc in range(8):
                TE.wait(tk_xnb[kc])
                mh = TE.matmul(t.pb[b][:, 0:TC], t.pool[wb][:, kc, :], t.xnb_c[:, kc, :],
                               start=(kc == 0), stop=(kc == 7))
            tk = mh.then_inc(sems["t"], 1)
            P.bank_written(b, tk)
            toks[("poolfree", wb)] = tk
            if oc == 31:
                toks[("xnbfree",)] = tk
            P.bank_read(S, b)
            S.wait(tk_vecs)
            tk2 = S.activation(t.h1[:, oc, :], t.pb[b][:, 0:TC], AF.Relu,
                               bias=t.b1[:, oc:oc + 1], scale=1.0).then_inc(sems["s"], 1)
            P.bank_read_done(b, tk2)
            toks[("h1", oc)] = tk2
        tk_xt1, tk_dif, tk_mixk = mheads[ci]
        dif = t.dif2[bi]
        Ss = t.Ss2[bi]
        Kt = t.K2[bi]
        s_c = t.s_c2[bi]
        mix_specs = [(MIXK, Kt, None, "K", aps["wk"], t.mix0, "mix0"),
                     (MIXV, t.Vv, None, "V", aps["wv"], t.mix1, "mix1"),
                     (MIXR, t.R, AF.Sigmoid, "R", aps["wr"], t.mix0, "mix0")]
        for mi, (mixcol, dst, act, key, wap, xin, xkey) in enumerate(mix_specs):
            if mi == 0:
                mix_toks = tk_mixk
            else:
                V.wait(tk_dif)
                V.wait(toks.get(("mixfree", xkey)))
                if xin is t.mix1:
                    V.waits(toks.get(("outfree",), []))
                mix_toks = []
                for kc in range(8):
                    V.scalar_tensor_tensor(xin[:, kc, 1:TC], dif[:, kc, 1:TC],
                                           t.l1vecs[:, mixcol, kc:kc + 1],
                                           t.xt1[:, kc, 0:TC - 1], ALU.mult, ALU.add)
                    mix_toks.append(
                        V.scalar_tensor_tensor(xin[:, kc, 0:1], dif[:, kc, 0:1],
                                               t.l1vecs[:, mixcol, kc:kc + 1],
                                               t.carry_x[:, kc:kc + 1],
                                               ALU.mult, ALU.add)
                        .then_inc(sems["v"], 1))
            for oc in range(8):
                wb, tk_w = wtile(wap[oc])
                b = 4 + oc % 2
                TE.wait(tk_w)
                P.bank_write(TE, b)
                for kc in range(8):
                    TE.wait(mix_toks[kc])
                    mh = TE.matmul(t.pb[b][:, 0:TC], t.pool[wb][:, kc, :], xin[:, kc, :],
                                   start=(kc == 0), stop=(kc == 7))
                tk = mh.then_inc(sems["t"], 1)
                P.bank_written(b, tk)
                toks[("poolfree", wb)] = tk
                if oc == 7:
                    toks[("mixfree", xkey)] = tk
                if act is None:
                    P.bank_read(V, b)
                    if dst is t.Vv:
                        V.waits(toks.get(("outfree",), []))
                    tk2 = V.tensor_copy(dst[:, oc, :], t.pb[b][:, 0:TC])\
                        .then_inc(sems["v"], 1)
                else:
                    P.bank_read(S, b)
                    tk2 = S.activation(dst[:, oc, :], t.pb[b][:, 0:TC], act)\
                        .then_inc(sems["s"], 1)
                P.bank_read_done(b, tk2)
                toks[(key, oc)] = tk2
        for kc in range(8):
            V.tensor_mul(t.xnc[:, kc, :], t.xnc[:, kc, :], t.bc_b[:, 0:TC])
        tk_xnf = V.memset(t.junk[:], 0.0).then_inc(sems["v"], 1)
        V.wait(tk_df); V.wait(tk_cs)
        for kc in range(8):
            V.wait(toks[("K", kc)])
            V.tensor_tensor_scan(Ss[:, kc, :],
                                 t.decay[:, kc:kc + 1].broadcast_to([128, TC]),
                                 Kt[:, kc, :], t.carry_s[:, kc:kc + 1],
                                 ALU.mult, ALU.add)
        tk_scan = V.memset(t.junk[:], 0.0).then_inc(sems["v"], 1)
        V.wait(tk_scan)
        tk_cs = V.tensor_copy(t.carry_s[:], Ss[:, :, TC - 1]).then_inc(sems["v"], 1)
        V.wait(tk_cs)
        for kc in range(8):
            V.scalar_tensor_tensor(dif[:, kc, :], Ss[:, kc, :], t.first[:, kc:kc + 1],
                                   Kt[:, kc, :], ALU.mult, ALU.add)
        tk = V.memset(t.junk[:], 0.0).then_inc(sems["v"], 1)
        V.wait(tk)
        for kc in range(8):
            V.wait(toks[("V", kc)])
            V.tensor_mul(dif[:, kc, :], dif[:, kc, :], t.Vv[:, kc, :])
        tk = V.memset(t.junk[:], 0.0).then_inc(sems["v"], 1)
        V.wait(tk)
        V.wait(toks.get(("mixfree", "mix0")))
        rk_toks = []
        for kc in range(8):
            V.wait(toks[("R", kc)])
            rk_toks.append(V.tensor_mul(t.rk[:, kc, :], dif[:, kc, :],
                                        t.R[:, kc, :]).then_inc(sems["v"], 1))
        tk_cx = V.tensor_copy(t.carry_x[:], t.xt1[:, :, TC - 1]).then_inc(sems["v"], 1)
        toks[("xt1free",)] = tk_cx
        cxtok[0] = tk_cx
        for oc in range(8):
            wb, tk_w = wtile(aps["wo1"][oc])
            b = 4 + oc % 2
            TE.wait(tk_w)
            P.bank_write(TE, b)
            for kc in range(8):
                TE.wait(rk_toks[kc])
                mh = TE.matmul(t.pb[b][:, 0:TC], t.pool[wb][:, kc, :], t.rk[:, kc, :],
                               start=(kc == 0), stop=(kc == 7))
            tk = mh.then_inc(sems["t"], 1)
            P.bank_written(b, tk)
            toks[("poolfree", wb)] = tk
            if oc == 7:
                toks[("mixfree", "mix0")] = tk
            P.bank_read(V, b)
            V.wait(tk_xnf)
            tk2 = V.tensor_add(s_c[:, oc, :], t.pb[b][:, 0:TC], t.xnc[:, oc, :])\
                .then_inc(sems["v"], 1)
            P.bank_read_done(b, tk2)
            toks[("s", oc)] = tk2
        toks[("xncfree",)] = tk2
        if ci + 1 < NTC:
            ln_head(ci + 1)
            mix_head(ci + 1)
        for ic in range(32):
            wb, tk_w = wtile(aps["w2"][ic])
            TE.wait(tk_w)
            TE.wait(toks[("h1", ic)])
            if ic == 0:
                for hc in range(8):
                    P.bank_write(TE, hc)
            for hc in range(8):
                mh = TE.matmul(t.pb[hc][:, 0:TC], t.pool[wb][:, hc, :], t.h1[:, ic, :],
                               start=(ic == 0), stop=(ic == 31))
            tk = mh.then_inc(sems["t"], 1)
            toks[("poolfree", wb)] = tk
        tk_f = tk
        toks[("h1free",)] = tk_f
        toks[("xnbfree",)] = tk_f
        for hc in range(8):
            P.bank_written(hc, tk_f)
        for hc in range(8):
            b = hc % 2
            P.bank_read(S, hc)
            S.wait(toks.get(("gtfree", b)))
            tk_g = S.activation(t.gt[b][:, 0:TC], t.pb[hc][:, 0:TC], AF.Silu,
                                bias=t.l1vecs[:, B2, hc:hc + 1], scale=1.0)\
                .then_inc(sems["s"], 1)
            P.bank_read_done(hc, tk_g)
            V.wait(tk_g)
            V.wait(toks[("s", hc)])
            V.waits(toks.get(("outfree",), []))
            tk1 = V.tensor_mul(t.gt[b][:, 0:TC], t.gt[b][:, 0:TC], s_c[:, hc, :])\
                .then_inc(sems["v"], 1)
            V.wait(tk1)
            tk_o = V.tensor_add(t.outc[:, hc, :], t.gt[b][:, 0:TC],
                                xc[:, hc, :]).then_inc(sems["v"], 1)
            toks[("gtfree", b)] = tk_o
            if hc == 7:
                toks[("x1free", bi)] = tk_o
            G.wait(tk_o)
            G.wait(toks.get(("ost", bi, hc % 2)))
            tk_st = P.dma(G, f"o{bi}{hc % 2}", out_ap[:, hc, sl],
                          t.outc[:, hc, :])
            toks[("ost", bi, hc % 2)] = tk_st
            if hc >= 6:
                toks[("outdone", bi, hc % 2)] = tk_st
        toks[("outfree",)] = [toks[("outdone", bi, 0)], toks[("outdone", bi, 1)]]
        x1load(ci + 2)
    return [toks[("outdone", 0, 0)], toks[("outdone", 0, 1)],
            toks[("outdone", 1, 0)], toks[("outdone", 1, 1)]]


def alloc_shared(nc, lay, t, w=QG):
    t.ones_bf = sb(nc, lay, "ones_bf", [128, 1], BF16)
    t.ones_row = sb(nc, lay, "ones_row", [1, 128], F32)
    t.neg10 = sb(nc, lay, "neg10", [128, 1], F32)
    t.eps_row = sb(nc, lay, "eps_row", [1, 1], F32)
    t.eps_row6 = sb(nc, lay, "eps_row6", [1, 1], F32)
    t.junk = sb(nc, lay, "junk", [1, 1], F32)
    t.m_row = sb(nc, lay, "m_row", [1, w], F32)
    t.rstd_row = sb(nc, lay, "rstd_row", [1, w], F32)
    t.rstd2_row = sb(nc, lay, "rstd2_row", [1, w], F32)
    t.row_a = sb(nc, lay, "row_a", [1, w], F32)
    t.row_b = sb(nc, lay, "row_b", [1, w], F32)
    t.bc_a = sb(nc, lay, "bc_a", [128, w], F32)
    t.bc_b = sb(nc, lay, "bc_b", [128, w], F32)
    t.xb_bf = [sb(nc, lay, f"xb_bf{i}", [128, w], BF16) for i in range(2)]
    t.xsq_bf = [sb(nc, lay, f"xsq_bf{i}", [128, w], BF16) for i in range(2)]
    t.gt = [sb(nc, lay, f"gt{i}", [128, w], F32) for i in range(2)]
    return t


def _region(lay):
    r = Bump(lay.limit)
    r.pos = lay.mark()
    r.peak = r.pos
    return r


def alloc_l0(nc, lay, t):
    t.eye = sb(nc, lay, "eye", [128, 128], F32)
    t.sel = sb(nc, lay, "sel", [8, 8, 128], F32)
    t.gate_wt = sb(nc, lay, "gate_wt", [128, 8, 8], F32)
    t.beta = sb(nc, lay, "beta", [8, 1], F32)
    t.gkv = sb(nc, lay, "gkv", [128, 2], F32)
    csb_off = lay.alloc(T * 4)
    t.c_sb = nc.alloc_sbuf_tensor_at("c_sb", [8, T], F32, offset=csb_off)
    t.csb_off = csb_off
    t.xn = sb(nc, lay, "xn", [128, 8, T], BF16)
    t.xn_f32 = [sb(nc, lay, f"xn_f32{i}", [128, QG], F32) for i in range(2)]
    t.lb_tok = sb(nc, lay, "lb_tok", [128, 128], F32)
    t.m1 = sb(nc, lay, "m1", [128, 16], F32)
    t.ge1 = sb(nc, lay, "ge1", [128, 128], F32)
    t.msk = sb(nc, lay, "msk", [128, 128], F32)

    rLN = _region(lay)
    t.x = sb(nc, rLN, "x", [128, 8, T], F32)
    t.logits = sb(nc, rLN, "logits", [8, T], F32)
    t.sig = sb(nc, rLN, "sig", [8, T], F32)
    t.lb = sb(nc, rLN, "lb", [8, T], F32)
    t.mask_fm = sb(nc, rLN, "mask_fm", [8, T], F32)

    rMOE = _region(lay)
    t.wd_sb = sb(nc, rMOE, "wd_sb", [128, 16, 8, 128], BF16)
    t.hprime = sb(nc, rMOE, "hprime", [128, 16, T], BF16)
    rMOE.pos = max(rMOE.pos, rLN.pos)
    t.wg_sb = [sb(nc, rMOE, f"wg_sb{i}", [128, 8, 128], BF16) for i in range(2)]
    t.wu_sb = [sb(nc, rMOE, f"wu_sb{i}", [128, 8, 128], BF16) for i in range(2)]
    t.sg = [sb(nc, rMOE, f"sg{i}", [128, QG], F32) for i in range(2)]
    t.cbc = [sb(nc, rMOE, f"cbc{i}", [128, QG], F32) for i in range(NQG)]
    t.eo = [sb(nc, rMOE, f"eo{i}", [128, QG], F32) for i in range(2)]

    rATT = _region(lay)
    t.wkv = sb(nc, rATT, "wkv", [128, 8, 256], BF16)
    wq_off = rATT.alloc(8 * 512 * 2)
    t.wq = nc.alloc_sbuf_tensor_at("wq", [128, 8, 512], BF16, offset=wq_off)
    t.pav_sb = [nc.alloc_sbuf_tensor_at(f"pav_sb{i}", [65, QG], F32,
                                        offset=wq_off + i * QG * 4)
                for i in range(NQG)]
    t.cos2 = sb(nc, rATT, "cos2", [64, T], BF16)
    t.sin2 = sb(nc, rATT, "sin2", [64, T], BF16)
    t.kvn = sb(nc, rATT, "kvn", [128, 2, T], BF16)
    t.q_c = sb(nc, rATT, "q_c", [128, 4, T], BF16)
    t.krw = [sb(nc, rATT, f"krw{i}", [128, 8, 128], BF16) for i in range(2)]
    t.krwp = [sb(nc, rATT, f"krwp{i}", [128, 8, 128], BF16) for i in range(2)]
    t.qrw = [sb(nc, rATT, f"qrw{i}", [128, 4, 128], BF16) for i in range(2)]
    t.qrwp = [sb(nc, rATT, f"qrwp{i}", [128, 4, 128], BF16) for i in range(2)]
    t.kdw = [sb(nc, rATT, f"kdw{i}", [128, 2, 128], BF16) for i in range(2)]
    t.vdw = [sb(nc, rATT, f"vdw{i}", [128, 2, 128], BF16) for i in range(2)]
    t.qdw = [sb(nc, rATT, f"qdw{i}", [128, 4, 128], BF16) for i in range(2)]
    t.vT = [sb(nc, rATT, f"vT{i}", [128, 16, 130], BF16) for i in range(2)]
    t.mask01 = sb(nc, rATT, "mask01", [128, 128], BF16)
    wsb1 = sb(nc, rATT, "wsb", [64, QG], F32)
    t.wsb = [wsb1, wsb1]
    t.ao_sb = [sb(nc, rATT, f"ao_sb{i}", [64, QG], BF16) for i in range(2)]
    t.ror_a = nc.alloc_sbuf_tensor_at("ror_a", [128, T], BF16, offset=t.csb_off)
    t.ror_b = nc.alloc_sbuf_tensor_at("ror_b", [128, T], BF16, offset=t.csb_off + T * 2)
    t.kpk = [sb(nc, rATT, f"kpk{i}", [128, 2, T], BF16) for i in range(3)]
    t.qpk = [sb(nc, rATT, f"qpk{i}", [128, 2, T], BF16) for i in range(3)]
    t.expT = [sb(nc, rATT, f"expT{i}", [128, T], BF16) for i in range(3)]

    rUPD = _region(lay)
    t.atile = [sb(nc, rUPD, f"atile{i}", [128, 8, QG], BF16) for i in range(2)]
    t.wo_sb = sb(nc, rUPD, "wo_sb", [128, 8, 8, 128], BF16)
    xt0 = sb(nc, rUPD, "xtile0", [128, 8, QG], F32)
    ct0 = sb(nc, rUPD, "ctile0", [128, 8, QG], F32)
    xt1 = sb(nc, rUPD, "xtile1", [128, 8, QG], F32)
    ct1 = sb(nc, rUPD, "ctile1", [128, 8, QG], F32)
    t.xtile = [xt0, xt1]
    t.ctile = [ct0, ct1]
    t.x1t = [sb(nc, rUPD, f"x1t{i}", [128, 8, QG], F32) for i in range(2)]
    lay.pos = max(rLN.peak, rMOE.peak, rATT.peak, rUPD.peak)
    lay.peak = max(lay.peak, lay.pos)
    assert lay.pos <= lay.limit, f"L0 SBUF overflow {lay.pos}"
    return t


def alloc_l1(nc, lay, t):
    NPOOL = 6
    t.l1vecs = sb(nc, lay, "l1vecs", [128, 8, 8], F32)
    t.b1 = sb(nc, lay, "b1", [128, 32], F32)
    t.decay = sb(nc, lay, "decay", [128, 8], F32)
    t.first = sb(nc, lay, "first", [128, 8], F32)
    t.carry_s = sb(nc, lay, "carry_s", [128, 8], F32)
    t.carry_x = sb(nc, lay, "carry_x", [128, 8], F32)
    t.pool = [sb(nc, lay, f"pool{i}", [128, 8, 128], BF16) for i in range(NPOOL)]
    t.x1cb = [sb(nc, lay, f"x1c{i}", [128, 8, TC], F32) for i in range(2)]
    t.bc_c = sb(nc, lay, "bc_c", [128, TC], F32)
    t.xnc = sb(nc, lay, "xnc", [128, 8, TC], F32)
    t.xnb_c = sb(nc, lay, "xnb_c", [128, 8, TC], BF16)
    t.xt1 = sb(nc, lay, "xt1", [128, 8, TC], F32)
    doff = lay.alloc(8 * TC * 4)
    koff = lay.alloc(8 * TC * 4)
    offs = [doff, koff]
    t.dif2 = [nc.alloc_sbuf_tensor_at(f"dif{i}", [128, 8, TC], F32, offset=offs[i])
              for i in range(2)]
    t.s_c2 = [nc.alloc_sbuf_tensor_at(f"s_c{i}", [128, 8, TC], F32, offset=offs[i])
              for i in range(2)]
    t.Ss2 = [nc.alloc_sbuf_tensor_at(f"Ss{i}", [128, 8, TC], F32, offset=offs[i])
             for i in range(2)]
    t.K2 = [nc.alloc_sbuf_tensor_at(f"K{i}", [128, 8, TC], F32, offset=offs[1 - i])
            for i in range(2)]
    t.h1 = sb(nc, lay, "h1", [128, 32, TC], BF16)
    moff = lay.alloc(8 * TC * 2)
    t.mix0 = nc.alloc_sbuf_tensor_at("mix0", [128, 8, TC], BF16, offset=moff)
    t.rk = nc.alloc_sbuf_tensor_at("rk", [128, 8, TC], BF16, offset=moff)
    m1off = lay.alloc(8 * TC * 2)
    t.mix1 = nc.alloc_sbuf_tensor_at("mix1", [128, 8, TC], BF16, offset=m1off)
    voff = lay.alloc(8 * TC * 2)
    assert voff == m1off + 8 * TC * 2
    t.Vv = nc.alloc_sbuf_tensor_at("Vv", [128, 8, TC], BF16, offset=voff)
    t.outc = nc.alloc_sbuf_tensor_at("outc", [128, 8, TC], F32, offset=m1off)
    t.mix = [t.mix0, t.mix1]
    t.R = t.mix1
    t.pool += [sb(nc, lay, f"pool{6 + i}", [128, 8, 128], BF16) for i in range(4)]
    assert lay.pos <= lay.limit, f"L1 SBUF overflow {lay.pos}"
    return t


L0_INPUTS = ["wkv", "wq", "wo", "kdec", "vdec", "qdec", "krope", "krope_p",
             "qrope", "qrope_p", "gate_wt", "wg", "wu", "wd", "gkv", "beta",
             "eye", "mask01", "ones_bf", "ones_row", "cos2", "sin2", "sel"]
L1_INPUTS = ["wk", "wv", "wr", "wo1", "w1", "w2", "l1vecs", "b1",
             "ones_bf", "ones_row"]


def mk_sems(P):
    d = {k: P.sem(k) for k in ["dx", "dc", "dw", "dacc", "dao", "dx1", "dout"]}
    for k in ["t", "v", "s", "g"]:
        d[k] = P.sem(k, chained=True)
    return d


def build_program_l0(packed, const):
    nc = bass.Bass("TRN2", target_bir_lowering=False)
    x_in = nc.dram_tensor("x", [128, 8, T], F32, kind="ExternalInput").ap()
    x1_out = nc.dram_tensor("x1", [128, 8, T], F32, kind="ExternalOutput").ap()
    aps = declare_inputs(nc, packed, const, L0_INPUTS)
    dram = {
        "xspill": nc.dram_tensor("xspill", [128, 8, T], F32, kind="Internal").ap(),
        "comb": nc.dram_tensor("comb", [128, 8, T], F32, kind="Internal").ap(),
        "aout": nc.dram_tensor("aout", [128, 8, T], BF16, kind="Internal").ap(),
    }
    lay = Bump()
    t = NS()
    alloc_shared(nc, lay, t)
    alloc_l0(nc, lay, t)
    t.pb = [nc.alloc_psum_tensor(f"pb{i}", [128, QG], F32) for i in range(8)]
    t.pav = {qg: t.pb[4 + qg] for qg in range(NQG)}
    P = Prog(nc)
    sems = mk_sems(P)
    out_toks = build_l0(P, t, aps, x_in, x1_out, sems, dram)
    P.on("sync").waits(out_toks)
    with nc.Block() as block:
        P.finalize(block)
    return nc


def build_program_l1(packed, const):
    nc = bass.Bass("TRN2", target_bir_lowering=False)
    x1_in = nc.dram_tensor("x1", [128, 8, T], F32, kind="ExternalInput").ap()
    out = nc.dram_tensor("out", [128, 8, T], F32, kind="ExternalOutput").ap()
    aps = declare_inputs(nc, packed, const, L1_INPUTS)
    lay = Bump()
    t = NS()
    alloc_shared(nc, lay, t, TC)
    alloc_l1(nc, lay, t)
    t.pb = [nc.alloc_psum_tensor(f"pb{i}", [128, QG], F32) for i in range(8)]
    P = Prog(nc)
    sems = mk_sems(P)
    out_toks = build_l1(P, t, aps, x1_in, out, sems)
    P.on("sync").waits(out_toks)
    with nc.Block() as block:
        P.finalize(block)
    return nc


def build_program_fused(packed, const):
    nc = bass.Bass("TRN2", target_bir_lowering=False)
    x_in = nc.dram_tensor("x", [128, 8, T], F32, kind="ExternalInput").ap()
    out = nc.dram_tensor("out", [128, 8, T], F32, kind="ExternalOutput").ap()
    need = list(dict.fromkeys(L0_INPUTS + L1_INPUTS))
    aps = declare_inputs(nc, packed, const, need)
    dram = {
        "xspill": nc.dram_tensor("xspill", [128, 8, T], F32, kind="Internal").ap(),
        "comb": nc.dram_tensor("comb", [128, 8, T], F32, kind="Internal").ap(),
        "aout": nc.dram_tensor("aout", [128, 8, T], BF16, kind="Internal").ap(),
        "x1": nc.dram_tensor("x1", [128, 8, T], F32, kind="Internal").ap(),
        "wl1": nc.dram_tensor("wl1", [96, 128, 8, 128], BF16, kind="Internal").ap(),
    }
    lay = Bump()
    t = NS()
    alloc_shared(nc, lay, t)
    base0 = lay.mark()
    alloc_l0(nc, lay, t)
    l0_peak = lay.pos
    rL1 = Bump()
    rL1.pos = base0
    alloc_l1(nc, rL1, t)
    lay.pos = max(l0_peak, rL1.pos)
    t.pb = [nc.alloc_psum_tensor(f"pb{i}", [128, QG], F32) for i in range(8)]
    t.pav = {qg: t.pb[4 + qg] for qg in range(NQG)}
    P = Prog(nc)
    sems = mk_sems(P)
    x1_toks = build_l0(P, t, aps, x_in, dram["x1"], sems, dram)
    out_toks = build_l1(P, t, aps, dram["x1"], out, sems, entry_bar=x1_toks,
                        wl1=dram["wl1"])
    P.on("sync").waits(out_toks)
    with nc.Block() as block:
        P.finalize(block)
    return nc


def kernel(**inputs):
    ins = {k: np.asarray(v) for k, v in inputs.items()}
    packed, const = pack_host(ins)
    x = ins["L0_attn_x"]
    xs = [np.ascontiguousarray(x[b].T.reshape(8, 128, T).transpose(1, 0, 2))
          for b in range(NCORES)]

    nc = build_program_fused(packed, const)
    need = list(dict.fromkeys(L0_INPUTS + L1_INPUTS))
    in_maps = []
    for b in range(NCORES):
        m = {"x": xs[b]}
        for nm in need:
            m[nm] = packed[nm] if nm in packed else const[nm]
        in_maps.append(m)
    res = run_bass_kernel_spmd(nc, in_maps, core_ids=list(range(NCORES)))

    out = np.empty((NCORES, T, 1024), np.float32)
    for b in range(NCORES):
        o = res.results[b]["out"]
        out[b] = o.transpose(1, 0, 2).reshape(1024, T).T
    return out
